# version 25
# baseline (speedup 1.0000x reference)
import numpy as np
import ml_dtypes
from contextlib import ExitStack

import concourse.bass as bass
import concourse.mybir as mybir
from concourse.bass_utils import run_bass_kernel_spmd
from concourse.alu_op_type import AluOpType as ALU

F32 = mybir.dt.float32
BF16 = mybir.dt.bfloat16
F16 = mybir.dt.float16
AF = mybir.ActivationFunctionType
AX = mybir.AxisListType.X

S, DM = 4096, 1024
NT = S // 128
H, DH = 16, 64
HI, DI = 4, 64
TOPK = 512
NEG = -1.0e9
CH = DM // 128
L1H, L1L = 16, 288
E0A, DF0A = 4, 4096
E0F, DF0F = 8, 2048
E1F = 8
NBIS = 18

ENGINES = ("sync", "gpsimd", "vector", "scalar", "tensor")

DECLARED_INPUTS = [
    "L0_attn_x", "L0_attn_wq", "L0_attn_wk", "L0_attn_wv", "L0_attn_wo",
    "L0_attn_idx_wq", "L0_attn_idx_wk", "L0_attn_idx_ww",
    "L0_attn_gate_w", "L0_attn_w1", "L0_attn_w2", "n1w_l", "n2w_l",
    "moe_b1_l", "moe_b2_l", "moe_b2_row", "ffn_b2_row", "l1f_b_row",
    "L0_ffn_gate_w", "L0_ffn_gate_b",
    "L0_ffn_w1", "L0_ffn_w2", "ffn_b1_l", "ffn_b2_l",
    "L1_attn_latent_w", "lat_b_l", "L1_attn_Wd_w", "wd_b_l",
    "out_w_pad", "out_b_l", "L1_ffn_weight", "l1f_b_l",
    "router_wT", "L1_ffn_router_bias",
]


class Plan:
    def __init__(self, nc, stack):
        self.nc = nc
        self.items = []
        self.sems = {}
        self.counts = {}
        self._stack = stack
        self.nbar = 0

    def sem(self, name):
        if name not in self.sems:
            self.sems[name] = self._stack.enter_context(self.nc.semaphore(name))
            self.counts[name] = 0
        return self.sems[name]

    def inc(self, name, by):
        self.sem(name)
        self.counts[name] += by
        return self.counts[name]

    def emit(self, engine, fn):
        assert engine in ENGINES
        self.items.append((engine, fn))

    def wait(self, engine, sem_name, val):
        if val <= 0:
            return
        s = self.sem(sem_name)
        self.emit(engine, lambda e, s=s, v=val: e.wait_ge(s, v))

    def op(self, engine, meth, *args, inc=None, **kwargs):
        if inc is not None:
            s = self.sem(inc[0])
            self.counts[inc[0]] += inc[1]
            def fn(e, meth=meth, args=args, kwargs=kwargs, s=s, by=inc[1]):
                getattr(e, meth)(*args, **kwargs).then_inc(s, by)
            self.emit(engine, fn)
            return self.counts[inc[0]]
        def fn(e, meth=meth, args=args, kwargs=kwargs):
            getattr(e, meth)(*args, **kwargs)
        self.emit(engine, fn)
        return None

    def mm(self, psum_ap, lhsT, rhs, start, stop, inc=None):
        return self.op("tensor", "matmul", psum_ap, lhsT, rhs,
                       start=start, stop=stop, inc=inc)

    def qop(self, engine, meth, *args, **kwargs):
        return self.op(engine, meth, *args, inc=("q_" + engine, 1), **kwargs)

    MAIN_SEM = {"vector": "dve", "scalar": "act", "tensor": "pe",
                "gpsimd": "pool", "sync": "dma"}

    def dwait(self, engine, val=None):
        self.wait(engine, "q_" + engine, self.counts.get("q_" + engine, 0))
        ms = self.MAIN_SEM[engine]
        self.wait(engine, ms, self.counts.get(ms, 0))

    def drip(self, ctx, sem="q_vector"):
        t, ps = ctx["t"], ctx["ps"]
        self.wait("tensor", sem, self.counts.get(sem, 0))
        self.op("tensor", "matmul", ps["D"][0:1, 508:509],
                t["ONES"][0:1, 0:1], t["ONES"][0:1, 0:1],
                start=True, stop=True)

    def barrier(self):
        self.nbar += 1
        k = self.nbar
        for eng in ENGINES:
            self.op(eng, "nop", inc=("bar_" + eng, 1))
        snapshot = [(s, c) for s, c in self.counts.items()
                    if c > 0 and not s.startswith("bar_")]
        for eng in ENGINES:
            for other in ENGINES:
                if other != eng:
                    self.wait(eng, "bar_" + other, k)
            for s, c in snapshot:
                self.wait(eng, s, c)

    def run_block(self):
        with self.nc.Block() as block:
            for eng_name in ENGINES:
                items = [fn for e, fn in self.items if e == eng_name]
                if not items:
                    continue
                def body(engine, items=items):
                    for fn in items:
                        fn(engine)
                getattr(block, eng_name)(body)


def host_prep(inputs):
    d = dict(inputs)
    def featvec(v):
        F = v.shape[-1]
        return np.ascontiguousarray(v.reshape(F // 128, 128).T)
    def padlast(v, tot):
        pv = np.zeros(v.shape[:-1] + (tot,), v.dtype)
        pv[..., :v.shape[-1]] = v
        return pv
    d["n1w_l"] = featvec(inputs["L0_attn_norm1_w"])
    d["n2w_l"] = featvec(inputs["L0_attn_norm2_w"])
    d["moe_b1_l"] = np.ascontiguousarray(
        inputs["L0_attn_b1"].reshape(E0A, DF0A // 128, 128).transpose(0, 2, 1))
    d["moe_b2_l"] = np.ascontiguousarray(
        inputs["L0_attn_b2"].reshape(E0A, CH, 128).transpose(0, 2, 1))
    d["ffn_b1_l"] = np.ascontiguousarray(
        inputs["L0_ffn_b1"].reshape(E0F, DF0F // 128, 128).transpose(0, 2, 1))
    d["ffn_b2_l"] = np.ascontiguousarray(
        inputs["L0_ffn_b2"].reshape(E0F, CH, 128).transpose(0, 2, 1))
    d["lat_b_l"] = np.ascontiguousarray(
        padlast(inputs["L1_attn_latent_b"], 384).reshape(3, 128).T)
    d["wd_b_l"] = np.ascontiguousarray(
        padlast(inputs["L1_attn_Wd_b"], 384).reshape(L1H, 3, 128).transpose(0, 2, 1))
    d["out_b_l"] = featvec(inputs["L1_attn_out_b"])
    d["l1f_b_l"] = np.ascontiguousarray(
        inputs["L1_ffn_bias"].reshape(E1F, CH, 128).transpose(0, 2, 1))
    d["router_wT"] = np.ascontiguousarray(inputs["L1_ffn_router_weight"].T)
    ow = inputs["L1_attn_out_w"]
    owp = np.zeros((L1H * 384, DM), ow.dtype)
    for h in range(L1H):
        owp[h * 384:h * 384 + L1L] = ow[h * L1L:(h + 1) * L1L]
    d["out_w_pad"] = owp
    d["L0_ffn_gate_b"] = np.ascontiguousarray(inputs["L0_ffn_gate_b"].reshape(1, -1))
    d["moe_b2_row"] = np.ascontiguousarray(inputs["L0_attn_b2"].reshape(E0A, 1, DM))
    d["ffn_b2_row"] = np.ascontiguousarray(inputs["L0_ffn_b2"].reshape(E0F, 1, DM))
    d["l1f_b_row"] = np.ascontiguousarray(inputs["L1_ffn_bias"].reshape(E1F, 1, DM))
    d["L1_ffn_router_bias"] = np.ascontiguousarray(inputs["L1_ffn_router_bias"].reshape(1, -1))
    return d


def make_ctx(taps=()):
    nc = bass.Bass("TRN2", target_bir_lowering=False)
    stack = ExitStack()
    p = Plan(nc, stack)

    def din(name, shape):
        return nc.dram_tensor(name, list(shape), F32, kind="ExternalInput").ap()

    w = {}
    x_d = din("L0_attn_x", (S, DM))
    w["wq"] = din("L0_attn_wq", (DM, DM))
    w["wk"] = din("L0_attn_wk", (DM, DM))
    w["wv"] = din("L0_attn_wv", (DM, DM))
    w["wo"] = din("L0_attn_wo", (DM, DM))
    w["idx_wq"] = din("L0_attn_idx_wq", (DM, HI * DI))
    w["idx_wk"] = din("L0_attn_idx_wk", (DM, DI))
    w["idx_ww"] = din("L0_attn_idx_ww", (DM, HI))
    w["gate_w"] = din("L0_attn_gate_w", (DM, E0A))
    w["moe_w1"] = din("L0_attn_w1", (E0A, DM, DF0A))
    w["moe_w2"] = din("L0_attn_w2", (E0A, DF0A, DM))
    w["n1w_l"] = din("n1w_l", (128, CH))
    w["n2w_l"] = din("n2w_l", (128, CH))
    w["moe_b1_l"] = din("moe_b1_l", (E0A, 128, DF0A // 128))
    w["moe_b2_l"] = din("moe_b2_l", (E0A, 128, CH))
    w["moe_b2_row"] = din("moe_b2_row", (E0A, 1, DM))
    w["ffn_b2_row"] = din("ffn_b2_row", (E0F, 1, DM))
    w["l1f_b_row"] = din("l1f_b_row", (E1F, 1, DM))
    w["fgate_w"] = din("L0_ffn_gate_w", (DM, E0F))
    w["fgate_b"] = din("L0_ffn_gate_b", (1, E0F))
    w["ffn_w1"] = din("L0_ffn_w1", (E0F, DM, DF0F))
    w["ffn_w2"] = din("L0_ffn_w2", (E0F, DF0F, DM))
    w["ffn_b1_l"] = din("ffn_b1_l", (E0F, 128, DF0F // 128))
    w["ffn_b2_l"] = din("ffn_b2_l", (E0F, 128, CH))
    w["lat_w"] = din("L1_attn_latent_w", (DM, L1L))
    w["lat_b_l"] = din("lat_b_l", (128, 3))
    w["wd_w"] = din("L1_attn_Wd_w", (L1H, DM, L1L))
    w["wd_b_l"] = din("wd_b_l", (L1H, 128, 3))
    w["out_w"] = din("out_w_pad", (L1H * 384, DM))
    w["out_b_l"] = din("out_b_l", (128, CH))
    w["l1f_w"] = din("L1_ffn_weight", (E1F, DM, DM))
    w["l1f_b_l"] = din("l1f_b_l", (E1F, 128, CH))
    w["router_wT"] = din("router_wT", (DM, E1F))
    w["router_b"] = din("L1_ffn_router_bias", (1, E1F))

    out_d = nc.dram_tensor("out", [S, DM], F32, kind="ExternalOutput").ap()

    scr = {}
    def dscr(nm, shape, dt=F32):
        kind = "ExternalOutput" if nm in taps else "Internal"
        scr[nm] = nc.dram_tensor("scr_" + nm, list(shape), dt, kind=kind).ap()
        return scr[nm]
    dscr("x_att", (S, DM))
    dscr("ff0", (S, DM))
    dscr("gffn", (S, DM))
    dscr("x1", (S, DM))
    dscr("s1", (S, DM))
    dscr("l1f", (S, DM))
    dscr("k0", (DM, S), BF16)
    dscr("v0", (S, DM), BF16)
    dscr("sel", (S, S), BF16)
    dscr("wdkvT", (S, 384), BF16)
    dscr("xn2f", (DM, S), BF16)
    dscr("xnf", (DM, S), BF16)
    dscr("xn1f", (DM, S), BF16)
    dscr("wdkvF", (384, S), BF16)

    def sbp(name, shape, dt=F32):
        return stack.enter_context(nc.sbuf_tensor(name, list(shape), dt))
    t = {}
    t["IDT"] = sbp("IDT", (128, 128), BF16)
    t["IDT2"] = sbp("IDT2", (128, 128), F32)
    t["ONES"] = sbp("ONES", (128, 512), BF16)
    t["ZER"] = sbp("ZER", (128, 512), BF16)
    t["DIAG"] = sbp("DIAG", (128, 4 * 512), BF16)
    t["DIAGM"] = sbp("DIAGM", (128, 4 * 512), BF16)
    t["STATS"] = sbp("STATS", (128, 192), F32)
    t["NRM"] = sbp("NRM", (128, 2 * CH), F32)
    t["KMX"] = sbp("KMX", (128, 48), F32)
    t["GATES"] = sbp("GATES", (128, 384), F32)

    ps = {}
    for nm in ("A", "B", "C", "D", "F", "G"):
        ps[nm] = stack.enter_context(nc.psum_tensor("ps" + nm, [128, 512], F32))
    ps["T"] = stack.enter_context(nc.psum_tensor("psT", [128, 2048], BF16))

    return {"nc": nc, "p": p, "stack": stack, "w": w, "x_d": x_d,
            "out_d": out_d, "scr": scr, "t": t, "ps": ps, "taps": taps,
            "stats_base": 0}


PHASE_NO = [0]


class Phase:
    def __init__(self, ctx):
        self.ctx = ctx
        self.stack = ExitStack()
        self.nc = ctx["nc"]
        PHASE_NO[0] += 1
        self.pid = PHASE_NO[0]

    def sb(self, name, shape, dt=F32):
        return self.stack.enter_context(
            self.nc.sbuf_tensor(f"p{self.pid}_{name}", list(shape), dt))

    def close(self):
        self.stack.close()
        self.ctx["p"].barrier()



def plan_init(ctx):
    p, t, w = ctx["p"], ctx["t"], ctx["w"]
    p.op("gpsimd", "memset", t["IDT"][:, :], 0.0)
    p.op("gpsimd", "affine_select", out=t["IDT"][:, :], in_=t["IDT"][:, :],
         compare_op=ALU.not_equal, fill=1.0, base=0, pattern=[[-1, 128]],
         channel_multiplier=1)
    p.op("gpsimd", "memset", t["IDT2"][:, :], 0.0)
    p.op("gpsimd", "affine_select", out=t["IDT2"][:, :], in_=t["IDT2"][:, :],
         compare_op=ALU.not_equal, fill=1.0, base=0, pattern=[[-1, 128]],
         channel_multiplier=1)
    p.op("gpsimd", "memset", t["ONES"][:, :], 1.0)
    p.op("gpsimd", "memset", t["ZER"][:, :], 0.0)
    for v in range(4):
        sl = t["DIAG"][:, v * 512:(v + 1) * 512]
        p.op("gpsimd", "memset", sl, 0.0)
        p.op("gpsimd", "affine_select", out=sl, in_=sl, compare_op=ALU.is_ge,
             fill=NEG, base=128 * v, pattern=[[-1, 512]], channel_multiplier=1)
        slm = t["DIAGM"][:, v * 512:(v + 1) * 512]
        p.op("gpsimd", "memset", slm, 1.0)
        p.op("gpsimd", "affine_select", out=slm, in_=slm, compare_op=ALU.is_ge,
             fill=0.0, base=128 * v, pattern=[[-1, 512]], channel_multiplier=1,
             inc=("pool", 1))
    p.op("sync", "dma_start", out=t["NRM"][:, 0:CH], in_=w["n1w_l"], inc=("dma", 16))
    p.op("sync", "dma_start", out=t["NRM"][:, CH:2 * CH], in_=w["n2w_l"],
         inc=("dma", 16))
    p.barrier()



def plan_stats(ctx, src_d, tag, rms=False, eps=1e-5):
    p, t = ctx["p"], ctx["t"]
    ph = Phase(ctx)
    base = ctx["stats_base"]
    ncol = NT if rms else 2 * NT
    ctx[tag + "_mu"] = None if rms else base
    ctx[tag + "_r"] = base + (0 if rms else NT)
    ctx["stats_base"] = base + ncol
    ST = t["STATS"]
    rbase = ctx[tag + "_r"]
    xw = ph.sb("stx", (128, 2048), F32)
    dump = ph.sb("stdump", (128, 1024), F32)
    act_after = []
    for i in range(NT):
        xt = xw[:, (i % 2) * 1024:(i % 2) * 1024 + 1024]
        if i >= 2:
            p.wait("sync", "act", act_after[i - 2])
        v = p.op("sync", "dma_start", out=xt, in_=src_d[i * 128:(i + 1) * 128, :],
                 inc=("dma", 16))
        p.wait("scalar", "dma", v)
        if not rms:
            p.op("scalar", "activation", out=dump[:, :], in_=xt,
                 func=AF.Copy, accum_out=ST[:, base + i:base + i + 1])
        va = p.op("scalar", "activation", out=dump[:, :], in_=xt,
                  func=AF.Square, accum_out=ST[:, rbase + i:rbase + i + 1],
                  inc=("act", 1))
        act_after.append(va)
        p.drip(ctx, "act")
    sq = ST[:, rbase:rbase + NT]
    p.wait("vector", "act", act_after[-1])
    p.qop("vector", "tensor_scalar", out=sq, in0=sq, scalar1=1.0 / DM,
          scalar2=None, op0=ALU.mult)
    if not rms:
        mu = ST[:, base:base + NT]
        p.qop("vector", "tensor_scalar", out=mu, in0=mu, scalar1=1.0 / DM,
              scalar2=None, op0=ALU.mult)
        p.dwait("vector")
        mu2 = ph.sb("stmu2", (128, NT), F32)
        p.qop("vector", "tensor_tensor", mu2[:, :], mu, mu, ALU.mult)
        p.dwait("vector")
        p.qop("vector", "tensor_tensor", sq, sq, mu2[:, :], ALU.subtract)
    p.dwait("vector")
    vv = p.op("vector", "tensor_scalar", out=sq, in0=sq, scalar1=eps,
              scalar2=None, op0=ALU.add, inc=("dve", 1))
    p.wait("scalar", "dve", vv)
    va = p.op("scalar", "activation", out=sq, in_=sq, func=AF.Sqrt, inc=("act", 1))
    p.wait("vector", "act", va)
    p.op("vector", "reciprocal", out=sq, in_=sq, inc=("dve", 1))
    ph.close()



def ln_apply(ctx, ph, p, xt_f32, out_f32, tile_i, mu_base, r_base, inc=None):
    ST = ctx["t"]["STATS"]
    if mu_base is None:
        return p.op("vector", "tensor_scalar", out=out_f32, in0=xt_f32,
                    scalar1=ST[:, r_base + tile_i:r_base + tile_i + 1],
                    scalar2=None, op0=ALU.mult, inc=inc)
    return p.op("vector", "tensor_scalar", out=out_f32, in0=xt_f32,
                scalar1=ST[:, mu_base + tile_i:mu_base + tile_i + 1],
                scalar2=ST[:, r_base + tile_i:r_base + tile_i + 1],
                op0=ALU.subtract, op1=ALU.mult, inc=inc)


def transpose_tm_to_fm(ctx, p, src_bf16_1024, dst_fm, wait_pe_base=None,
                       scale_cols=None, n1w_off=None, width=128):
    raise NotImplementedError



def plan_l0_kv(ctx):
    p, t, w, ps = ctx["p"], ctx["t"], ctx["w"], ctx["ps"]
    scr = ctx["scr"]
    nc = ctx["nc"]
    ph = Phase(ctx)
    ST = t["STATS"]
    mu0, r0 = ctx["st0_mu"], ctx["st0_r"]
    rho = ctx["stats_base"]; ctx["rho1"] = rho
    ctx["stats_base"] += NT

    wkB = ph.sb("wkB", (128, CH * DM), BF16)
    wvB = ph.sb("wvB", (128, CH * DM), BF16)
    wiB = ph.sb("wiB", (128, CH * DI), BF16)
    xw = ph.sb("kv_x", (128, 2048), F32)
    xn = ph.sb("kv_xn", (128, 1024), F32)
    xnb = ph.sb("kv_xnb", (128, 1024), BF16)
    xnF = ph.sb("kv_xnF", (128, CH * 512), BF16)
    kout = ph.sb("kv_out", (128, 2 * 512), BF16)
    kiout = ph.sb("kv_kio", (128, 512), BF16)
    vt = ph.sb("kv_vt", (128, 512), BF16)
    ksq = ph.sb("kv_ksq", (128, 512), BF16)
    knb = ph.sb("kv_knb", (128, 8), BF16)
    dump = ph.sb("kv_dump", (128, 1024), F32)
    kn = ph.sb("kv_kn", (128, 128), F32)
    s01 = ph.sb("kv_s01", (128, 2), F32)

    for kc in range(CH):
        p.op("gpsimd", "dma_start", out=wkB[:, kc * DM:(kc + 1) * DM],
             in_=w["wk"][kc * 128:(kc + 1) * 128, :], inc=("dmag", 16))
        p.op("gpsimd", "dma_start", out=wvB[:, kc * DM:(kc + 1) * DM],
             in_=w["wv"][kc * 128:(kc + 1) * 128, :], inc=("dmag", 16))
        vw = p.op("gpsimd", "dma_start", out=wiB[:, kc * DI:(kc + 1) * DI],
                  in_=w["idx_wk"][kc * 128:(kc + 1) * 128, :], inc=("dmag", 16))
    p.wait("tensor", "dmag", vw)

    kI_hbm = nc.dram_tensor(
        "scr_kI", [DI, S], BF16,
        kind="ExternalOutput" if "kI" in ctx["taps"] else "Internal").ap()
    scr["kI"] = kI_hbm

    evac_act = {}
    ln_dve = {}
    nw = S // 512
    for wi in range(nw):
        p.wait("tensor", "dve", p.counts["dve"])
        for j in range(4):
            i = wi * 4 + j
            xt = xw[:, (i % 2) * 1024:(i % 2) * 1024 + 1024]
            if i >= 2:
                p.wait("sync", "dve", ln_dve[i - 2])
            v = p.op("sync", "dma_start", out=xt,
                     in_=ctx["x_d"][i * 128:(i + 1) * 128, :], inc=("dma", 16))
            p.wait("vector", "dma", v)
            ln_dve[i] = ln_apply(ctx, ph, p, xt, xn[:, :], i, mu0, r0,
                                 inc=("dve", 1))
            p.wait("scalar", "dve", ln_dve[i])
            p.op("scalar", "activation", out=dump[:, 0:512], in_=xn[:, 0:512],
                 func=AF.Square, accum_out=s01[:, 0:1])
            va = p.op("scalar", "activation", out=dump[:, 512:1024],
                      in_=xn[:, 512:1024], func=AF.Square,
                      accum_out=s01[:, 1:2], inc=("act", 1))
            p.wait("vector", "act", va)
            rr = ST[:, rho + i:rho + i + 1]
            p.qop("vector", "tensor_tensor", rr, s01[:, 0:1], s01[:, 1:2],
                  ALU.add)
            p.dwait("vector")
            vd = p.op("vector", "tensor_scalar", out=rr, in0=rr,
                      scalar1=1.0 / DM, scalar2=1e-6, op0=ALU.mult, op1=ALU.add,
                      inc=("dve", 1))
            p.wait("scalar", "dve", vd)
            va = p.op("scalar", "activation", out=rr, in_=rr, func=AF.Sqrt,
                      inc=("act", 1))
            p.wait("vector", "act", va)
            p.qop("vector", "reciprocal", rr, rr)
            p.dwait("vector")
            p.wait("vector", "pe", p.counts.get("pe", 0))
            vd = p.op("vector", "tensor_scalar", out=xnb[:, :], in0=xn[:, :],
                      scalar1=rr, scalar2=None, op0=ALU.mult, inc=("dve", 1))
            p.wait("tensor", "dve", vd)
            for c in range(CH):
                vp = p.op("tensor", "transpose", ps["T"][:, 0:128],
                          xnb[:, c * 128:(c + 1) * 128], t["IDT"][:, :],
                          inc=("pe", 1))
                p.wait("scalar", "pe", vp)
                va = p.op("scalar", "activation",
                          out=xnF[:, c * 512 + j * 128:c * 512 + (j + 1) * 128],
                          in_=ps["T"][:, 0:128], func=AF.Copy,
                          scale=t["NRM"][:, c:c + 1], inc=("act", 1))
                p.wait("tensor", "act", va)

        p.wait("tensor", "act", p.counts["act"])
        for which, wbuf, dst in (("k", wkB, scr["k0"]), ("v", wvB, scr["v0"])):
            for oc in range(CH):
                bank = ps["A"] if oc % 2 == 0 else ps["B"]
                key = (which, oc % 2)
                if key in evac_act:
                    p.wait("tensor", "act", evac_act[key])
                for kc in range(CH):
                    p.mm(bank[:, :],
                         wbuf[:, kc * DM + oc * 128:kc * DM + (oc + 1) * 128],
                         xnF[:, kc * 512:(kc + 1) * 512],
                         start=(kc == 0), stop=(kc == CH - 1),
                         inc=("pe", 1) if kc == CH - 1 else None)
                p.wait("scalar", "pe", p.counts["pe"])
                ko = kout[:, (oc % 2) * 512:(oc % 2) * 512 + 512]
                if oc >= 2:
                    p.wait("scalar", "dma", dma_guard[(which, oc % 2)])
                va = p.op("scalar", "activation", out=ko, in_=bank[:, :],
                          func=AF.Copy, inc=("act", 1))
                evac_act[key] = va
                if "dma_guard" not in dir():
                    dma_guard = {}
                if which == "k":
                    p.wait("sync", "act", va)
                    vs = p.op("sync", "dma_start",
                              out=dst[oc * 128:(oc + 1) * 128,
                                      wi * 512:(wi + 1) * 512],
                              in_=ko, inc=("dma", 16))
                    dma_guard[(which, oc % 2)] = vs
                else:
                    p.wait("tensor", "act", va)
                    for u in range(4):
                        p.op("tensor", "transpose",
                             ps["T"][:, u * 128:(u + 1) * 128],
                             ko[:, u * 128:(u + 1) * 128], t["IDT"][:, :],
                             inc=("pe", 1) if u == 3 else None)
                    p.wait("vector", "pe", p.counts["pe"])
                    vd = p.op("vector", "tensor_copy", vt[:, :],
                              ps["T"][:, 0:512], inc=("dve", 1))
                    p.wait("tensor", "dve", vd)
                    p.wait("sync", "dve", vd)
                    vs = p.op("sync", "dma_start",
                              out=dst[wi * 512:(wi + 1) * 512,
                                      oc * 128:(oc + 1) * 128].rearrange(
                                          "(u p) f -> p u f", p=128),
                              in_=vt[:, :].rearrange("p (u f) -> p u f", u=4),
                              inc=("dma", 16))
                    dma_guard[(which, oc % 2)] = vs
                    p.wait("vector", "dma", vs)
                if which == "k":
                    p.wait("vector", "act", va)
                    vd = p.op("vector", "tensor_tensor", ksq[:, :], ko, ko,
                              ALU.mult, inc=("dve", 1))
                    p.wait("tensor", "dve", vd)
                    vp = p.mm(ps["C"][0:1, 0:512], t["ONES"][:, 0:1],
                              ksq[:, :], start=True, stop=True, inc=("pe", 1))
                    p.wait("vector", "pe", vp)
                    p.qop("vector", "reduce_max",
                          kn[0:1, oc * 8 + wi:oc * 8 + wi + 1],
                          ps["C"][0:1, 0:512], axis=AX)
        if ("ki", 0) in evac_act:
            p.wait("tensor", "act", evac_act[("ki", 0)])
        for kc in range(CH):
            p.mm(ps["D"][0:64, :], wiB[:, kc * DI:(kc + 1) * DI],
                 xnF[:, kc * 512:(kc + 1) * 512],
                 start=(kc == 0), stop=(kc == CH - 1),
                 inc=("pe", 1) if kc == CH - 1 else None)
        p.wait("scalar", "pe", p.counts["pe"])
        va = p.op("scalar", "activation", out=kiout[0:64, 0:512],
                  in_=ps["D"][0:64, :], func=AF.Copy, inc=("act", 1))
        evac_act[("ki", 0)] = va
        p.wait("sync", "act", va)
        vki = p.op("sync", "dma_start", out=kI_hbm[:, wi * 512:(wi + 1) * 512],
                   in_=kiout[0:64, 0:512], inc=("dma", 16))
        p.wait("scalar", "dma", vki)

    km = t["KMX"]
    p.dwait("vector")
    for c in range(CH):
        p.qop("vector", "reduce_max", kn[0:1, 64 + c:65 + c],
              kn[0:1, c * 8:c * 8 + 8], axis=AX)
    p.dwait("vector")
    p.op("vector", "tensor_copy", knb[0:1, 0:CH], kn[0:1, 64:64 + CH],
         inc=("dve", 1))
    p.wait("tensor", "dve", p.counts["dve"])
    p.mm(ps["C"][:, 0:CH], t["ONES"][0:1, 0:128], knb[0:1, 0:CH],
         start=True, stop=True, inc=("pe", 1))
    p.wait("scalar", "pe", p.counts["pe"])
    va = p.op("scalar", "activation", out=km[:, 0:CH], in_=ps["C"][:, 0:CH],
              func=AF.Sqrt, inc=("act", 1))
    p.wait("vector", "act", va)
    p.op("vector", "tensor_scalar", out=km[:, CH:2 * CH], in0=km[:, 0:CH],
         scalar1=-0.125, scalar2=None, op0=ALU.mult, inc=("dve", 1))
    ph.close()



def plan_l0_attn(ctx):
    _l0_attn_pass(ctx, 0)
    _l0_attn_pass(ctx, 1)
    _l0_xatt_fix(ctx)


def _l0_attn_pass(ctx, P):
    p, t, w, ps = ctx["p"], ctx["t"], ctx["w"], ctx["ps"]
    scr = ctx["scr"]
    ph = Phase(ctx)
    ST = t["STATS"]
    mu0, r0, rho = ctx["st0_mu"], ctx["st0_r"], ctx["rho1"]

    KH = ph.sb("a_KH", (128, 4 * 4096), BF16)
    VH = ph.sb("a_VH", (128, 32 * 512), BF16)
    WQB = ph.sb("a_WQB", (128, CH * 512), BF16)
    WOB = ph.sb("a_WOB", (64, 8 * 1024), BF16)
    XNF = ph.sb("a_XNF", (128, CH * 256), BF16)
    QW = ph.sb("a_QW", (128, 4 * 256), BF16)
    QSQ = ph.sb("a_QSQ", (128, 256), BF16)
    PB = ph.sb("a_PB", (128, 2 * 4096), BF16)
    PTB = ph.sb("a_PTB", (128, 2048), BF16)
    SEL = ph.sb("a_SEL", (128, 2 * 4096), BF16)
    XNT = ph.sb("a_XNT", (128, 1024), F32)
    XNB = ph.sb("a_XNB", (128, 1024), BF16)
    XT = ph.sb("a_XT", (128, 1024), F32)
    AOH = ph.sb("a_AOH", (64, 8 * 256), BF16)
    AWBs = ph.sb("a_AWBs", (128, 4 * 256), F32)
    XATT = ph.sb("a_XATT", (128, 1024), F32)
    QN = ph.sb("a_QN", (128, 8), F32)
    BND = ph.sb("a_BND", (128, 8), F32)
    BKM = ph.sb("a_BKM", (128, 8), F32)
    SMS = ph.sb("a_SMS", (128, 36), F32)
    if P == 0:
        IQB = ph.sb("a_IQB", (128, CH * 256), BF16)
        IWB = ph.sb("a_IWB", (128, CH * 4), BF16)
        KIB = ph.sb("a_KIB", (64, 4096), BF16)
        QI = ph.sb("a_QI", (64, 4 * 256), BF16)
        WIT = ph.sb("a_WIT", (128, 8), F32)
        VM = ph.sb("a_VM", (128, 4096), BF16)
        U0 = ph.sb("a_U0", (128, 512), BF16)
        U1 = ph.sb("a_U1", (128, 512), BF16)
        UB = ph.sb("a_UB", (128, 4096), BF16)
        CSB = ph.sb("a_CS", (128, 4096), F16)
        BIS = ph.sb("a_BIS", (128, 20), F32)
        bis = {"VM": VM, "U0": U0, "U1": U1, "UB": UB, "CSB": CSB, "BIS": BIS,
               "QI": QI, "WIT": WIT, "KIB": KIB, "SEL": SEL, "PB": PB}

    hbase = 8 * P

    for kc in range(CH):
        p.op("gpsimd", "dma_start", out=WQB[:, kc * 512:(kc + 1) * 512],
             in_=w["wq"][kc * 128:(kc + 1) * 128, 512 * P:512 * P + 512],
             inc=("dmag", 16))
    for h in range(8):
        p.op("gpsimd", "dma_start", out=WOB[0:64, h * 1024:(h + 1) * 1024],
             in_=w["wo"][512 * P + 64 * h:512 * P + 64 * (h + 1), :],
             inc=("dmag", 16))
    if P == 0:
        for kc in range(CH):
            p.op("gpsimd", "dma_start", out=IQB[:, kc * 256:(kc + 1) * 256],
                 in_=w["idx_wq"][kc * 128:(kc + 1) * 128, :], inc=("dmag", 16))
            p.op("gpsimd", "dma_start", out=IWB[:, kc * 4:(kc + 1) * 4],
                 in_=w["idx_ww"][kc * 128:(kc + 1) * 128, :], inc=("dmag", 16))
        p.op("sync", "dma_start", out=KIB[0:64, :], in_=scr["kI"],
             inc=("dma", 16))
    for c in range(4):
        p.op("sync", "dma_start", out=KH[:, c * 4096:(c + 1) * 4096],
             in_=scr["k0"][512 * P + 128 * c:512 * P + 128 * (c + 1), :],
             inc=("dma", 16))
    p.op("sync", "dma_start",
         out=VH[:, :].rearrange("p (u f) -> p u f", u=32),
         in_=scr["v0"][:, 512 * P:512 * P + 512].rearrange(
             "(u p) f -> p u f", p=128), inc=("dma", 16))
    wload = p.counts["dmag"]
    kvload = p.counts["dma"]
    p.wait("vector", "dve", p.counts["dve"])
    p.qop("vector", "tensor_copy", BKM[:, 0:4],
          t["KMX"][:, CH + 4 * P:CH + 4 * P + 4])
    p.qop("vector", "tensor_copy", BKM[:, 4:8],
          t["KMX"][:, CH + 4 * P:CH + 4 * P + 4])
    p.wait("tensor", "dmag", wload)
    p.wait("tensor", "dma", kvload)

    dbg = ctx.get("attn_dbg", 5)
    nwin = ctx.get("attn_nwin", 16)
    evac = {}
    for wi in range(nwin):
        for j in (0, 1):
            i = 2 * wi + j
            if wi > 0 or j > 0:
                p.wait("sync", "dve", xt_free)
            vx = p.op("sync", "dma_start", out=XT[:, :],
                      in_=ctx["x_d"][i * 128:(i + 1) * 128, :], inc=("dma", 16))
            p.wait("vector", "dma", vx)
            v1 = ln_apply(ctx, ph, p, XT[:, :], XNT[:, :], i, mu0, r0,
                          inc=("dve", 1))
            xt_free = v1
            p.dwait("vector")
            p.wait("vector", "pe", p.counts.get("pe", 0))
            v2 = p.op("vector", "tensor_scalar", out=XNB[:, :], in0=XNT[:, :],
                      scalar1=ST[:, rho + i:rho + i + 1], scalar2=None,
                      op0=ALU.mult, inc=("dve", 1))
            p.wait("tensor", "dve", v2)
            for c in range(CH):
                vp = p.op("tensor", "transpose", ps["T"][:, 0:128],
                          XNB[:, c * 128:(c + 1) * 128], t["IDT"][:, :],
                          inc=("pe", 1))
                p.wait("scalar", "pe", vp)
                va = p.op("scalar", "activation",
                          out=XNF[:, c * 256 + j * 128:c * 256 + (j + 1) * 128],
                          in_=ps["T"][:, 0:128], func=AF.Copy,
                          scale=t["NRM"][:, c:c + 1], inc=("act", 1))
                p.wait("tensor", "act", va)
        xnf_ready = p.counts["act"]
        p.wait("tensor", "act", xnf_ready)
        p.wait("tensor", "dve", p.counts["dve"])

        for qc in range(4):
            if ("qw", qc % 2) in evac:
                p.wait("tensor", "dve", evac[("qw", qc % 2)])
            bank = ps["A"] if qc % 2 == 0 else ps["B"]
            for kc in range(CH):
                p.mm(bank[:, 0:256],
                     WQB[:, kc * 512 + qc * 128:kc * 512 + (qc + 1) * 128],
                     XNF[:, kc * 256:(kc + 1) * 256],
                     start=(kc == 0), stop=(kc == CH - 1),
                     inc=("pe", 1) if kc == CH - 1 else None)
            p.wait("scalar", "pe", p.counts["pe"])
            va = p.op("scalar", "activation", out=QW[:, qc * 256:(qc + 1) * 256],
                      in_=bank[:, 0:256], func=AF.Copy, inc=("act", 1))
            p.wait("vector", "act", va)
            p.wait("vector", "pe", p.counts["pe"])
            vd = p.op("vector", "tensor_tensor", QSQ[:, :],
                      QW[:, qc * 256:(qc + 1) * 256],
                      QW[:, qc * 256:(qc + 1) * 256], ALU.mult, inc=("dve", 1))
            evac[("qw", qc % 2)] = vd
            p.wait("tensor", "dve", vd)
            for j in (0, 1):
                p.mm(ps["D"][:, 8 + j * 4 + qc:9 + j * 4 + qc],
                     QSQ[:, j * 128:(j + 1) * 128], t["ONES"][:, 0:1],
                     start=True, stop=True, inc=("pe", 1) if j == 1 else None)
        p.wait("scalar", "pe", p.counts["pe"])
        qn_evac = p.op("scalar", "activation", out=QN[:, 0:8],
                       in_=ps["D"][:, 8:16], func=AF.Copy, inc=("act", 1))
        p.wait("scalar", "act", qn_evac)
        va = p.op("scalar", "activation", out=QN[:, 0:8], in_=QN[:, 0:8],
                  func=AF.Sqrt, inc=("act", 1))
        p.wait("vector", "act", va)
        vbnd = p.op("vector", "tensor_tensor", BND[:, 0:8], QN[:, 0:8],
                    BKM[:, 0:8], ALU.mult, inc=("dve", 1))
        p.wait("scalar", "dve", vbnd)
        p.wait("tensor", "dve", vbnd)

        if dbg < 2:
            continue
        if P == 0:
            for h in range(4):
                if ("qi", h % 2) in evac:
                    p.wait("tensor", "act", evac[("qi", h % 2)])
                bank = ps["D"] if h % 2 == 0 else ps["F"]
                for kc in range(CH):
                    p.mm(bank[0:64, 0:256],
                         IQB[:, kc * 256 + h * 64:kc * 256 + (h + 1) * 64],
                         XNF[:, kc * 256:(kc + 1) * 256],
                         start=(kc == 0), stop=(kc == CH - 1),
                         inc=("pe", 1) if kc == CH - 1 else None)
                p.wait("scalar", "pe", p.counts["pe"])
                va = p.op("scalar", "activation",
                          out=QI[0:64, h * 256:(h + 1) * 256],
                          in_=bank[0:64, 0:256], func=AF.Copy, inc=("act", 1))
                evac[("qi", h % 2)] = va
            p.wait("tensor", "act", qn_evac)
            for j in (0, 1):
                for kc in range(CH):
                    p.mm(ps["D"][:, 0:4],
                         XNF[:, kc * 256 + j * 128:kc * 256 + (j + 1) * 128],
                         IWB[:, kc * 4:(kc + 1) * 4],
                         start=(kc == 0), stop=(kc == CH - 1),
                         inc=("pe", 1) if kc == CH - 1 else None)
                p.wait("scalar", "pe", p.counts["pe"])
                va = p.op("scalar", "activation", out=WIT[:, j * 4:(j + 1) * 4],
                          in_=ps["D"][:, 0:4], func=AF.Copy, inc=("act", 1))
                p.wait("tensor", "act", va)
            p.wait("vector", "act", p.counts["act"])

        if dbg < 3:
            continue
        for j in (0, 1):
            i = 2 * wi + j
            t0 = i * 128
            extent = t0 + 128
            if P == 0:
                _l0_idx_bisect(ctx, ph, p, bis, wi, j, evac)
            else:
                vs = p.op("sync", "dma_start",
                          out=SEL[:, j * 4096:j * 4096 + extent],
                          in_=scr["sel"][t0:t0 + 128, 0:extent],
                          inc=("dma", 16))
                p.wait("vector", "dma", vs)

        if dbg < 4:
            continue
        aoh_evac = None
        p.wait("tensor", "dve", p.counts["dve"])
        p.wait("tensor", "q_vector", p.counts.get("q_vector", 0))
        for hl in range(8):
            normdone = {}
            lc = hl // 2
            hp = (hl % 2) * 64
            for j in (0, 1):
                i = 2 * wi + j
                t0 = i * 128
                extent = t0 + 128
                nch = (extent + 511) // 512
                for sc in range(nch):
                    sw = min(512, extent - sc * 512)
                    bank = (ps["A"], ps["B"], ps["D"])[sc % 3]
                    meng = "vector"
                    msem = "dve"
                    if ("sb", sc % 3) in evac:
                        gs, gv = evac[("sb", sc % 3)]
                        p.wait("tensor", gs, gv)
                    p.mm(bank[:, 0:sw],
                         QW[hp:hp + 64,
                            lc * 256 + j * 128:lc * 256 + (j + 1) * 128],
                         KH[hp:hp + 64,
                            lc * 4096 + sc * 512:lc * 4096 + sc * 512 + sw],
                         start=True, stop=True, inc=("pe", 1))
                    p.wait("scalar", "pe", p.counts["pe"])
                    pbsl = PB[:, j * 4096 + sc * 512:j * 4096 + sc * 512 + sw]
                    va = p.op("scalar", "activation", out=pbsl,
                              in_=bank[:, 0:sw], func=AF.Exp, scale=0.125,
                              bias=BND[:, j * 4 + lc:j * 4 + lc + 1],
                              inc=("act", 1))
                    p.wait(meng, "act", va)
                    vd = p.op(meng, "scalar_tensor_tensor", out=pbsl,
                              in0=pbsl, scalar=1.0,
                              in1=SEL[:, j * 4096 + sc * 512:j * 4096 + sc * 512 + sw],
                              op0=ALU.mult, op1=ALU.mult,
                              accum_out=SMS[:, j * 18 + sc:j * 18 + sc + 1],
                              inc=(msem, 1))
                    evac[("sb", sc % 3)] = (msem, vd)
                p.dwait("vector")
                p.qop("vector", "reduce_sum", SMS[:, j * 18 + 16:j * 18 + 17],
                      SMS[:, j * 18:j * 18 + nch], axis=AX)
                p.dwait("vector")
                p.qop("vector", "reciprocal", SMS[:, j * 18 + 16:j * 18 + 17],
                      SMS[:, j * 18 + 16:j * 18 + 17])
                p.dwait("vector")
                for sc in range(nch):
                    sw = min(512, extent - sc * 512)
                    normdone[(j, sc)] = p.op(
                        "vector", "tensor_scalar",
                        out=PB[:, j * 4096 + sc * 512:j * 4096 + sc * 512 + sw],
                        in0=PB[:, j * 4096 + sc * 512:j * 4096 + sc * 512 + sw],
                        scalar1=SMS[:, j * 18 + 16:j * 18 + 17], scalar2=None,
                        op0=ALU.mult, inc=("dve", 1))
                if j == 0:
                    tail = p.op("vector", "memset", PB[:, extent:extent + 128],
                                0.0, inc=("dve", 1))
                    normdone[(0, "tail")] = tail
                p.drip(ctx)
            if aoh_evac is not None:
                p.wait("tensor", "act", aoh_evac)
            extent1 = (2 * wi + 2) * 128
            nch1 = (extent1 + 511) // 512
            first = True
            copy_done = {}
            nsub_of = {}
            def emit_T(sc):
                par = sc % 2
                nsub = (min(512, extent1 - sc * 512) + 127) // 128
                nsub_of[sc] = nsub
                nd0 = normdone.get((0, sc), normdone.get((0, "tail"), 0))
                p.wait("tensor", "dve", max(nd0, normdone.get((1, sc), 0)))
                if sc >= 2:
                    cd = copy_done[sc - 2]
                    p.wait("tensor", cd[0], cd[1])
                    p.wait("tensor", cd[2], cd[3])
                for u in range(nsub):
                    if u == 2:
                        cs_, cv_ = copy_done[(sc, 0)]
                        p.wait("tensor", cs_, cv_)
                    for j in (0, 1):
                        p.op("tensor", "transpose",
                             ps["T"][:, par * 1024 + (u % 2) * 256 + j * 128:
                                     par * 1024 + (u % 2) * 256 + (j + 1) * 128],
                             PB[:, j * 4096 + sc * 512 + u * 128:
                                j * 4096 + sc * 512 + (u + 1) * 128],
                             t["IDT"][:, :],
                             inc=("pe", 1) if (u % 2 == 1 or u == nsub - 1) and j == 1 else None)
                    if u % 2 == 1 or u == nsub - 1:
                        grp = u // 2
                        v = p.counts["pe"]
                        ceng = "vector" if grp % 2 == 0 else "scalar"
                        csem = "dve" if grp % 2 == 0 else "act"
                        p.wait(ceng, "pe", v)
                        w = min(2, nsub - grp * 2) * 256
                        if ceng == "vector":
                            vd = p.op("vector", "tensor_copy",
                                      PTB[:, par * 1024 + grp * 512:par * 1024 + grp * 512 + w],
                                      ps["T"][:, par * 1024:par * 1024 + w],
                                      inc=("dve", 1))
                        else:
                            vd = p.op("scalar", "activation",
                                      out=PTB[:, par * 1024 + grp * 512:par * 1024 + grp * 512 + w],
                                      in_=ps["T"][:, par * 1024:par * 1024 + w],
                                      func=AF.Copy, inc=("act", 1))
                        copy_done[(sc, grp)] = (csem, vd)
                copy_done[sc] = ("dve", p.counts["dve"], "act", p.counts["act"])
            def emit_PV(sc):
                par = sc % 2
                nsub = nsub_of[sc]
                for u in range(nsub):
                    cs_, cv_ = copy_done[(sc, u // 2)]
                    p.wait("tensor", cs_, cv_)
                    last = (sc == nch1 - 1 and u == nsub - 1)
                    p.mm(ps["C"][0:64, 0:256],
                         VH[:, (sc * 4 + u) * 512 + hl * 64:
                            (sc * 4 + u) * 512 + (hl + 1) * 64],
                         PTB[:, par * 1024 + u * 256:par * 1024 + (u + 1) * 256],
                         start=emit_PV.first, stop=last,
                         inc=("pe", 1) if last else None)
                    emit_PV.first = False
            emit_PV.first = True
            for sc in range(nch1):
                emit_T(sc)
                if sc >= 1:
                    emit_PV(sc - 1)
            emit_PV(nch1 - 1)
            p.wait("scalar", "pe", p.counts["pe"])
            aoh_evac = p.op("scalar", "activation",
                            out=AOH[0:64, hl * 256:(hl + 1) * 256],
                            in_=ps["C"][0:64, 0:256], func=AF.Copy,
                            inc=("act", 1))
        p.wait("tensor", "act", p.counts["act"])

        if dbg < 5:
            continue
        for half in (0, 1):
            for oc2 in range(4):
                oc = half * 4 + oc2
                bank = ps["C"] if oc2 % 2 == 0 else ps["F"]
                if ("wo", oc2 % 2) in evac:
                    p.wait("tensor", "act", evac[("wo", oc2 % 2)])
                for hl in range(8):
                    p.mm(bank[:, 0:256],
                         WOB[0:64, hl * 1024 + oc * 128:hl * 1024 + (oc + 1) * 128],
                         AOH[0:64, hl * 256:(hl + 1) * 256],
                         start=(hl == 0), stop=(hl == 7),
                         inc=("pe", 1) if hl == 7 else None)
                p.wait("scalar", "pe", p.counts["pe"])
                va = p.op("scalar", "activation",
                          out=AWBs[:, oc2 * 256:(oc2 + 1) * 256],
                          in_=bank[:, 0:256], func=AF.Copy, inc=("act", 1))
                evac[("wo", oc2 % 2)] = va
            p.wait("tensor", "act", p.counts["act"])
            for j in (0, 1):
                i = 2 * wi + j
                for oc2 in range(4):
                    oc = half * 4 + oc2
                    vp = p.op("tensor", "transpose", ps["G"][:, 0:128],
                              AWBs[:, oc2 * 256 + j * 128:oc2 * 256 + (j + 1) * 128],
                              t["IDT2"][:, :], inc=("pe", 1))
                    p.wait("vector", "pe", vp)
                    vd = p.op("vector", "tensor_copy",
                              XATT[:, oc2 * 128:(oc2 + 1) * 128],
                              ps["G"][:, 0:128], inc=("dve", 1))
                    p.wait("tensor", "dve", vd)
                p.wait("sync", "dve", p.counts["dve"])
                p.wait("gpsimd", "dve", p.counts["dve"])
                if P == 0:
                    vs = p.op("sync", "dma_start",
                              out=scr["x_att"][i * 128:(i + 1) * 128,
                                               half * 512:(half + 1) * 512],
                              in_=XATT[:, 0:512], inc=("dma", 16))
                    p.wait("vector", "dma", vs)
                else:
                    vs = p.op("gpsimd", "dma_start",
                              out=scr["x_att"][i * 128:(i + 1) * 128,
                                               half * 512:(half + 1) * 512],
                              in_=XATT[:, 0:512], accum_op=ALU.add,
                              inc=("dmag", 16))
                    p.wait("vector", "dmag", vs)
    ph.close()


def _l0_idx_bisect(ctx, ph, p, b, wi, j, evac):
    t, ps = ctx["t"], ctx["ps"]
    scr = ctx["scr"]
    p.wait("tensor", "act", p.counts["act"])
    p.wait("vector", "dma", p.counts["dma"])
    QI, WIT, KIB = b["QI"], b["WIT"], b["KIB"]
    SEL = b["SEL"]
    U0, U1, CSB, BIS = b["U0"], b["U1"], b["CSB"], b["BIS"]
    if j == 0:
        VM, UB = b["VM"], b["UB"]
        bcol = 0
    else:
        VM = b["PB"][:, 0:4096]
        UB = b["PB"][:, 4096:8192]
        bcol = 8
    i = 2 * wi + j
    t0 = i * 128
    extent = t0 + 128
    nch = (extent + 511) // 512
    for sc in range(nch):
        sw = min(512, extent - sc * 512)
        vmsl = VM[:, sc * 512:sc * 512 + sw]
        for h in range(4):
            bank = ps["D"] if h % 2 == 0 else ps["F"]
            if ("il", h % 2) in evac:
                p.wait("tensor", "dve", evac[("il", h % 2)])
            p.mm(bank[:, 0:sw],
                 QI[0:64, h * 256 + j * 128:h * 256 + (j + 1) * 128],
                 KIB[0:64, sc * 512:sc * 512 + sw],
                 start=True, stop=True, inc=("pe", 1))
            p.wait("vector", "pe", p.counts["pe"])
            U = U0 if h % 2 == 0 else U1
            vd = p.op("vector", "tensor_scalar", out=U[:, 0:sw],
                      in0=bank[:, 0:sw], scalar1=0.0,
                      scalar2=WIT[:, j * 4 + h:j * 4 + h + 1],
                      op0=ALU.max, op1=ALU.mult, inc=("dve", 1))
            evac[("il", h % 2)] = vd
            if h == 1:
                p.wait("vector", "dve", vd)
                p.qop("vector", "tensor_tensor", vmsl, U0[:, 0:sw], U1[:, 0:sw],
                      ALU.add)
            elif h == 3:
                p.wait("vector", "dve", vd)
                p.qop("vector", "tensor_tensor", U0[:, 0:sw], U0[:, 0:sw],
                      U1[:, 0:sw], ALU.add)
                p.dwait("vector")
                vd = p.op("vector", "tensor_tensor", vmsl, vmsl, U0[:, 0:sw],
                          ALU.add, inc=("dve", 1))
                evac[("il", 0)] = vd
                evac[("il", 1)] = vd
    p.dwait("vector")
    p.qop("vector", "tensor_reduce", BIS[:, bcol:bcol + 1], VM[:, 0:extent],
          AX, ALU.min)
    p.dwait("vector")
    p.qop("vector", "tensor_scalar", out=BIS[:, bcol:bcol + 1],
          in0=BIS[:, bcol:bcol + 1], scalar1=0.0, scalar2=-1.0, op0=ALU.min,
          op1=ALU.add)
    dv = i % 4
    ds = ((extent - 1) // 512) * 512
    p.dwait("vector")
    p.qop("vector", "tensor_tensor", VM[:, ds:ds + 512], VM[:, ds:ds + 512],
          t["DIAG"][:, dv * 512:(dv + 1) * 512], ALU.add)
    p.dwait("vector")
    p.qop("vector", "tensor_reduce", BIS[:, bcol + 2:bcol + 3], VM[:, 0:extent],
          AX, ALU.max)
    p.dwait("vector")
    v = p.qop("vector", "tensor_scalar", out=BIS[:, bcol + 2:bcol + 3],
              in0=BIS[:, bcol + 2:bcol + 3], scalar1=0.0, scalar2=1.0,
              op0=ALU.max, op1=ALU.add)
    b.setdefault("chain", {})[j] = ("q_vector", p.counts["q_vector"])
    b.setdefault("vmref", {})[j] = (VM, UB, bcol, extent, nch)
    if j == 0:
        return

    chains = {}
    for jj in (0, 1):
        VMj, UBj, bc, ext, nchj = b["vmref"][jj]
        chains[jj] = {"VM": VMj, "UB": UBj, "bc": bc, "ext": ext,
                      "last": b["chain"][jj][1] if jj in b["chain"] else 0}
    def step(c, it, k):
        bc = c["bc"]
        lo = BIS[:, bc + it % 2:bc + it % 2 + 1]
        lon = BIS[:, bc + (it + 1) % 2:bc + (it + 1) % 2 + 1]
        hi = BIS[:, bc + 2 + it % 2:bc + 3 + it % 2]
        hin = BIS[:, bc + 2 + (it + 1) % 2:bc + 3 + (it + 1) % 2]
        m = BIS[:, bc + 4:bc + 5]
        cnt = BIS[:, bc + 5:bc + 6]
        cond = BIS[:, bc + 6:bc + 7]
        d1 = BIS[:, bc + 7:bc + 8]
        d2 = BIS[:, 17 + c["bc"] // 8:18 + c["bc"] // 8]
        p.wait("vector", "q_vector", c["last"])
        if k == 0:
            c["last"] = p.qop("vector", "tensor_tensor", m, lo, hi, ALU.add)
        elif k == 1:
            c["last"] = p.qop("vector", "tensor_scalar", out=m, in0=m,
                              scalar1=0.5, scalar2=None, op0=ALU.mult)
        elif k == 2:
            c["last"] = p.qop("vector", "tensor_scalar",
                              out=c["UB"][:, 0:c["ext"]],
                              in0=c["VM"][:, 0:c["ext"]], scalar1=m,
                              scalar2=None, op0=ALU.is_gt, op1=ALU.add,
                              accum_out=cnt)
        elif k == 3:
            p.qop("vector", "tensor_scalar", out=cond, in0=cnt, scalar1=511.5,
                  scalar2=None, op0=ALU.is_le)
            p.qop("vector", "tensor_tensor", d1, m, hi, ALU.subtract)
            c["last"] = p.qop("vector", "tensor_tensor", d2, lo, m,
                              ALU.subtract)
        elif k == 4:
            c["last"] = p.qop("vector", "scalar_tensor_tensor", out=hin,
                              in0=d1, scalar=cond, in1=hi, op0=ALU.mult,
                              op1=ALU.add)
        else:
            c["last"] = p.qop("vector", "scalar_tensor_tensor", out=lon,
                              in0=d2, scalar=cond, in1=m, op0=ALU.mult,
                              op1=ALU.add)
    for it in range(NBIS):
        for k in range(6):
            step(chains[0], it, k)
            step(chains[1], it, k)
        p.drip(ctx)
    for jj in (0, 1):
        c = chains[jj]
        VMj, UBj, bc, ext = c["VM"], c["UB"], c["bc"], c["ext"]
        i_j = 2 * wi + jj
        t0j = i_j * 128
        nchj = (ext + 511) // 512
        lo = BIS[:, bc + NBIS % 2:bc + NBIS % 2 + 1]
        hi = BIS[:, bc + 2 + NBIS % 2:bc + 3 + NBIS % 2]
        chi = BIS[:, bc + 7:bc + 8]
        need = BIS[:, 16:17]
        selsl = SEL[:, jj * 4096:jj * 4096 + ext]
        p.wait("vector", "q_vector", c["last"])
        p.qop("vector", "tensor_scalar", out=UBj[:, 0:ext],
              in0=VMj[:, 0:ext], scalar1=hi, scalar2=None, op0=ALU.is_gt,
              op1=ALU.add, accum_out=chi)
        p.dwait("vector")
        p.qop("vector", "tensor_scalar", out=need, in0=chi, scalar1=-1.0,
              scalar2=512.0, op0=ALU.mult, op1=ALU.add)
        p.qop("vector", "tensor_scalar", out=selsl, in0=VMj[:, 0:ext],
              scalar1=lo, scalar2=None, op0=ALU.is_gt)
        p.dwait("vector")
        p.qop("vector", "tensor_tensor", selsl, selsl, UBj[:, 0:ext],
              ALU.subtract)
        for sc in range(nchj):
            sw = min(512, ext - sc * 512)
            init = 0.0 if sc == 0 else CSB[:, sc * 512 - 1:sc * 512]
            p.dwait("vector")
            p.qop("vector", "tensor_tensor_scan",
                  out=CSB[:, sc * 512:sc * 512 + sw],
                  data0=SEL[:, jj * 4096 + sc * 512:jj * 4096 + sc * 512 + sw],
                  data1=t["ZER"][:, 0:sw], initial=init, op0=ALU.add,
                  op1=ALU.add)
        p.dwait("vector")
        p.qop("vector", "scalar_tensor_tensor", out=selsl,
              in0=CSB[:, 0:ext], scalar=need, in1=selsl, op0=ALU.is_le,
              op1=ALU.mult)
        p.dwait("vector")
        vfin = p.op("vector", "tensor_tensor", selsl, selsl, UBj[:, 0:ext],
                    ALU.add, inc=("dve", 1))
        p.wait("sync", "dve", vfin)
        vs = p.op("sync", "dma_start",
                  out=scr["sel"][t0j:t0j + 128, 0:ext], in_=selsl,
                  inc=("dma", 16))
    p.wait("scalar", "dve", p.counts["dve"])
    p.wait("scalar", "q_vector", p.counts["q_vector"])


def _l0_xatt_fix(ctx):
    p, t = ctx["p"], ctx["t"]
    scr = ctx["scr"]
    ph = Phase(ctx)
    ST = t["STATS"]
    mu0, r0 = ctx["st0_mu"], ctx["st0_r"]
    base = ctx["stats_base"]
    ctx["st2_r"] = base
    ctx["stats_base"] += NT
    XT = ph.sb("f_XT", (128, 2048), F32)
    XA = ph.sb("f_XA", (128, 2048), F32)
    DU = ph.sb("f_DU", (128, 1024), F32)
    guard = {}
    for i in range(NT):
        sl = (i % 2) * 1024
        if i >= 2:
            p.wait("sync", "act", guard[i - 2])
        v1 = p.op("sync", "dma_start", out=XT[:, sl:sl + 1024],
                  in_=ctx["x_d"][i * 128:(i + 1) * 128, :], inc=("dma", 16))
        v2 = p.op("sync", "dma_start", out=XA[:, sl:sl + 1024],
                  in_=scr["x_att"][i * 128:(i + 1) * 128, :], inc=("dma", 16))
        p.wait("vector", "dma", v2)
        ln_apply(ctx, ph, p, XT[:, sl:sl + 1024], XT[:, sl:sl + 1024], i,
                 mu0, r0, inc=("dve", 1))
        p.dwait("vector")
        vd = p.op("vector", "tensor_tensor", XA[:, sl:sl + 1024],
                  XA[:, sl:sl + 1024], XT[:, sl:sl + 1024], ALU.add,
                  inc=("dve", 1))
        p.wait("scalar", "dve", vd)
        va = p.op("scalar", "activation", out=DU[:, :], in_=XA[:, sl:sl + 1024],
                  func=AF.Square, accum_out=ST[:, base + i:base + i + 1],
                  inc=("act", 1))
        guard[i] = va
        p.drip(ctx, "act")
        p.wait("sync", "act", va)
        p.op("sync", "dma_start", out=scr["x_att"][i * 128:(i + 1) * 128, :],
             in_=XA[:, sl:sl + 1024], inc=("dma", 16))
    sq = ST[:, base:base + NT]
    p.wait("vector", "act", p.counts["act"])
    vv = p.op("vector", "tensor_scalar", out=sq, in0=sq, scalar1=1.0 / DM,
              scalar2=1e-6, op0=ALU.mult, op1=ALU.add, inc=("dve", 1))
    p.wait("scalar", "dve", vv)
    va = p.op("scalar", "activation", out=sq, in_=sq, func=AF.Sqrt,
              inc=("act", 1))
    p.wait("vector", "act", va)
    p.op("vector", "reciprocal", out=sq, in_=sq, inc=("dve", 1))
    ph.close()


def emit_rank_top2(ctx, ph, p, E, logits, sel, CG, CE, junk):
    for e in range(E):
        p.qop("vector", "tensor_scalar", out=junk[:, 0:E], in0=logits,
              scalar1=logits[:, e:e + 1], scalar2=None, op0=ALU.is_gt,
              op1=ALU.add, accum_out=CG[:, e:e + 1])
    p.qop("vector", "memset", CE[:, 0:1], 0.0)
    for e in range(1, E):
        p.qop("vector", "tensor_scalar", out=junk[:, 0:e], in0=logits[:, 0:e],
              scalar1=logits[:, e:e + 1], scalar2=None, op0=ALU.is_equal,
              op1=ALU.add, accum_out=CE[:, e:e + 1])
    p.dwait("vector")
    p.qop("vector", "tensor_tensor", CG[:, 0:E], CG[:, 0:E], CE[:, 0:E],
          ALU.add)
    p.dwait("vector")
    p.qop("vector", "tensor_scalar", out=sel, in0=CG[:, 0:E], scalar1=1.5,
          scalar2=None, op0=ALU.is_le)
    p.dwait("vector")


def _gate_prepass(ctx, src_d, r_base, mu_base, nrm_off, gw_d, E, gates_off,
                  xnf_d, gate_bias=None, mode="renorm"):
    p, t, ps = ctx["p"], ctx["t"], ctx["ps"]
    ph = Phase(ctx)
    ST = t["STATS"]
    XT = ph.sb("g_XT", (128, 1024), F32)
    XNB = ph.sb("g_XNB", (128, 1024), BF16)
    XNF = ph.sb("g_XNF", (128, CH * 512), BF16)
    GWB = ph.sb("g_GWB", (128, CH * E), BF16)
    GL = ph.sb("g_GL", (128, 16), F32)
    EE = ph.sb("g_EE", (128, 16), F32)
    CG = ph.sb("g_CG", (128, 8), F32)
    CE = ph.sb("g_CE", (128, 8), F32)
    JK = ph.sb("g_JK", (128, 8), F32)
    SELg = ph.sb("g_SEL", (128, 8), F32)
    BBg = ph.sb("g_BB", (128, 8), F32)
    SM1 = ph.sb("g_SM1", (128, 4), F32)
    for kc in range(CH):
        vw = p.op("gpsimd", "dma_start", out=GWB[:, kc * E:(kc + 1) * E],
                  in_=gw_d[kc * 128:(kc + 1) * 128, :], inc=("dmag", 16))
    if gate_bias is not None:
        vb = p.op("sync", "dma_start", out=BBg[0:1, 0:E], in_=gate_bias,
                  inc=("dma", 16))
        p.wait("vector", "dma", vb)
        p.qop("vector", "tensor_copy", XNB[0:1, 0:E], BBg[0:1, 0:E])
        p.dwait("vector")
        vd = p.inc("dve", 1)
        p.emit("vector", lambda e, s=p.sems["dve"]: e.nop().then_inc(s, 1))
        p.wait("tensor", "dve", vd)
        vp = p.mm(ps["D"][:, 0:E], t["ONES"][0:1, 0:128], XNB[0:1, 0:E],
                  start=True, stop=True, inc=("pe", 1))
        p.wait("vector", "pe", vp)
        p.qop("vector", "tensor_copy", BBg[:, 0:E], ps["D"][:, 0:E])
        p.dwait("vector")
    p.wait("tensor", "dmag", p.counts["dmag"])

    for wi in range(8):
        for j in range(4):
            i = wi * 4 + j
            if wi > 0 or j > 0:
                p.wait("sync", "dve", xfree)
            vx = p.op("sync", "dma_start", out=XT[:, :],
                      in_=src_d[i * 128:(i + 1) * 128, :], inc=("dma", 16))
            p.wait("vector", "dma", vx)
            ln_apply(ctx, ph, p, XT[:, :], XT[:, :], i, mu_base, r_base,
                     inc=("dve", 1))
            p.dwait("vector")
            p.wait("vector", "pe", p.counts.get("pe", 0))
            v2 = p.op("vector", "tensor_copy", XNB[:, :], XT[:, :],
                      inc=("dve", 1))
            xfree = v2
            p.wait("tensor", "dve", v2)
            for c in range(CH):
                vp = p.op("tensor", "transpose", ps["T"][:, 0:128],
                          XNB[:, c * 128:(c + 1) * 128], t["IDT"][:, :],
                          inc=("pe", 1))
                p.wait("scalar", "pe", vp)
                va = p.op("scalar", "activation",
                          out=XNF[:, c * 512 + j * 128:c * 512 + (j + 1) * 128],
                          in_=ps["T"][:, 0:128], func=AF.Copy,
                          scale=t["NRM"][:, nrm_off + c:nrm_off + c + 1]
                          if nrm_off is not None else 1.0, inc=("act", 1))
                p.wait("tensor", "act", va)
        p.drip(ctx, "act")
        va_all = p.counts["act"]
        p.wait("sync", "act", va_all)
        for c in range(CH):
            vs = p.op("sync", "dma_start",
                      out=xnf_d[c * 128:(c + 1) * 128, wi * 512:(wi + 1) * 512],
                      in_=XNF[:, c * 512:(c + 1) * 512], inc=("dma", 16))
        for j in range(4):
            i = wi * 4 + j
            p.wait("tensor", "dve", p.counts["dve"])
            for kc in range(CH):
                p.mm(ps["D"][:, 0:E],
                     XNF[:, kc * 512 + j * 128:kc * 512 + (j + 1) * 128],
                     GWB[:, kc * E:(kc + 1) * E],
                     start=(kc == 0), stop=(kc == CH - 1),
                     inc=("pe", 1) if kc == CH - 1 else None)
            p.wait("vector", "pe", p.counts["pe"])
            p.qop("vector", "tensor_copy", GL[:, 0:E], ps["D"][:, 0:E])
            p.dwait("vector")
            if gate_bias is not None:
                p.qop("vector", "tensor_tensor", GL[:, 0:E], GL[:, 0:E],
                      BBg[:, 0:E], ALU.add)
                p.dwait("vector")
            p.qop("vector", "reduce_max", SM1[:, 0:1], GL[:, 0:E], axis=AX,
                  negate=True)
            p.dwait("vector")
            vneg = p.inc("dve", 1)
            p.emit("vector", lambda e, s=p.sems["dve"]: e.nop().then_inc(s, 1))
            p.wait("scalar", "dve", vneg)
            va = p.op("scalar", "activation", out=EE[:, 0:E], in_=GL[:, 0:E],
                      func=AF.Exp, bias=SM1[:, 0:1], inc=("act", 1))
            p.wait("vector", "act", va)
            emit_rank_top2(ctx, ph, p, E, GL[:, 0:E], SELg[:, 0:E], CG, CE, JK)
            if mode == "sel":
                p.qop("vector", "tensor_copy",
                      t["GATES"][:, gates_off + i * E:gates_off + (i + 1) * E],
                      SELg[:, 0:E])
                p.dwait("vector")
                continue
            if mode == "renorm":
                p.qop("vector", "scalar_tensor_tensor", out=JK[:, 0:E],
                      in0=EE[:, 0:E], scalar=1.0, in1=SELg[:, 0:E],
                      op0=ALU.mult, op1=ALU.mult, accum_out=SM1[:, 1:2])
            else:
                p.qop("vector", "tensor_scalar", out=JK[:, 0:E],
                      in0=EE[:, 0:E], scalar1=1.0, scalar2=None, op0=ALU.mult,
                      op1=ALU.add, accum_out=SM1[:, 1:2])
                p.dwait("vector")
                p.qop("vector", "tensor_tensor", JK[:, 0:E], EE[:, 0:E],
                      SELg[:, 0:E], ALU.mult)
            p.dwait("vector")
            p.qop("vector", "reciprocal", SM1[:, 2:3], SM1[:, 1:2])
            p.dwait("vector")
            p.qop("vector", "tensor_scalar",
                  out=t["GATES"][:, gates_off + i * E:gates_off + (i + 1) * E],
                  in0=JK[:, 0:E], scalar1=SM1[:, 2:3], scalar2=None,
                  op0=ALU.mult)
            p.dwait("vector")
            p.drip(ctx)
        p.wait("scalar", "dma", p.counts["dma"])
        p.wait("tensor", "dma", p.counts["dma"])
    ph.close()


def _moe_experts(ctx, E, DF, w1_d, w2_d, b1_l_d, b2row_d, xnf_d, out_d,
                 gates_off, act_func, tbw=512, ffdb=True):
    p, t, ps = ctx["p"], ctx["t"], ctx["ps"]
    ph = Phase(ctx)
    NHC = DF // 128
    NJ = tbw // 128
    NTB = S // tbw
    wdb = 2 if DF <= 2048 else 1
    W1B2 = ph.sb("m_W1B", (128, wdb * CH * DF), BF16)
    W2B2 = ph.sb("m_W2B", (128, wdb * NHC * DM), BF16)
    HB = ph.sb("m_HB", (128, NHC * tbw), BF16)
    XL2 = ph.sb("m_XL", (128, 2 * CH * tbw), BF16)
    EOB = ph.sb("m_EOB", (128, tbw), BF16)
    FF2 = ph.sb("m_FF", (128, (2 if ffdb else 1) * NJ * 1024), BF16)
    B1 = ph.sb("m_B1", (128, 2 * NHC), F32)
    B2R = ph.sb("m_B2R", (1, 2 * DM), BF16)
    evac = {}
    xl_consumed = {}
    ff_stored = {}
    wload = {}
    def load_exp(e):
        wp = (e % wdb)
        W1B = W1B2[:, wp * CH * DF:(wp + 1) * CH * DF]
        W2B = W2B2[:, wp * NHC * DM:(wp + 1) * NHC * DM]
        vb1 = p.op("gpsimd", "dma_start", out=B1[:, (e % 2) * NHC:(e % 2 + 1) * NHC],
                   in_=b1_l_d[e], inc=("dmag", 16))
        vb2 = p.op("gpsimd", "dma_start", out=B2R[0:1, (e % 2) * DM:(e % 2 + 1) * DM], in_=b2row_d[e],
                   inc=("dmag", 16))
        for kc in range(CH):
            p.op("gpsimd", "dma_start", out=W1B[:, kc * DF:(kc + 1) * DF],
                 in_=w1_d[e, kc * 128:(kc + 1) * 128, :], inc=("dmag", 16))
        for hc in range(NHC):
            vw = p.op("gpsimd", "dma_start", out=W2B[:, hc * DM:(hc + 1) * DM],
                      in_=w2_d[e, hc * 128:(hc + 1) * 128, :], inc=("dmag", 16))
        wload[e] = vw
    load_exp(0)
    for e in range(E):
        wp = (e % wdb)
        W1B = W1B2[:, wp * CH * DF:(wp + 1) * CH * DF]
        W2B = W2B2[:, wp * NHC * DM:(wp + 1) * NHC * DM]
        if wdb == 1 and e > 0:
            load_exp(e)
        elif wdb == 2 and e + 1 < E:
            load_exp(e + 1)
        p.wait("tensor", "dmag", wload[e])
        p.wait("scalar", "dmag", wload[e])
        xl_loaded = {}
        for tbl in range(NTB + 1):
            if tbl < NTB:
                par = tbl % 2
                if (e, tbl - 2) in xl_consumed:
                    p.wait("sync", "pe", xl_consumed[(e, tbl - 2)])
                elif tbl < 2 and (e - 1, tbl + NTB - 2) in xl_consumed:
                    p.wait("sync", "pe", xl_consumed[(e - 1, tbl + NTB - 2)])
                XL = XL2[:, par * CH * tbw:(par + 1) * CH * tbw]
                vx = p.op("sync", "dma_start",
                          out=XL[:, :].rearrange("p (c n) -> p c n", c=CH),
                          in_=xnf_d[:, tbl * tbw:(tbl + 1) * tbw].rearrange(
                              "(c p) n -> p c n", p=128), inc=("dma", 16))
                xl_loaded[tbl] = vx
            if tbl == 0:
                continue
            tb = tbl - 1
            par = tb % 2
            XL = XL2[:, par * CH * tbw:(par + 1) * CH * tbw]
            FF = FF2[:, par * NJ * 1024:(par + 1) * NJ * 1024] if ffdb else FF2
            p.wait("tensor", "dma", xl_loaded[tb])
            back = 2 if ffdb else 1
            prev = ff_stored.get((e, tb - back)) or ff_stored.get((e - 1, tb + NTB - back))
            if prev is not None:
                p.wait("scalar", prev[0], prev[1])
            for hc in range(NHC):
                bank = (ps["A"], ps["B"], ps["D"])[hc % 3]
                if ("h", hc % 3) in evac:
                    p.wait("tensor", "act", evac[("h", hc % 3)])
                for kc in range(CH):
                    p.mm(bank[:, 0:tbw],
                         W1B[:, kc * DF + hc * 128:kc * DF + (hc + 1) * 128],
                         XL[:, kc * tbw:(kc + 1) * tbw],
                         start=(kc == 0), stop=(kc == CH - 1),
                         inc=("pe", 1) if kc == CH - 1 else None)
                p.wait("scalar", "pe", p.counts["pe"])
                va = p.op("scalar", "activation",
                          out=HB[:, hc * tbw:(hc + 1) * tbw], in_=bank[:, 0:tbw],
                          func=act_func, bias=B1[:, (e % 2) * NHC + hc:(e % 2) * NHC + hc + 1], inc=("act", 1))
                evac[("h", hc % 3)] = va
            xl_consumed[(e, tb)] = p.counts["pe"]
            p.wait("tensor", "act", p.counts["act"])
            for oc in range(CH):
                bank = (ps["C"], ps["F"], ps["G"])[oc % 3]
                if ("eo", oc % 3) in evac:
                    p.wait("tensor", "dve", evac[("eo", oc % 3)])
                for hc in range(NHC):
                    p.mm(bank[:, 0:tbw],
                         W2B[:, hc * DM + oc * 128:hc * DM + (oc + 1) * 128],
                         HB[:, hc * tbw:(hc + 1) * tbw],
                         start=(hc == 0), stop=False)
                p.mm(bank[:, 0:tbw], B2R[0:1, (e % 2) * DM + oc * 128:(e % 2) * DM + (oc + 1) * 128],
                     t["ONES"][0:1, 0:tbw], start=False, stop=True,
                     inc=("pe", 1))
                p.wait("vector", "pe", p.counts["pe"])
                va = p.op("vector", "tensor_copy", EOB[:, 0:tbw],
                          bank[:, 0:tbw], inc=("dve", 1))
                evac[("eo", oc % 3)] = va
                p.wait("tensor", "dve", va)
                for jj in range(NJ):
                    i = tb * NJ + jj
                    if ("fft", jj % 2) in evac:
                        p.wait("tensor", "act", evac[("fft", jj % 2)])
                    vp = p.op("tensor", "transpose",
                              ps["T"][:, (jj % 2) * 1024:(jj % 2) * 1024 + 128],
                              EOB[:, jj * 128:(jj + 1) * 128], t["IDT"][:, :],
                              inc=("pe", 1))
                    p.wait("scalar", "pe", vp)
                    vd = p.op("scalar", "activation",
                              out=FF[:, jj * 1024 + oc * 128:jj * 1024 + (oc + 1) * 128],
                              in_=ps["T"][:, (jj % 2) * 1024:(jj % 2) * 1024 + 128],
                              func=AF.Copy,
                              scale=t["GATES"][:, gates_off + i * E + e:
                                               gates_off + i * E + e + 1],
                              inc=("act", 1))
                    evac[("fft", jj % 2)] = vd
            p.wait("gpsimd", "dve", p.counts["dve"])
            p.wait("gpsimd", "act", p.counts["act"])
            for jj in range(NJ):
                i = tb * NJ + jj
                vs = p.op("gpsimd", "dma_start",
                          out=out_d[i * 128:(i + 1) * 128, :],
                          in_=FF[:, jj * 1024:(jj + 1) * 1024],
                          **({} if e == 0 else {"accum_op": ALU.add}),
                          inc=("dmag", 16))
                ff_stored[(e, tb)] = ("dmag", p.counts["dmag"])
        p.wait("gpsimd", "pe", p.counts["pe"])
        p.wait("gpsimd", "act", p.counts["act"])
        p.wait("gpsimd", "dve", p.counts["dve"])
    ph.close()


def plan_l0_moe(ctx):
    w, scr = ctx["w"], ctx["scr"]
    _gate_prepass(ctx, scr["x_att"], ctx["st2_r"], None, CH, w["gate_w"],
                  E0A, 0, scr["xn2f"], gate_bias=None, mode="renorm")
    _moe_experts(ctx, E0A, DF0A, w["moe_w1"], w["moe_w2"], w["moe_b1_l"],
                 w["moe_b2_row"], scr["xn2f"], scr["ff0"], 0,
                 AF.Gelu_apprx_tanh, tbw=512, ffdb=False)


def plan_l0_ffn(ctx):
    w, scr = ctx["w"], ctx["scr"]
    _gate_prepass(ctx, ctx["x_d"], ctx["st0_r"], ctx["st0_mu"], None,
                  w["fgate_w"], E0F, 128, scr["xnf"], gate_bias=w["fgate_b"],
                  mode="probs")
    _moe_experts(ctx, E0F, DF0F, w["ffn_w1"], w["ffn_w2"], w["ffn_b1_l"],
                 w["ffn_b2_row"], scr["xnf"], scr["gffn"], 128, AF.Relu)


def plan_l0_combine(ctx):
    p, t = ctx["p"], ctx["t"]
    scr = ctx["scr"]
    ph = Phase(ctx)
    ST = t["STATS"]
    base = ctx["stats_base"]
    ctx["st1_mu"] = base
    ctx["st1_r"] = base + NT
    ctx["stats_base"] = base + 2 * NT
    XA = ph.sb("c_XA", (128, 1024), F32)
    XF = ph.sb("c_XF", (128, 1024), F32)
    XG = ph.sb("c_XG", (128, 1024), F32)
    XX = ph.sb("c_XX", (128, 1024), F32)
    DU = ph.sb("c_DU", (128, 1024), F32)
    for i in range(NT):
        sl = slice(i * 128, (i + 1) * 128)
        if i > 0:
            p.wait("sync", "dma", store_v)
            p.wait("sync", "act", p.counts["act"])
        v1 = p.op("sync", "dma_start", out=XA[:, :], in_=scr["x_att"][sl, :],
                  inc=("dma", 16))
        v2 = p.op("sync", "dma_start", out=XF[:, :], in_=scr["ff0"][sl, :],
                  inc=("dma", 16))
        v3 = p.op("sync", "dma_start", out=XG[:, :], in_=scr["gffn"][sl, :],
                  inc=("dma", 16))
        v4 = p.op("sync", "dma_start", out=XX[:, :], in_=ctx["x_d"][sl, :],
                  inc=("dma", 16))
        p.wait("scalar", "dma", v3)
        va = p.op("scalar", "activation", out=XG[:, :], in_=XG[:, :],
                  func=AF.Silu, inc=("act", 1))
        p.wait("vector", "dma", v4)
        p.wait("vector", "act", va)
        p.qop("vector", "tensor_tensor", XA[:, :], XA[:, :], XF[:, :], ALU.add)
        p.dwait("vector")
        p.qop("vector", "tensor_tensor", XA[:, :], XA[:, :], XG[:, :], ALU.mult)
        p.dwait("vector")
        vd = p.op("vector", "tensor_tensor", XA[:, :], XA[:, :], XX[:, :],
                  ALU.add, inc=("dve", 1))
        p.wait("scalar", "dve", vd)
        p.op("scalar", "activation", out=DU[:, :], in_=XA[:, :], func=AF.Copy,
             accum_out=ST[:, base + i:base + i + 1])
        va = p.op("scalar", "activation", out=DU[:, :], in_=XA[:, :],
                  func=AF.Square, accum_out=ST[:, base + NT + i:base + NT + i + 1],
                  inc=("act", 1))
        p.wait("sync", "dve", vd)
        store_v = p.op("sync", "dma_start", out=scr["x1"][sl, :], in_=XA[:, :],
                       inc=("dma", 16))
        p.drip(ctx)
    mu = ST[:, base:base + NT]
    sq = ST[:, base + NT:base + 2 * NT]
    p.wait("vector", "act", p.counts["act"])
    p.qop("vector", "tensor_scalar", out=sq, in0=sq, scalar1=1.0 / DM,
          scalar2=None, op0=ALU.mult)
    p.qop("vector", "tensor_scalar", out=mu, in0=mu, scalar1=1.0 / DM,
          scalar2=None, op0=ALU.mult)
    p.dwait("vector")
    mu2 = ph.sb("c_mu2", (128, NT), F32)
    p.qop("vector", "tensor_tensor", mu2[:, :], mu, mu, ALU.mult)
    p.dwait("vector")
    p.qop("vector", "tensor_tensor", sq, sq, mu2[:, :], ALU.subtract)
    p.dwait("vector")
    vv = p.op("vector", "tensor_scalar", out=sq, in0=sq, scalar1=1e-5,
              scalar2=None, op0=ALU.add, inc=("dve", 1))
    p.wait("scalar", "dve", vv)
    va = p.op("scalar", "activation", out=sq, in_=sq, func=AF.Sqrt,
              inc=("act", 1))
    p.wait("vector", "act", va)
    p.op("vector", "reciprocal", out=sq, in_=sq, inc=("dve", 1))
    ph.close()

def plan_l1_router(ctx):
    w, scr = ctx["w"], ctx["scr"]
    _gate_prepass(ctx, scr["x1"], ctx["st1_r"], ctx["st1_mu"], None,
                  w["router_wT"], E1F, 0, scr["xn1f"],
                  gate_bias=w["router_b"], mode="sel")


def plan_l1_wdkv(ctx):
    p, t, w, ps = ctx["p"], ctx["t"], ctx["w"], ctx["ps"]
    scr = ctx["scr"]
    ph = Phase(ctx)
    LWB = ph.sb("wd_LW", (128, CH * L1L), BF16)
    LB = ph.sb("wd_LB", (128, 3), F32)
    XNF = ph.sb("wd_XNF", (128, CH * 512), BF16)
    WFM = ph.sb("wd_WFM", (128, 3 * 512), BF16)
    WTM = ph.sb("wd_WTM", (128, 4 * 384), BF16)
    NRMS = ph.sb("wd_NR", (128, 40), F32)
    DU = ph.sb("wd_DU", (128, 512), F32)
    for kc in range(CH):
        vw = p.op("gpsimd", "dma_start", out=LWB[:, kc * L1L:(kc + 1) * L1L],
                  in_=w["lat_w"][kc * 128:(kc + 1) * 128, :], inc=("dmag", 16))
    vb = p.op("sync", "dma_start", out=LB[:, :], in_=w["lat_b_l"], inc=("dma", 16))
    p.wait("tensor", "dmag", vw)
    p.wait("tensor", "dma", vb)
    evac = {}
    for wi in range(8):
        p.wait("sync", "pe", p.counts["pe"])
        vx = p.op("sync", "dma_start",
                  out=XNF[:, :].rearrange("p (c n) -> p c n", c=CH),
                  in_=scr["xn1f"][:, wi * 512:(wi + 1) * 512].rearrange(
                      "(c p) n -> p c n", p=128), inc=("dma", 16))
        p.wait("tensor", "dma", vx)
        p.wait("vector", "pe", p.counts["pe"])
        p.wait("vector", "dma", p.counts["dma"])
        p.op("vector", "memset", WFM[:, 2 * 512:3 * 512], 0.0, inc=("dve", 1))
        for mc in range(3):
            mw = 128 if mc < 2 else 32
            bank = ps["A"] if mc % 2 == 0 else ps["B"]
            if ("wd", mc % 2) in evac:
                p.wait("tensor", "dve", evac[("wd", mc % 2)])
            for kc in range(CH):
                p.mm(bank[0:mw, :],
                     LWB[:, kc * L1L + mc * 128:kc * L1L + mc * 128 + mw],
                     XNF[:, kc * 512:(kc + 1) * 512],
                     start=(kc == 0), stop=(kc == CH - 1),
                     inc=("pe", 1) if kc == CH - 1 else None)
            p.wait("vector", "pe", p.counts["pe"])
            p.wait("vector", "dma", p.counts["dma"])
            vd = p.op("vector", "tensor_scalar", out=WFM[0:mw, mc * 512:(mc + 1) * 512],
                      in0=bank[0:mw, :], scalar1=LB[0:mw, mc:mc + 1],
                      scalar2=None, op0=ALU.add, inc=("dve", 1))
            evac[("wd", mc % 2)] = vd
        p.dwait("vector")
        p.wait("sync", "dve", p.counts["dve"])
        for mc in range(3):
            vs = p.op("sync", "dma_start",
                      out=scr["wdkvF"][mc * 128:mc * 128 + (128 if mc < 2 else 32),
                                       wi * 512:(wi + 1) * 512],
                      in_=WFM[0:(128 if mc < 2 else 32), mc * 512:(mc + 1) * 512],
                      inc=("dma", 16))
        p.wait("tensor", "dve", p.counts["dve"])
        for u in range(4):
            for mc in range(3):
                p.op("tensor", "transpose",
                     ps["T"][:, mc * 128:(mc + 1) * 128],
                     WFM[:, mc * 512 + u * 128:mc * 512 + (u + 1) * 128],
                     t["IDT"][:, :],
                     inc=("pe", 1) if mc == 2 else None)
            p.wait("vector", "pe", p.counts["pe"])
            vd = p.op("vector", "tensor_copy", WTM[:, u * 384:(u + 1) * 384],
                      ps["T"][:, 0:384], inc=("dve", 1))
            p.wait("tensor", "dve", vd)
        p.wait("scalar", "dve", p.counts["dve"])
        for u in range(4):
            va = p.op("scalar", "activation", out=DU[:, 0:384],
                      in_=WTM[:, u * 384:(u + 1) * 384], func=AF.Square,
                      accum_out=NRMS[:, wi * 4 + u:wi * 4 + u + 1],
                      inc=("act", 1))
        p.wait("sync", "act", p.counts["act"])
        p.wait("sync", "dve", p.counts["dve"])
        vs = p.op("sync", "dma_start",
                  out=scr["wdkvT"][wi * 512:(wi + 1) * 512, :].rearrange(
                      "(u p) l -> p u l", p=128),
                  in_=WTM[:, :].rearrange("p (u l) -> p u l", u=4),
                  inc=("dma", 16))
        p.wait("vector", "dma", vs)
    p.wait("vector", "act", p.counts["act"])
    p.qop("vector", "reduce_max", NRMS[:, 32:33], NRMS[:, 0:32], axis=AX)
    p.dwait("vector")
    p.qop("vector", "tensor_copy", WTM[0:128, 0:1], NRMS[:, 32:33])
    p.dwait("vector")
    vd = p.inc("dve", 1)
    p.emit("vector", lambda e, s=p.sems["dve"]: e.nop().then_inc(s, 1))
    p.wait("tensor", "dve", vd)
    vp = p.op("tensor", "transpose", ps["T"][:, 0:128], WTM[:, 0:128],
              t["IDT"][:, :], inc=("pe", 1))
    p.wait("vector", "pe", vp)
    p.qop("vector", "reduce_max", NRMS[0:1, 33:34], ps["T"][0:1, 0:128], axis=AX)
    p.dwait("vector")
    p.qop("vector", "tensor_copy", WTM[0:1, 0:1], NRMS[0:1, 33:34])
    p.dwait("vector")
    vd = p.inc("dve", 1)
    p.emit("vector", lambda e, s=p.sems["dve"]: e.nop().then_inc(s, 1))
    p.wait("tensor", "dve", vd)
    p.mm(ps["C"][:, 0:1], t["ONES"][0:1, 0:128], WTM[0:1, 0:1],
         start=True, stop=True, inc=("pe", 1))
    p.wait("scalar", "pe", p.counts["pe"])
    va = p.op("scalar", "activation", out=t["KMX"][:, 32:33],
              in_=ps["C"][:, 0:1], func=AF.Sqrt, inc=("act", 1))
    p.wait("vector", "act", va)
    p.op("vector", "tensor_scalar", out=t["KMX"][:, 32:33],
         in0=t["KMX"][:, 32:33], scalar1=-0.125, scalar2=None, op0=ALU.mult,
         inc=("dve", 1))
    ph.close()


def plan_l1_attn(ctx):
    _l1_attn_pass(ctx, 0)
    _l1_attn_pass(ctx, 1)


def _l1_attn_pass(ctx, P):
    p, t, w, ps = ctx["p"], ctx["t"], ctx["w"], ctx["ps"]
    scr = ctx["scr"]
    ph = Phase(ctx)
    WDW = ph.sb("l1_WDW", (128, 8 * CH * L1L), BF16)
    WDB = ph.sb("l1_WDB", (128, 8 * 3), F32)
    OWB = ph.sb("l1_OWB", (128, 24 * 1024), BF16)
    WFM = ph.sb("l1_WFM", (128, 3 * 4096), BF16)
    WTM = ph.sb("l1_WTM", (128, 32 * 384), BF16)
    XNF = ph.sb("l1_XNF", (128, CH * 256), BF16)
    Q8 = ph.sb("l1_Q8", (128, 24 * 256), BF16)
    QSQ = ph.sb("l1_QSQ", (128, 256), BF16)
    ZP = ph.sb("l1_ZP", (128, 24 * 256), BF16)
    PB = ph.sb("l1_PB", (128, 2 * 4096), BF16)
    PTB = ph.sb("l1_PTB", (128, 2048), BF16)
    AWB = ph.sb("l1_AWB", (128, 4 * 256), F32)
    XATT = ph.sb("l1_XATT", (128, 1024), F32)
    OB = ph.sb("l1_OB", (128, CH), F32)
    QN = ph.sb("l1_QN", (128, 16), F32)
    BND = ph.sb("l1_BND", (128, 16), F32)
    NSK = ph.sb("l1_NSK", (128, 16), F32)
    SMS = ph.sb("l1_SMS", (128, 36), F32)
    hbase = 8 * P

    for h in range(8):
        for kc in range(CH):
            p.op("gpsimd", "dma_start",
                 out=WDW[:, (h * CH + kc) * L1L:(h * CH + kc + 1) * L1L],
                 in_=w["wd_w"][hbase + h, kc * 128:(kc + 1) * 128, :],
                 inc=("dmag", 16))
        p.op("gpsimd", "dma_start", out=WDB[:, h * 3:(h + 1) * 3],
             in_=w["wd_b_l"][hbase + h], inc=("dmag", 16))
    for kc in range(24):
        p.op("gpsimd", "dma_start", out=OWB[:, kc * 1024:(kc + 1) * 1024],
             in_=w["out_w"][P * 3072 + kc * 128:P * 3072 + (kc + 1) * 128, :],
             inc=("dmag", 16))
    vb = p.op("sync", "dma_start", out=OB[:, :], in_=w["out_b_l"], inc=("dma", 16))
    for mc in range(3):
        mw = 128 if mc < 2 else 32
        p.op("sync", "dma_start", out=WFM[0:mw, mc * 4096:(mc + 1) * 4096],
             in_=scr["wdkvF"][mc * 128:mc * 128 + mw, :], inc=("dma", 16))
    vv_ = p.op("sync", "dma_start",
               out=WTM[:, :].rearrange("p (u l) -> p u l", u=32),
               in_=scr["wdkvT"][:, :].rearrange("(u p) l -> p u l", p=128),
               inc=("dma", 16))
    p.wait("vector", "dve", p.counts["dve"])
    for jj in range(16):
        p.qop("vector", "tensor_copy", NSK[:, jj:jj + 1], t["KMX"][:, 32:33])
    p.dwait("vector")
    p.wait("tensor", "dmag", p.counts["dmag"])
    p.wait("tensor", "dma", p.counts["dma"])
    p.op("gpsimd", "memset", ZP[:, :], 0.0, inc=("pool", 1))
    p.wait("scalar", "pool", p.counts["pool"])
    p.wait("vector", "pool", p.counts["pool"])

    evac = {}
    for wi in range(16):
        p.wait("sync", "pe", p.counts["pe"])
        vx = p.op("sync", "dma_start",
                  out=XNF[:, :].rearrange("p (c n) -> p c n", c=CH),
                  in_=scr["xn1f"][:, wi * 256:(wi + 1) * 256].rearrange(
                      "(c p) n -> p c n", p=128), inc=("dma", 16))
        p.wait("tensor", "dma", vx)
        p.wait("tensor", "dve", p.counts["dve"])
        p.wait("tensor", "act", p.counts["act"])
        for h in range(8):
            for mc in range(3):
                mw = 128 if mc < 2 else 32
                bank = ps["A"] if (h * 3 + mc) % 2 == 0 else ps["B"]
                if ("q", (h * 3 + mc) % 2) in evac:
                    p.wait("tensor", "dve", evac[("q", (h * 3 + mc) % 2)])
                for kc in range(CH):
                    p.mm(bank[0:mw, 0:256],
                         WDW[:, (h * CH + kc) * L1L + mc * 128:
                             (h * CH + kc) * L1L + mc * 128 + mw],
                         XNF[:, kc * 256:(kc + 1) * 256],
                         start=(kc == 0), stop=(kc == CH - 1),
                         inc=("pe", 1) if kc == CH - 1 else None)
                p.wait("vector", "pe", p.counts["pe"])
                vd = p.op("vector", "tensor_scalar",
                          out=Q8[0:mw, (h * 3 + mc) * 256:(h * 3 + mc + 1) * 256],
                          in0=bank[0:mw, 0:256], scalar1=WDB[0:mw, h * 3 + mc:h * 3 + mc + 1],
                          scalar2=None, op0=ALU.add, inc=("dve", 1))
                evac[("q", (h * 3 + mc) % 2)] = vd
        p.dwait("vector")
        p.wait("tensor", "dve", p.counts["dve"])
        for h in range(8):
            for mc in range(3):
                mw = 128 if mc < 2 else 32
                p.qop("vector", "tensor_tensor", QSQ[0:mw, 0:256],
                      Q8[0:mw, (h * 3 + mc) * 256:(h * 3 + mc + 1) * 256],
                      Q8[0:mw, (h * 3 + mc) * 256:(h * 3 + mc + 1) * 256],
                      ALU.mult)
                p.dwait("vector")
                vd = p.inc("dve", 1)
                p.emit("vector", lambda e, s=p.sems["dve"]: e.nop().then_inc(s, 1))
                p.wait("tensor", "dve", vd)
                for j in (0, 1):
                    p.mm(ps["D"][:, 8 + j * 8 + h:9 + j * 8 + h],
                         QSQ[0:mw, j * 128:j * 128 + 128], t["ONES"][0:mw, 0:1],
                         start=(mc == 0), stop=(mc == 2),
                         inc=("pe", 1) if (mc == 2 and j == 1) else None)
        p.wait("scalar", "pe", p.counts["pe"])
        qn_evac = p.op("scalar", "activation", out=QN[:, 0:16],
                       in_=ps["D"][:, 8:24], func=AF.Copy, inc=("act", 1))
        p.wait("scalar", "act", qn_evac)
        va = p.op("scalar", "activation", out=QN[:, 0:16], in_=QN[:, 0:16],
                  func=AF.Sqrt, inc=("act", 1))
        p.wait("vector", "act", va)
        vbnd = p.op("vector", "tensor_tensor", BND[:, 0:16], QN[:, 0:16],
                    NSK[:, 0:16], ALU.mult, inc=("dve", 1))
        p.wait("scalar", "dve", vbnd)
        p.wait("tensor", "dve", vbnd)

        for hl in range(8):
            normdone = {}
            for j in (0, 1):
                i = 2 * wi + j
                t0 = i * 128
                extent = t0 + 128
                nch = (extent + 511) // 512
                dchunk = (extent - 1) // 512
                dvar = i % 4
                for sc in range(nch):
                    sw = min(512, extent - sc * 512)
                    bank = ps["A"] if sc % 2 == 0 else ps["B"]
                    if ("sb", sc % 2) in evac:
                        ga, gd = evac[("sb", sc % 2)]
                        p.wait("tensor", "act", ga)
                        p.wait("tensor", "dve", gd)
                    for mc in range(3):
                        mw = 128 if mc < 2 else 32
                        p.mm(bank[:, 0:sw],
                             Q8[0:mw, (hl * 3 + mc) * 256 + j * 128:
                                (hl * 3 + mc) * 256 + (j + 1) * 128],
                             WFM[0:mw, mc * 4096 + sc * 512:mc * 4096 + sc * 512 + sw],
                             start=(mc == 0), stop=(mc == 2),
                             inc=("pe", 1) if mc == 2 else None)
                    p.wait("scalar", "pe", p.counts["pe"])
                    pbsl = PB[:, j * 4096 + sc * 512:j * 4096 + sc * 512 + sw]
                    if sc == dchunk:
                        va = p.op("scalar", "activation", out=pbsl,
                                  in_=bank[:, 0:sw], func=AF.Exp, scale=0.125,
                                  bias=BND[:, j * 8 + hl:j * 8 + hl + 1],
                                  inc=("act", 1))
                        p.wait("vector", "act", va)
                        vd = p.op("vector", "scalar_tensor_tensor", out=pbsl,
                                  in0=pbsl, scalar=1.0,
                                  in1=t["DIAGM"][:, dvar * 512:dvar * 512 + sw],
                                  op0=ALU.mult, op1=ALU.mult,
                                  accum_out=SMS[:, j * 18 + sc:j * 18 + sc + 1],
                                  inc=("dve", 1))
                        evac[("sb", sc % 2)] = (p.counts["act"], vd)
                    else:
                        va = p.op("scalar", "activation", out=pbsl,
                                  in_=bank[:, 0:sw], func=AF.Exp, scale=0.125,
                                  bias=BND[:, j * 8 + hl:j * 8 + hl + 1],
                                  accum_out=SMS[:, j * 18 + sc:j * 18 + sc + 1],
                                  inc=("act", 1))
                        evac[("sb", sc % 2)] = (va, p.counts["dve"])
                        p.wait("vector", "act", va)
                p.dwait("vector")
                p.wait("vector", "act", p.counts["act"])
                p.qop("vector", "reduce_sum", SMS[:, j * 18 + 16:j * 18 + 17],
                      SMS[:, j * 18:j * 18 + nch], axis=AX)
                p.dwait("vector")
                p.qop("vector", "reciprocal", SMS[:, j * 18 + 16:j * 18 + 17],
                      SMS[:, j * 18 + 16:j * 18 + 17])
                p.dwait("vector")
                for sc in range(nch):
                    sw = min(512, extent - sc * 512)
                    normdone[(j, sc)] = p.op(
                        "vector", "tensor_scalar",
                        out=PB[:, j * 4096 + sc * 512:j * 4096 + sc * 512 + sw],
                        in0=PB[:, j * 4096 + sc * 512:j * 4096 + sc * 512 + sw],
                        scalar1=SMS[:, j * 18 + 16:j * 18 + 17], scalar2=None,
                        op0=ALU.mult, inc=("dve", 1))
                if j == 0:
                    tail = p.op("vector", "memset", PB[:, extent:extent + 128],
                                0.0, inc=("dve", 1))
                    normdone[(0, "tail")] = tail
                p.drip(ctx)
            if ("y", 0) in evac:
                p.wait("tensor", "dve", evac[("y", 0)])
            extent1 = (2 * wi + 2) * 128
            nch1 = (extent1 + 511) // 512
            copy_done = {}
            nsub_of = {}
            def emit_T(sc):
                par = sc % 2
                nsub = (min(512, extent1 - sc * 512) + 127) // 128
                nsub_of[sc] = nsub
                nd0 = normdone.get((0, sc), normdone.get((0, "tail"), 0))
                p.wait("tensor", "dve", max(nd0, normdone.get((1, sc), 0)))
                if sc >= 2:
                    cd = copy_done[sc - 2]
                    p.wait("tensor", cd[0], cd[1])
                    p.wait("tensor", cd[2], cd[3])
                for u in range(nsub):
                    if u == 2:
                        cs_, cv_ = copy_done[(sc, 0)]
                        p.wait("tensor", cs_, cv_)
                    for j in (0, 1):
                        p.op("tensor", "transpose",
                             ps["T"][:, par * 1024 + (u % 2) * 256 + j * 128:
                                     par * 1024 + (u % 2) * 256 + (j + 1) * 128],
                             PB[:, j * 4096 + sc * 512 + u * 128:
                                j * 4096 + sc * 512 + (u + 1) * 128],
                             t["IDT"][:, :],
                             inc=("pe", 1) if (u % 2 == 1 or u == nsub - 1) and j == 1 else None)
                    if u % 2 == 1 or u == nsub - 1:
                        grp = u // 2
                        v = p.counts["pe"]
                        ceng = "vector" if grp % 2 == 0 else "scalar"
                        csem = "dve" if grp % 2 == 0 else "act"
                        p.wait(ceng, "pe", v)
                        w = min(2, nsub - grp * 2) * 256
                        if ceng == "vector":
                            vd = p.op("vector", "tensor_copy",
                                      PTB[:, par * 1024 + grp * 512:par * 1024 + grp * 512 + w],
                                      ps["T"][:, par * 1024:par * 1024 + w],
                                      inc=("dve", 1))
                        else:
                            vd = p.op("scalar", "activation",
                                      out=PTB[:, par * 1024 + grp * 512:par * 1024 + grp * 512 + w],
                                      in_=ps["T"][:, par * 1024:par * 1024 + w],
                                      func=AF.Copy, inc=("act", 1))
                        copy_done[(sc, grp)] = (csem, vd)
                copy_done[sc] = ("dve", p.counts["dve"], "act", p.counts["act"])
            def emit_PV(sc):
                par = sc % 2
                nsub = nsub_of[sc]
                for u in range(nsub):
                    cs_, cv_ = copy_done[(sc, u // 2)]
                    p.wait("tensor", cs_, cv_)
                    ug = sc * 4 + u
                    last = (sc == nch1 - 1 and u == nsub - 1)
                    for mc in range(3):
                        mw = 128 if mc < 2 else 32
                        ybank = (ps["C"][0:128, 0:256], ps["D"][0:128, 0:256],
                                 ps["D"][0:32, 256:512])[mc]
                        p.mm(ybank,
                             WTM[:, ug * 384 + mc * 128:ug * 384 + mc * 128 + mw],
                             PTB[:, par * 1024 + u * 256:par * 1024 + (u + 1) * 256],
                             start=emit_PV.first, stop=last,
                             inc=("pe", 1) if (last and mc == 2) else None)
                    emit_PV.first = False
            emit_PV.first = True
            for sc in range(nch1):
                emit_T(sc)
                if sc >= 1:
                    emit_PV(sc - 1)
            emit_PV(nch1 - 1)
            p.wait("vector", "pe", p.counts["pe"])
            for mc in range(3):
                mw = 128 if mc < 2 else 32
                ybank = (ps["C"][0:128, 0:256], ps["D"][0:128, 0:256],
                         ps["D"][0:32, 256:512])[mc]
                vd = p.op("vector", "tensor_copy",
                          ZP[0:mw, (hl * 3 + mc) * 256:(hl * 3 + mc + 1) * 256],
                          ybank, inc=("dve", 1) if mc == 2 else None)
            evac[("y", 0)] = p.counts["dve"]
        p.wait("tensor", "dve", p.counts["dve"])

        p.wait("tensor", "dve", p.counts["dve"])
        for half in (0, 1):
            for oc2 in range(4):
                oc = half * 4 + oc2
                bank = ps["F"]
                if ("wo", 0) in evac:
                    p.wait("tensor", "dve", evac[("wo", 0)])
                for kc in range(24):
                    p.mm(bank[:, 0:256],
                         OWB[:, kc * 1024 + oc * 128:kc * 1024 + (oc + 1) * 128],
                         ZP[:, kc * 256:(kc + 1) * 256],
                         start=(kc == 0), stop=(kc == 23),
                         inc=("pe", 1) if kc == 23 else None)
                p.wait("vector", "pe", p.counts["pe"])
                if P == 0:
                    vd = p.op("vector", "tensor_scalar",
                              out=AWB[:, oc2 * 256:(oc2 + 1) * 256],
                              in0=bank[:, 0:256], scalar1=OB[:, oc:oc + 1],
                              scalar2=None, op0=ALU.add, inc=("dve", 1))
                else:
                    vd = p.op("vector", "tensor_copy",
                              AWB[:, oc2 * 256:(oc2 + 1) * 256],
                              bank[:, 0:256], inc=("dve", 1))
                evac[("wo", 0)] = vd
            p.wait("tensor", "dve", p.counts["dve"])
            for j in (0, 1):
                i = 2 * wi + j
                for oc2 in range(4):
                    oc = half * 4 + oc2
                    vp = p.op("tensor", "transpose", ps["G"][:, 0:128],
                              AWB[:, oc2 * 256 + j * 128:oc2 * 256 + (j + 1) * 128],
                              t["IDT2"][:, :], inc=("pe", 1))
                    p.wait("vector", "pe", vp)
                    vd = p.op("vector", "tensor_copy",
                              XATT[:, oc2 * 128:(oc2 + 1) * 128],
                              ps["G"][:, 0:128], inc=("dve", 1))
                    p.wait("tensor", "dve", vd)
                p.wait("sync", "dve", p.counts["dve"])
                p.wait("gpsimd", "dve", p.counts["dve"])
                if P == 0:
                    vs = p.op("sync", "dma_start",
                              out=scr["s1"][i * 128:(i + 1) * 128,
                                            half * 512:(half + 1) * 512],
                              in_=XATT[:, 0:512], inc=("dma", 16))
                    p.wait("vector", "dma", vs)
                else:
                    vs = p.op("gpsimd", "dma_start",
                              out=scr["s1"][i * 128:(i + 1) * 128,
                                            half * 512:(half + 1) * 512],
                              in_=XATT[:, 0:512], accum_op=ALU.add,
                              inc=("dmag", 16))
                    p.wait("vector", "dmag", vs)
    ph.close()


def plan_l1_ffn(ctx):
    p, t, ps = ctx["p"], ctx["t"], ctx["ps"]
    w, scr = ctx["w"], ctx["scr"]
    ph = Phase(ctx)
    WB = ph.sb("f1_WB", (128, CH * DM), BF16)
    XL2 = ph.sb("f1_XL", (128, 2 * CH * 512), BF16)
    EOB2 = ph.sb("f1_EOB", (128, 512), BF16)
    FF2 = ph.sb("f1_FF", (128, 2 * 4 * 1024), F32)
    B2R = ph.sb("f1_B2R", (1, DM), BF16)
    evac = {}
    xl_consumed = {}
    ff_stored = {}
    for e in range(E1F):
        vb2 = p.op("gpsimd", "dma_start", out=B2R[0:1, :], in_=w["l1f_b_row"][e],
                   inc=("dmag", 16))
        for kc in range(CH):
            vw = p.op("gpsimd", "dma_start", out=WB[:, kc * DM:(kc + 1) * DM],
                      in_=w["l1f_w"][e, kc * 128:(kc + 1) * 128, :],
                      inc=("dmag", 16))
        p.wait("tensor", "dmag", vw)
        p.wait("vector", "dmag", vw)
        xl_loaded = {}
        for tbl in range(9):
            if tbl < 8:
                par = tbl % 2
                if (e, tbl - 2) in xl_consumed:
                    p.wait("sync", "pe", xl_consumed[(e, tbl - 2)])
                elif tbl < 2 and (e - 1, tbl + 6) in xl_consumed:
                    p.wait("sync", "pe", xl_consumed[(e - 1, tbl + 6)])
                XL = XL2[:, par * CH * 512:(par + 1) * CH * 512]
                vx = p.op("sync", "dma_start",
                          out=XL[:, :].rearrange("p (c n) -> p c n", c=CH),
                          in_=scr["xn1f"][:, tbl * 512:(tbl + 1) * 512].rearrange(
                              "(c p) n -> p c n", p=128), inc=("dma", 16))
                xl_loaded[tbl] = vx
            if tbl == 0:
                continue
            tb = tbl - 1
            par = tb % 2
            XL = XL2[:, par * CH * 512:(par + 1) * CH * 512]
            FF = FF2[:, par * 4096:(par + 1) * 4096]
            p.wait("tensor", "dma", xl_loaded[tb])
            prev = ff_stored.get((e, tb - 2)) or ff_stored.get((e - 1, tb + 6))
            if prev is not None:
                p.wait("scalar", prev[0], prev[1])
            for oc in range(CH):
                bank = (ps["C"], ps["F"], ps["G"])[oc % 3]
                if ("eo", oc % 3) in evac:
                    p.wait("tensor", "dve", evac[("eo", oc % 3)])
                for kc in range(CH):
                    p.mm(bank[:, :],
                         WB[:, kc * DM + oc * 128:kc * DM + (oc + 1) * 128],
                         XL[:, kc * 512:(kc + 1) * 512],
                         start=(kc == 0), stop=False)
                p.mm(bank[:, :], B2R[0:1, oc * 128:(oc + 1) * 128],
                     t["ONES"][0:1, 0:512], start=False, stop=True,
                     inc=("pe", 1))
                p.wait("vector", "pe", p.counts["pe"])
                va = p.op("vector", "tensor_copy", EOB2[:, :],
                          bank[:, :], inc=("dve", 1))
                evac[("eo", oc % 3)] = va
                p.wait("tensor", "dve", va)
                for jj in range(4):
                    i = tb * 4 + jj
                    if ("fft", jj % 2) in evac:
                        p.wait("tensor", "act", evac[("fft", jj % 2)])
                    vp = p.op("tensor", "transpose",
                              ps["T"][:, (jj % 2) * 1024:(jj % 2) * 1024 + 128],
                              EOB2[:, jj * 128:(jj + 1) * 128], t["IDT"][:, :],
                              inc=("pe", 1))
                    p.wait("scalar", "pe", vp)
                    vd = p.op("scalar", "activation",
                              out=FF[:, jj * 1024 + oc * 128:jj * 1024 + (oc + 1) * 128],
                              in_=ps["T"][:, (jj % 2) * 1024:(jj % 2) * 1024 + 128],
                              func=AF.Copy,
                              scale=t["GATES"][:, i * E1F + e:
                                               i * E1F + e + 1],
                              inc=("act", 1))
                    evac[("fft", jj % 2)] = vd
            xl_consumed[(e, tb)] = p.counts["pe"]
            p.wait("sync", "dve", p.counts["dve"])
            p.wait("sync", "act", p.counts["act"])
            p.wait("gpsimd", "dve", p.counts["dve"])
            p.wait("gpsimd", "act", p.counts["act"])
            for jj in range(4):
                i = tb * 4 + jj
                if e == 0:
                    vs = p.op("sync", "dma_start",
                              out=scr["l1f"][i * 128:(i + 1) * 128, :],
                              in_=FF[:, jj * 1024:(jj + 1) * 1024],
                              inc=("dma", 16))
                    ff_stored[(e, tb)] = ("dma", p.counts["dma"])
                else:
                    vs = p.op("gpsimd", "dma_start",
                              out=scr["l1f"][i * 128:(i + 1) * 128, :],
                              in_=FF[:, jj * 1024:(jj + 1) * 1024],
                              accum_op=ALU.add, inc=("dmag", 16))
                    ff_stored[(e, tb)] = ("dmag", p.counts["dmag"])
        p.wait("gpsimd", "pe", p.counts["pe"])
        p.wait("gpsimd", "act", p.counts["act"])
        p.wait("gpsimd", "dve", p.counts["dve"])
        if e == 0:
            p.wait("gpsimd", "dma", p.counts["dma"])
    ph.close()


def plan_l1_combine(ctx):
    p, t = ctx["p"], ctx["t"]
    scr = ctx["scr"]
    ph = Phase(ctx)
    XA = ph.sb("o_XA", (128, 1024), F32)
    XG = ph.sb("o_XG", (128, 1024), F32)
    XX = ph.sb("o_XX", (128, 1024), F32)
    for i in range(NT):
        sl = slice(i * 128, (i + 1) * 128)
        if i > 0:
            p.wait("sync", "dma", store_v)
            p.wait("sync", "dve", p.counts["dve"])
            p.wait("sync", "act", p.counts["act"])
        v1 = p.op("sync", "dma_start", out=XA[:, :], in_=scr["s1"][sl, :],
                  inc=("dma", 16))
        v3 = p.op("sync", "dma_start", out=XG[:, :], in_=scr["l1f"][sl, :],
                  inc=("dma", 16))
        v4 = p.op("sync", "dma_start", out=XX[:, :], in_=scr["x1"][sl, :],
                  inc=("dma", 16))
        p.wait("scalar", "dma", v3)
        va = p.op("scalar", "activation", out=XG[:, :], in_=XG[:, :],
                  func=AF.Silu, inc=("act", 1))
        p.wait("vector", "dma", v4)
        p.wait("vector", "act", va)
        p.qop("vector", "tensor_tensor", XA[:, :], XA[:, :], XG[:, :], ALU.mult)
        p.dwait("vector")
        vd = p.op("vector", "tensor_tensor", XA[:, :], XA[:, :], XX[:, :],
                  ALU.add, inc=("dve", 1))
        p.wait("sync", "dve", vd)
        store_v = p.op("sync", "dma_start", out=ctx["out_d"][sl, :],
                       in_=XA[:, :], inc=("dma", 16))
        p.drip(ctx)
    ph.close()


def plan_finish(ctx):
    p = ctx["p"]
    for s_name in list(p.counts):
        p.wait("sync", s_name, p.counts[s_name])


DEFAULT_PLANNERS = None


def build(taps=(), planners=None):
    ctx = make_ctx(taps)
    if planners is None:
        planners = [plan_init,
                    lambda c: plan_stats(c, c["x_d"], "st0"),
                    plan_l0_kv, plan_l0_attn, plan_l0_moe, plan_l0_ffn,
                    plan_l0_combine,
                    plan_l1_router,
                    plan_l1_wdkv, plan_l1_attn, plan_l1_ffn, plan_l1_combine,
                    plan_finish]
    for fn in planners:
        fn(ctx)
    ctx["p"].run_block()
    ctx["stack"].close()
    return ctx


def make_in_maps(d, cores=8):
    x_full = d["L0_attn_x"]
    return [{k: (np.ascontiguousarray(x_full[b]) if k == "L0_attn_x" else d[k])
             for k in DECLARED_INPUTS} for b in range(cores)]


def kernel(**inputs):
    d = host_prep(inputs)
    ctx = build()
    per_core = make_in_maps(d, 8)
    res = run_bass_kernel_spmd(ctx["nc"], per_core, core_ids=list(range(8)))
    return np.stack([res.results[b]["out"] for b in range(8)], axis=0)


# version 26
# speedup vs baseline: 1.1931x; 1.1931x over previous
import numpy as np
import ml_dtypes
from contextlib import ExitStack

import concourse.bass as bass
import concourse.mybir as mybir
from concourse.bass_utils import run_bass_kernel_spmd
from concourse.alu_op_type import AluOpType as ALU

F32 = mybir.dt.float32
BF16 = mybir.dt.bfloat16
F16 = mybir.dt.float16
AF = mybir.ActivationFunctionType
AX = mybir.AxisListType.X

S, DM = 4096, 1024
NT = S // 128
H, DH = 16, 64
HI, DI = 4, 64
TOPK = 512
NEG = -1.0e9
CH = DM // 128
L1H, L1L = 16, 288
E0A, DF0A = 4, 4096
E0F, DF0F = 8, 2048
E1F = 8
NBIS = 16

ENGINES = ("sync", "gpsimd", "vector", "scalar", "tensor")

DECLARED_INPUTS = [
    "L0_attn_x", "L0_attn_wq", "L0_attn_wk", "L0_attn_wv", "L0_attn_wo",
    "L0_attn_idx_wq", "L0_attn_idx_wk", "L0_attn_idx_ww",
    "L0_attn_gate_w", "L0_attn_w1", "L0_attn_w2", "n1w_l", "n2w_l",
    "moe_b1_l", "moe_b2_l", "moe_b2_row", "ffn_b2_row", "l1f_b_row",
    "L0_ffn_gate_w", "L0_ffn_gate_b",
    "L0_ffn_w1", "L0_ffn_w2", "ffn_b1_l", "ffn_b2_l",
    "L1_attn_latent_w", "lat_b_l", "L1_attn_Wd_w", "wd_b_l",
    "out_w_pad", "out_b_l", "L1_ffn_weight", "l1f_b_l",
    "router_wT", "L1_ffn_router_bias",
]


class Plan:
    def __init__(self, nc, stack):
        self.nc = nc
        self.items = []
        self.sems = {}
        self.counts = {}
        self._stack = stack
        self.nbar = 0

    def sem(self, name):
        if name not in self.sems:
            self.sems[name] = self._stack.enter_context(self.nc.semaphore(name))
            self.counts[name] = 0
        return self.sems[name]

    def inc(self, name, by):
        self.sem(name)
        self.counts[name] += by
        return self.counts[name]

    def emit(self, engine, fn):
        assert engine in ENGINES
        self.items.append((engine, fn))

    def wait(self, engine, sem_name, val):
        if val <= 0:
            return
        s = self.sem(sem_name)
        self.emit(engine, lambda e, s=s, v=val: e.wait_ge(s, v))

    def op(self, engine, meth, *args, inc=None, **kwargs):
        if inc is not None:
            s = self.sem(inc[0])
            self.counts[inc[0]] += inc[1]
            def fn(e, meth=meth, args=args, kwargs=kwargs, s=s, by=inc[1]):
                getattr(e, meth)(*args, **kwargs).then_inc(s, by)
            self.emit(engine, fn)
            return self.counts[inc[0]]
        def fn(e, meth=meth, args=args, kwargs=kwargs):
            getattr(e, meth)(*args, **kwargs)
        self.emit(engine, fn)
        return None

    def mm(self, psum_ap, lhsT, rhs, start, stop, inc=None):
        return self.op("tensor", "matmul", psum_ap, lhsT, rhs,
                       start=start, stop=stop, inc=inc)

    def qop(self, engine, meth, *args, **kwargs):
        return self.op(engine, meth, *args, inc=("q_" + engine, 1), **kwargs)

    MAIN_SEM = {"vector": "dve", "scalar": "act", "tensor": "pe",
                "gpsimd": "pool", "sync": "dma"}

    def dwait(self, engine, val=None):
        self.wait(engine, "q_" + engine, self.counts.get("q_" + engine, 0))
        ms = self.MAIN_SEM[engine]
        self.wait(engine, ms, self.counts.get(ms, 0))

    def drip(self, ctx, sem="q_vector"):
        t, ps = ctx["t"], ctx["ps"]
        self.wait("tensor", sem, self.counts.get(sem, 0))
        self.op("tensor", "matmul", ps["D"][0:1, 508:509],
                t["ONES"][0:1, 0:1], t["ONES"][0:1, 0:1],
                start=True, stop=True)

    def barrier(self):
        self.nbar += 1
        k = self.nbar
        for eng in ENGINES:
            self.op(eng, "nop", inc=("bar_" + eng, 1))
        snapshot = [(s, c) for s, c in self.counts.items()
                    if c > 0 and not s.startswith("bar_")]
        for eng in ENGINES:
            for other in ENGINES:
                if other != eng:
                    self.wait(eng, "bar_" + other, k)
            for s, c in snapshot:
                self.wait(eng, s, c)

    def run_block(self):
        with self.nc.Block() as block:
            for eng_name in ENGINES:
                items = [fn for e, fn in self.items if e == eng_name]
                if not items:
                    continue
                def body(engine, items=items):
                    for fn in items:
                        fn(engine)
                getattr(block, eng_name)(body)


def host_prep(inputs):
    d = dict(inputs)
    def featvec(v):
        F = v.shape[-1]
        return np.ascontiguousarray(v.reshape(F // 128, 128).T)
    def padlast(v, tot):
        pv = np.zeros(v.shape[:-1] + (tot,), v.dtype)
        pv[..., :v.shape[-1]] = v
        return pv
    d["n1w_l"] = featvec(inputs["L0_attn_norm1_w"])
    d["n2w_l"] = featvec(inputs["L0_attn_norm2_w"])
    d["moe_b1_l"] = np.ascontiguousarray(
        inputs["L0_attn_b1"].reshape(E0A, DF0A // 128, 128).transpose(0, 2, 1))
    d["moe_b2_l"] = np.ascontiguousarray(
        inputs["L0_attn_b2"].reshape(E0A, CH, 128).transpose(0, 2, 1))
    d["ffn_b1_l"] = np.ascontiguousarray(
        inputs["L0_ffn_b1"].reshape(E0F, DF0F // 128, 128).transpose(0, 2, 1))
    d["ffn_b2_l"] = np.ascontiguousarray(
        inputs["L0_ffn_b2"].reshape(E0F, CH, 128).transpose(0, 2, 1))
    d["lat_b_l"] = np.ascontiguousarray(
        padlast(inputs["L1_attn_latent_b"], 384).reshape(3, 128).T)
    d["wd_b_l"] = np.ascontiguousarray(
        padlast(inputs["L1_attn_Wd_b"], 384).reshape(L1H, 3, 128).transpose(0, 2, 1))
    d["out_b_l"] = featvec(inputs["L1_attn_out_b"])
    d["l1f_b_l"] = np.ascontiguousarray(
        inputs["L1_ffn_bias"].reshape(E1F, CH, 128).transpose(0, 2, 1))
    d["router_wT"] = np.ascontiguousarray(inputs["L1_ffn_router_weight"].T)
    ow = inputs["L1_attn_out_w"]
    owp = np.zeros((L1H * 384, DM), ow.dtype)
    for h in range(L1H):
        owp[h * 384:h * 384 + L1L] = ow[h * L1L:(h + 1) * L1L]
    d["out_w_pad"] = owp
    d["L0_ffn_gate_b"] = np.ascontiguousarray(inputs["L0_ffn_gate_b"].reshape(1, -1))
    d["moe_b2_row"] = np.ascontiguousarray(inputs["L0_attn_b2"].reshape(E0A, 1, DM))
    d["ffn_b2_row"] = np.ascontiguousarray(inputs["L0_ffn_b2"].reshape(E0F, 1, DM))
    d["l1f_b_row"] = np.ascontiguousarray(inputs["L1_ffn_bias"].reshape(E1F, 1, DM))
    d["L1_ffn_router_bias"] = np.ascontiguousarray(inputs["L1_ffn_router_bias"].reshape(1, -1))
    return d


def make_ctx(taps=()):
    nc = bass.Bass("TRN2", target_bir_lowering=False)
    stack = ExitStack()
    p = Plan(nc, stack)

    def din(name, shape):
        return nc.dram_tensor(name, list(shape), F32, kind="ExternalInput").ap()

    w = {}
    x_d = din("L0_attn_x", (S, DM))
    w["wq"] = din("L0_attn_wq", (DM, DM))
    w["wk"] = din("L0_attn_wk", (DM, DM))
    w["wv"] = din("L0_attn_wv", (DM, DM))
    w["wo"] = din("L0_attn_wo", (DM, DM))
    w["idx_wq"] = din("L0_attn_idx_wq", (DM, HI * DI))
    w["idx_wk"] = din("L0_attn_idx_wk", (DM, DI))
    w["idx_ww"] = din("L0_attn_idx_ww", (DM, HI))
    w["gate_w"] = din("L0_attn_gate_w", (DM, E0A))
    w["moe_w1"] = din("L0_attn_w1", (E0A, DM, DF0A))
    w["moe_w2"] = din("L0_attn_w2", (E0A, DF0A, DM))
    w["n1w_l"] = din("n1w_l", (128, CH))
    w["n2w_l"] = din("n2w_l", (128, CH))
    w["moe_b1_l"] = din("moe_b1_l", (E0A, 128, DF0A // 128))
    w["moe_b2_l"] = din("moe_b2_l", (E0A, 128, CH))
    w["moe_b2_row"] = din("moe_b2_row", (E0A, 1, DM))
    w["ffn_b2_row"] = din("ffn_b2_row", (E0F, 1, DM))
    w["l1f_b_row"] = din("l1f_b_row", (E1F, 1, DM))
    w["fgate_w"] = din("L0_ffn_gate_w", (DM, E0F))
    w["fgate_b"] = din("L0_ffn_gate_b", (1, E0F))
    w["ffn_w1"] = din("L0_ffn_w1", (E0F, DM, DF0F))
    w["ffn_w2"] = din("L0_ffn_w2", (E0F, DF0F, DM))
    w["ffn_b1_l"] = din("ffn_b1_l", (E0F, 128, DF0F // 128))
    w["ffn_b2_l"] = din("ffn_b2_l", (E0F, 128, CH))
    w["lat_w"] = din("L1_attn_latent_w", (DM, L1L))
    w["lat_b_l"] = din("lat_b_l", (128, 3))
    w["wd_w"] = din("L1_attn_Wd_w", (L1H, DM, L1L))
    w["wd_b_l"] = din("wd_b_l", (L1H, 128, 3))
    w["out_w"] = din("out_w_pad", (L1H * 384, DM))
    w["out_b_l"] = din("out_b_l", (128, CH))
    w["l1f_w"] = din("L1_ffn_weight", (E1F, DM, DM))
    w["l1f_b_l"] = din("l1f_b_l", (E1F, 128, CH))
    w["router_wT"] = din("router_wT", (DM, E1F))
    w["router_b"] = din("L1_ffn_router_bias", (1, E1F))

    out_d = nc.dram_tensor("out", [S, DM], F32, kind="ExternalOutput").ap()

    scr = {}
    def dscr(nm, shape, dt=F32):
        kind = "ExternalOutput" if nm in taps else "Internal"
        scr[nm] = nc.dram_tensor("scr_" + nm, list(shape), dt, kind=kind).ap()
        return scr[nm]
    dscr("x_att", (S, DM))
    dscr("ff0", (S, DM))
    dscr("gffn", (S, DM))
    dscr("x1", (S, DM))
    dscr("s1", (S, DM))
    dscr("l1f", (S, DM))
    dscr("k0", (DM, S), BF16)
    dscr("v0", (S, DM), BF16)
    dscr("sel", (S, S), BF16)
    dscr("wdkvT", (S, 384), BF16)
    dscr("xn2f", (DM, S), BF16)
    dscr("xnf", (DM, S), BF16)
    dscr("xn1f", (DM, S), BF16)
    dscr("wdkvF", (384, S), BF16)

    def sbp(name, shape, dt=F32):
        return stack.enter_context(nc.sbuf_tensor(name, list(shape), dt))
    t = {}
    t["IDT"] = sbp("IDT", (128, 128), BF16)
    t["IDT2"] = sbp("IDT2", (128, 128), F32)
    t["ONES"] = sbp("ONES", (128, 512), BF16)
    t["ZER"] = sbp("ZER", (128, 512), BF16)
    t["DIAG"] = sbp("DIAG", (128, 4 * 512), BF16)
    t["DIAGM"] = sbp("DIAGM", (128, 4 * 512), BF16)
    t["STATS"] = sbp("STATS", (128, 192), F32)
    t["NRM"] = sbp("NRM", (128, 2 * CH), F32)
    t["KMX"] = sbp("KMX", (128, 48), F32)
    t["GATES"] = sbp("GATES", (128, 384), F32)

    ps = {}
    for nm in ("A", "B", "C", "D", "F", "G"):
        ps[nm] = stack.enter_context(nc.psum_tensor("ps" + nm, [128, 512], F32))
    ps["T"] = stack.enter_context(nc.psum_tensor("psT", [128, 2048], BF16))

    return {"nc": nc, "p": p, "stack": stack, "w": w, "x_d": x_d,
            "out_d": out_d, "scr": scr, "t": t, "ps": ps, "taps": taps,
            "stats_base": 0}


PHASE_NO = [0]


class Phase:
    def __init__(self, ctx):
        self.ctx = ctx
        self.stack = ExitStack()
        self.nc = ctx["nc"]
        PHASE_NO[0] += 1
        self.pid = PHASE_NO[0]

    def sb(self, name, shape, dt=F32):
        return self.stack.enter_context(
            self.nc.sbuf_tensor(f"p{self.pid}_{name}", list(shape), dt))

    def close(self):
        self.stack.close()
        self.ctx["p"].barrier()



def plan_init(ctx):
    p, t, w = ctx["p"], ctx["t"], ctx["w"]
    p.op("gpsimd", "memset", t["IDT"][:, :], 0.0)
    p.op("gpsimd", "affine_select", out=t["IDT"][:, :], in_=t["IDT"][:, :],
         compare_op=ALU.not_equal, fill=1.0, base=0, pattern=[[-1, 128]],
         channel_multiplier=1)
    p.op("gpsimd", "memset", t["IDT2"][:, :], 0.0)
    p.op("gpsimd", "affine_select", out=t["IDT2"][:, :], in_=t["IDT2"][:, :],
         compare_op=ALU.not_equal, fill=1.0, base=0, pattern=[[-1, 128]],
         channel_multiplier=1)
    p.op("gpsimd", "memset", t["ONES"][:, :], 1.0)
    p.op("gpsimd", "memset", t["ZER"][:, :], 0.0)
    for v in range(4):
        sl = t["DIAG"][:, v * 512:(v + 1) * 512]
        p.op("gpsimd", "memset", sl, 0.0)
        p.op("gpsimd", "affine_select", out=sl, in_=sl, compare_op=ALU.is_ge,
             fill=NEG, base=128 * v, pattern=[[-1, 512]], channel_multiplier=1)
        slm = t["DIAGM"][:, v * 512:(v + 1) * 512]
        p.op("gpsimd", "memset", slm, 1.0)
        p.op("gpsimd", "affine_select", out=slm, in_=slm, compare_op=ALU.is_ge,
             fill=0.0, base=128 * v, pattern=[[-1, 512]], channel_multiplier=1,
             inc=("pool", 1))
    p.op("sync", "dma_start", out=t["NRM"][:, 0:CH], in_=w["n1w_l"], inc=("dma", 16))
    p.op("sync", "dma_start", out=t["NRM"][:, CH:2 * CH], in_=w["n2w_l"],
         inc=("dma", 16))
    p.barrier()



def plan_stats(ctx, src_d, tag, rms=False, eps=1e-5):
    p, t = ctx["p"], ctx["t"]
    ph = Phase(ctx)
    base = ctx["stats_base"]
    ncol = NT if rms else 2 * NT
    ctx[tag + "_mu"] = None if rms else base
    ctx[tag + "_r"] = base + (0 if rms else NT)
    ctx["stats_base"] = base + ncol
    ST = t["STATS"]
    rbase = ctx[tag + "_r"]
    xw = ph.sb("stx", (128, 2048), F32)
    dump = ph.sb("stdump", (128, 1024), F32)
    act_after = []
    for i in range(NT):
        xt = xw[:, (i % 2) * 1024:(i % 2) * 1024 + 1024]
        if i >= 2:
            p.wait("sync", "act", act_after[i - 2])
        v = p.op("sync", "dma_start", out=xt, in_=src_d[i * 128:(i + 1) * 128, :],
                 inc=("dma", 16))
        p.wait("scalar", "dma", v)
        if not rms:
            p.op("scalar", "activation", out=dump[:, :], in_=xt,
                 func=AF.Copy, accum_out=ST[:, base + i:base + i + 1])
        va = p.op("scalar", "activation", out=dump[:, :], in_=xt,
                  func=AF.Square, accum_out=ST[:, rbase + i:rbase + i + 1],
                  inc=("act", 1))
        act_after.append(va)
        p.drip(ctx, "act")
    sq = ST[:, rbase:rbase + NT]
    p.wait("vector", "act", act_after[-1])
    p.qop("vector", "tensor_scalar", out=sq, in0=sq, scalar1=1.0 / DM,
          scalar2=None, op0=ALU.mult)
    if not rms:
        mu = ST[:, base:base + NT]
        p.qop("vector", "tensor_scalar", out=mu, in0=mu, scalar1=1.0 / DM,
              scalar2=None, op0=ALU.mult)
        p.dwait("vector")
        mu2 = ph.sb("stmu2", (128, NT), F32)
        p.qop("vector", "tensor_tensor", mu2[:, :], mu, mu, ALU.mult)
        p.dwait("vector")
        p.qop("vector", "tensor_tensor", sq, sq, mu2[:, :], ALU.subtract)
    p.dwait("vector")
    vv = p.op("vector", "tensor_scalar", out=sq, in0=sq, scalar1=eps,
              scalar2=None, op0=ALU.add, inc=("dve", 1))
    p.wait("scalar", "dve", vv)
    va = p.op("scalar", "activation", out=sq, in_=sq, func=AF.Sqrt, inc=("act", 1))
    p.wait("vector", "act", va)
    p.op("vector", "reciprocal", out=sq, in_=sq, inc=("dve", 1))
    ph.close()



def ln_apply(ctx, ph, p, xt_f32, out_f32, tile_i, mu_base, r_base, inc=None):
    ST = ctx["t"]["STATS"]
    if mu_base is None:
        return p.op("vector", "tensor_scalar", out=out_f32, in0=xt_f32,
                    scalar1=ST[:, r_base + tile_i:r_base + tile_i + 1],
                    scalar2=None, op0=ALU.mult, inc=inc)
    return p.op("vector", "tensor_scalar", out=out_f32, in0=xt_f32,
                scalar1=ST[:, mu_base + tile_i:mu_base + tile_i + 1],
                scalar2=ST[:, r_base + tile_i:r_base + tile_i + 1],
                op0=ALU.subtract, op1=ALU.mult, inc=inc)


def transpose_tm_to_fm(ctx, p, src_bf16_1024, dst_fm, wait_pe_base=None,
                       scale_cols=None, n1w_off=None, width=128):
    raise NotImplementedError



def plan_l0_kv(ctx):
    p, t, w, ps = ctx["p"], ctx["t"], ctx["w"], ctx["ps"]
    scr = ctx["scr"]
    nc = ctx["nc"]
    ph = Phase(ctx)
    ST = t["STATS"]
    mu0, r0 = ctx["st0_mu"], ctx["st0_r"]
    rho = ctx["stats_base"]; ctx["rho1"] = rho
    ctx["stats_base"] += NT

    wkB = ph.sb("wkB", (128, CH * DM), BF16)
    wvB = ph.sb("wvB", (128, CH * DM), BF16)
    wiB = ph.sb("wiB", (128, CH * DI), BF16)
    xw = ph.sb("kv_x", (128, 2048), F32)
    xn = ph.sb("kv_xn", (128, 1024), F32)
    xnb = ph.sb("kv_xnb", (128, 1024), BF16)
    xnF = ph.sb("kv_xnF", (128, CH * 512), BF16)
    kout = ph.sb("kv_out", (128, 2 * 512), BF16)
    kiout = ph.sb("kv_kio", (128, 512), BF16)
    vt = ph.sb("kv_vt", (128, 512), BF16)
    ksq = ph.sb("kv_ksq", (128, 512), BF16)
    knb = ph.sb("kv_knb", (128, 8), BF16)
    dump = ph.sb("kv_dump", (128, 1024), F32)
    kn = ph.sb("kv_kn", (128, 128), F32)
    s01 = ph.sb("kv_s01", (128, 2), F32)

    for kc in range(CH):
        p.op("gpsimd", "dma_start", out=wkB[:, kc * DM:(kc + 1) * DM],
             in_=w["wk"][kc * 128:(kc + 1) * 128, :], inc=("dmag", 16))
        p.op("gpsimd", "dma_start", out=wvB[:, kc * DM:(kc + 1) * DM],
             in_=w["wv"][kc * 128:(kc + 1) * 128, :], inc=("dmag", 16))
        vw = p.op("gpsimd", "dma_start", out=wiB[:, kc * DI:(kc + 1) * DI],
                  in_=w["idx_wk"][kc * 128:(kc + 1) * 128, :], inc=("dmag", 16))
    p.wait("tensor", "dmag", vw)

    kI_hbm = nc.dram_tensor(
        "scr_kI", [DI, S], BF16,
        kind="ExternalOutput" if "kI" in ctx["taps"] else "Internal").ap()
    scr["kI"] = kI_hbm

    evac_act = {}
    ln_dve = {}
    nw = S // 512
    for wi in range(nw):
        p.wait("tensor", "dve", p.counts["dve"])
        for j in range(4):
            i = wi * 4 + j
            xt = xw[:, (i % 2) * 1024:(i % 2) * 1024 + 1024]
            if i >= 2:
                p.wait("sync", "dve", ln_dve[i - 2])
            v = p.op("sync", "dma_start", out=xt,
                     in_=ctx["x_d"][i * 128:(i + 1) * 128, :], inc=("dma", 16))
            p.wait("vector", "dma", v)
            ln_dve[i] = ln_apply(ctx, ph, p, xt, xn[:, :], i, mu0, r0,
                                 inc=("dve", 1))
            p.wait("scalar", "dve", ln_dve[i])
            p.op("scalar", "activation", out=dump[:, 0:512], in_=xn[:, 0:512],
                 func=AF.Square, accum_out=s01[:, 0:1])
            va = p.op("scalar", "activation", out=dump[:, 512:1024],
                      in_=xn[:, 512:1024], func=AF.Square,
                      accum_out=s01[:, 1:2], inc=("act", 1))
            p.wait("vector", "act", va)
            rr = ST[:, rho + i:rho + i + 1]
            p.qop("vector", "tensor_tensor", rr, s01[:, 0:1], s01[:, 1:2],
                  ALU.add)
            p.dwait("vector")
            vd = p.op("vector", "tensor_scalar", out=rr, in0=rr,
                      scalar1=1.0 / DM, scalar2=1e-6, op0=ALU.mult, op1=ALU.add,
                      inc=("dve", 1))
            p.wait("scalar", "dve", vd)
            va = p.op("scalar", "activation", out=rr, in_=rr, func=AF.Sqrt,
                      inc=("act", 1))
            p.wait("vector", "act", va)
            p.qop("vector", "reciprocal", rr, rr)
            p.dwait("vector")
            p.wait("vector", "pe", p.counts.get("pe", 0))
            vd = p.op("vector", "tensor_scalar", out=xnb[:, :], in0=xn[:, :],
                      scalar1=rr, scalar2=None, op0=ALU.mult, inc=("dve", 1))
            p.wait("tensor", "dve", vd)
            for c in range(CH):
                vp = p.op("tensor", "transpose", ps["T"][:, 0:128],
                          xnb[:, c * 128:(c + 1) * 128], t["IDT"][:, :],
                          inc=("pe", 1))
                p.wait("scalar", "pe", vp)
                va = p.op("scalar", "activation",
                          out=xnF[:, c * 512 + j * 128:c * 512 + (j + 1) * 128],
                          in_=ps["T"][:, 0:128], func=AF.Copy,
                          scale=t["NRM"][:, c:c + 1], inc=("act", 1))
                p.wait("tensor", "act", va)

        p.wait("tensor", "act", p.counts["act"])
        for which, wbuf, dst in (("k", wkB, scr["k0"]), ("v", wvB, scr["v0"])):
            for oc in range(CH):
                bank = ps["A"] if oc % 2 == 0 else ps["B"]
                key = (which, oc % 2)
                if key in evac_act:
                    p.wait("tensor", "act", evac_act[key])
                for kc in range(CH):
                    p.mm(bank[:, :],
                         wbuf[:, kc * DM + oc * 128:kc * DM + (oc + 1) * 128],
                         xnF[:, kc * 512:(kc + 1) * 512],
                         start=(kc == 0), stop=(kc == CH - 1),
                         inc=("pe", 1) if kc == CH - 1 else None)
                p.wait("scalar", "pe", p.counts["pe"])
                ko = kout[:, (oc % 2) * 512:(oc % 2) * 512 + 512]
                if oc >= 2:
                    p.wait("scalar", "dma", dma_guard[(which, oc % 2)])
                va = p.op("scalar", "activation", out=ko, in_=bank[:, :],
                          func=AF.Copy, inc=("act", 1))
                evac_act[key] = va
                if "dma_guard" not in dir():
                    dma_guard = {}
                if which == "k":
                    p.wait("sync", "act", va)
                    vs = p.op("sync", "dma_start",
                              out=dst[oc * 128:(oc + 1) * 128,
                                      wi * 512:(wi + 1) * 512],
                              in_=ko, inc=("dma", 16))
                    dma_guard[(which, oc % 2)] = vs
                else:
                    p.wait("tensor", "act", va)
                    for u in range(4):
                        p.op("tensor", "transpose",
                             ps["T"][:, u * 128:(u + 1) * 128],
                             ko[:, u * 128:(u + 1) * 128], t["IDT"][:, :],
                             inc=("pe", 1) if u == 3 else None)
                    p.wait("vector", "pe", p.counts["pe"])
                    vd = p.op("vector", "tensor_copy", vt[:, :],
                              ps["T"][:, 0:512], inc=("dve", 1))
                    p.wait("tensor", "dve", vd)
                    p.wait("sync", "dve", vd)
                    vs = p.op("sync", "dma_start",
                              out=dst[wi * 512:(wi + 1) * 512,
                                      oc * 128:(oc + 1) * 128].rearrange(
                                          "(u p) f -> p u f", p=128),
                              in_=vt[:, :].rearrange("p (u f) -> p u f", u=4),
                              inc=("dma", 16))
                    dma_guard[(which, oc % 2)] = vs
                    p.wait("vector", "dma", vs)
                if which == "k":
                    p.wait("vector", "act", va)
                    vd = p.op("vector", "tensor_tensor", ksq[:, :], ko, ko,
                              ALU.mult, inc=("dve", 1))
                    p.wait("tensor", "dve", vd)
                    vp = p.mm(ps["C"][0:1, 0:512], t["ONES"][:, 0:1],
                              ksq[:, :], start=True, stop=True, inc=("pe", 1))
                    p.wait("vector", "pe", vp)
                    p.qop("vector", "reduce_max",
                          kn[0:1, oc * 8 + wi:oc * 8 + wi + 1],
                          ps["C"][0:1, 0:512], axis=AX)
        if ("ki", 0) in evac_act:
            p.wait("tensor", "act", evac_act[("ki", 0)])
        for kc in range(CH):
            p.mm(ps["D"][0:64, :], wiB[:, kc * DI:(kc + 1) * DI],
                 xnF[:, kc * 512:(kc + 1) * 512],
                 start=(kc == 0), stop=(kc == CH - 1),
                 inc=("pe", 1) if kc == CH - 1 else None)
        p.wait("scalar", "pe", p.counts["pe"])
        va = p.op("scalar", "activation", out=kiout[0:64, 0:512],
                  in_=ps["D"][0:64, :], func=AF.Copy, inc=("act", 1))
        evac_act[("ki", 0)] = va
        p.wait("sync", "act", va)
        vki = p.op("sync", "dma_start", out=kI_hbm[:, wi * 512:(wi + 1) * 512],
                   in_=kiout[0:64, 0:512], inc=("dma", 16))
        p.wait("scalar", "dma", vki)

    km = t["KMX"]
    p.dwait("vector")
    for c in range(CH):
        p.qop("vector", "reduce_max", kn[0:1, 64 + c:65 + c],
              kn[0:1, c * 8:c * 8 + 8], axis=AX)
    p.dwait("vector")
    p.op("vector", "tensor_copy", knb[0:1, 0:CH], kn[0:1, 64:64 + CH],
         inc=("dve", 1))
    p.wait("tensor", "dve", p.counts["dve"])
    p.mm(ps["C"][:, 0:CH], t["ONES"][0:1, 0:128], knb[0:1, 0:CH],
         start=True, stop=True, inc=("pe", 1))
    p.wait("scalar", "pe", p.counts["pe"])
    va = p.op("scalar", "activation", out=km[:, 0:CH], in_=ps["C"][:, 0:CH],
              func=AF.Sqrt, inc=("act", 1))
    p.wait("vector", "act", va)
    p.op("vector", "tensor_scalar", out=km[:, CH:2 * CH], in0=km[:, 0:CH],
         scalar1=-0.125, scalar2=None, op0=ALU.mult, inc=("dve", 1))
    ph.close()



def plan_l0_attn(ctx):
    _l0_attn_pass(ctx, 0)
    _l0_attn_pass(ctx, 1)
    _l0_xatt_fix(ctx)


def _l0_attn_pass(ctx, P):
    p, t, w, ps = ctx["p"], ctx["t"], ctx["w"], ctx["ps"]
    scr = ctx["scr"]
    ph = Phase(ctx)
    ST = t["STATS"]
    mu0, r0, rho = ctx["st0_mu"], ctx["st0_r"], ctx["rho1"]

    KH = ph.sb("a_KH", (128, 4 * 4096), BF16)
    VH = ph.sb("a_VH", (128, 32 * 512), BF16)
    WQB = ph.sb("a_WQB", (128, CH * 512), BF16)
    WOB = ph.sb("a_WOB", (64, 8 * 1024), BF16)
    XNF = ph.sb("a_XNF", (128, CH * 256), BF16)
    QW = ph.sb("a_QW", (128, 4 * 256), BF16)
    QSQ = ph.sb("a_QSQ", (128, 256), BF16)
    PB = ph.sb("a_PB", (128, 2 * 4096), BF16)
    PTB = ph.sb("a_PTB", (128, 2048), BF16)
    SEL = ph.sb("a_SEL", (128, 2 * 4096), BF16)
    XNT = ph.sb("a_XNT", (128, 1024), F32)
    XNB = ph.sb("a_XNB", (128, 1024), BF16)
    XT = ph.sb("a_XT", (128, 1024), F32)
    AOH = ph.sb("a_AOH", (64, 8 * 256), BF16)
    AWBs = ph.sb("a_AWBs", (128, 4 * 256), F32)
    XATT = ph.sb("a_XATT", (128, 1024), F32)
    QN = ph.sb("a_QN", (128, 8), F32)
    BND = ph.sb("a_BND", (128, 8), F32)
    BKM = ph.sb("a_BKM", (128, 8), F32)
    SMS = ph.sb("a_SMS", (128, 36), F32)
    if P == 0:
        IQB = ph.sb("a_IQB", (128, CH * 256), BF16)
        IWB = ph.sb("a_IWB", (128, CH * 4), BF16)
        KIB = ph.sb("a_KIB", (64, 4096), BF16)
        QI = ph.sb("a_QI", (64, 4 * 256), BF16)
        WIT = ph.sb("a_WIT", (128, 8), F32)
        VM = ph.sb("a_VM", (128, 4096), BF16)
        U0 = ph.sb("a_U0", (128, 512), BF16)
        U1 = ph.sb("a_U1", (128, 512), BF16)
        UB = ph.sb("a_UB", (128, 4096), BF16)
        CSB = ph.sb("a_CS", (128, 4096), F16)
        BIS = ph.sb("a_BIS", (128, 20), F32)
        bis = {"VM": VM, "U0": U0, "U1": U1, "UB": UB, "CSB": CSB, "BIS": BIS,
               "QI": QI, "WIT": WIT, "KIB": KIB, "SEL": SEL, "PB": PB}

    hbase = 8 * P

    for kc in range(CH):
        p.op("gpsimd", "dma_start", out=WQB[:, kc * 512:(kc + 1) * 512],
             in_=w["wq"][kc * 128:(kc + 1) * 128, 512 * P:512 * P + 512],
             inc=("dmag", 16))
    for h in range(8):
        p.op("gpsimd", "dma_start", out=WOB[0:64, h * 1024:(h + 1) * 1024],
             in_=w["wo"][512 * P + 64 * h:512 * P + 64 * (h + 1), :],
             inc=("dmag", 16))
    if P == 0:
        for kc in range(CH):
            p.op("gpsimd", "dma_start", out=IQB[:, kc * 256:(kc + 1) * 256],
                 in_=w["idx_wq"][kc * 128:(kc + 1) * 128, :], inc=("dmag", 16))
            p.op("gpsimd", "dma_start", out=IWB[:, kc * 4:(kc + 1) * 4],
                 in_=w["idx_ww"][kc * 128:(kc + 1) * 128, :], inc=("dmag", 16))
        p.op("sync", "dma_start", out=KIB[0:64, :], in_=scr["kI"],
             inc=("dma", 16))
    for c in range(4):
        p.op("sync", "dma_start", out=KH[:, c * 4096:(c + 1) * 4096],
             in_=scr["k0"][512 * P + 128 * c:512 * P + 128 * (c + 1), :],
             inc=("dma", 16))
    p.op("sync", "dma_start",
         out=VH[:, :].rearrange("p (u f) -> p u f", u=32),
         in_=scr["v0"][:, 512 * P:512 * P + 512].rearrange(
             "(u p) f -> p u f", p=128), inc=("dma", 16))
    wload = p.counts["dmag"]
    kvload = p.counts["dma"]
    p.wait("vector", "dve", p.counts["dve"])
    p.qop("vector", "tensor_copy", BKM[:, 0:4],
          t["KMX"][:, CH + 4 * P:CH + 4 * P + 4])
    p.qop("vector", "tensor_copy", BKM[:, 4:8],
          t["KMX"][:, CH + 4 * P:CH + 4 * P + 4])
    p.wait("tensor", "dmag", wload)
    p.wait("tensor", "dma", kvload)

    dbg = ctx.get("attn_dbg", 5)
    nwin = ctx.get("attn_nwin", 16)
    evac = {}
    for wi in range(nwin):
        for j in (0, 1):
            i = 2 * wi + j
            if wi > 0 or j > 0:
                p.wait("sync", "dve", xt_free)
            vx = p.op("sync", "dma_start", out=XT[:, :],
                      in_=ctx["x_d"][i * 128:(i + 1) * 128, :], inc=("dma", 16))
            p.wait("vector", "dma", vx)
            v1 = ln_apply(ctx, ph, p, XT[:, :], XNT[:, :], i, mu0, r0,
                          inc=("dve", 1))
            xt_free = v1
            p.dwait("vector")
            p.wait("vector", "pe", p.counts.get("pe", 0))
            v2 = p.op("vector", "tensor_scalar", out=XNB[:, :], in0=XNT[:, :],
                      scalar1=ST[:, rho + i:rho + i + 1], scalar2=None,
                      op0=ALU.mult, inc=("dve", 1))
            p.wait("tensor", "dve", v2)
            for c in range(CH):
                vp = p.op("tensor", "transpose", ps["T"][:, 0:128],
                          XNB[:, c * 128:(c + 1) * 128], t["IDT"][:, :],
                          inc=("pe", 1))
                p.wait("scalar", "pe", vp)
                va = p.op("scalar", "activation",
                          out=XNF[:, c * 256 + j * 128:c * 256 + (j + 1) * 128],
                          in_=ps["T"][:, 0:128], func=AF.Copy,
                          scale=t["NRM"][:, c:c + 1], inc=("act", 1))
                p.wait("tensor", "act", va)
        xnf_ready = p.counts["act"]
        p.wait("tensor", "act", xnf_ready)
        p.wait("tensor", "dve", p.counts["dve"])

        for qc in range(4):
            if ("qw", qc % 2) in evac:
                p.wait("tensor", "dve", evac[("qw", qc % 2)])
            bank = ps["A"] if qc % 2 == 0 else ps["B"]
            for kc in range(CH):
                p.mm(bank[:, 0:256],
                     WQB[:, kc * 512 + qc * 128:kc * 512 + (qc + 1) * 128],
                     XNF[:, kc * 256:(kc + 1) * 256],
                     start=(kc == 0), stop=(kc == CH - 1),
                     inc=("pe", 1) if kc == CH - 1 else None)
            p.wait("scalar", "pe", p.counts["pe"])
            va = p.op("scalar", "activation", out=QW[:, qc * 256:(qc + 1) * 256],
                      in_=bank[:, 0:256], func=AF.Copy, inc=("act", 1))
            p.wait("vector", "act", va)
            p.wait("vector", "pe", p.counts["pe"])
            vd = p.op("vector", "tensor_tensor", QSQ[:, :],
                      QW[:, qc * 256:(qc + 1) * 256],
                      QW[:, qc * 256:(qc + 1) * 256], ALU.mult, inc=("dve", 1))
            evac[("qw", qc % 2)] = vd
            p.wait("tensor", "dve", vd)
            for j in (0, 1):
                p.mm(ps["D"][:, 8 + j * 4 + qc:9 + j * 4 + qc],
                     QSQ[:, j * 128:(j + 1) * 128], t["ONES"][:, 0:1],
                     start=True, stop=True, inc=("pe", 1) if j == 1 else None)
        p.wait("scalar", "pe", p.counts["pe"])
        qn_evac = p.op("scalar", "activation", out=QN[:, 0:8],
                       in_=ps["D"][:, 8:16], func=AF.Copy, inc=("act", 1))
        p.wait("scalar", "act", qn_evac)
        va = p.op("scalar", "activation", out=QN[:, 0:8], in_=QN[:, 0:8],
                  func=AF.Sqrt, inc=("act", 1))
        p.wait("vector", "act", va)
        vbnd = p.op("vector", "tensor_tensor", BND[:, 0:8], QN[:, 0:8],
                    BKM[:, 0:8], ALU.mult, inc=("dve", 1))
        p.wait("scalar", "dve", vbnd)
        p.wait("tensor", "dve", vbnd)

        if dbg < 2:
            continue
        if P == 0:
            for h in range(4):
                if ("qi", h % 2) in evac:
                    p.wait("tensor", "act", evac[("qi", h % 2)])
                bank = ps["D"] if h % 2 == 0 else ps["F"]
                for kc in range(CH):
                    p.mm(bank[0:64, 0:256],
                         IQB[:, kc * 256 + h * 64:kc * 256 + (h + 1) * 64],
                         XNF[:, kc * 256:(kc + 1) * 256],
                         start=(kc == 0), stop=(kc == CH - 1),
                         inc=("pe", 1) if kc == CH - 1 else None)
                p.wait("scalar", "pe", p.counts["pe"])
                va = p.op("scalar", "activation",
                          out=QI[0:64, h * 256:(h + 1) * 256],
                          in_=bank[0:64, 0:256], func=AF.Copy, inc=("act", 1))
                evac[("qi", h % 2)] = va
            p.wait("tensor", "act", qn_evac)
            for j in (0, 1):
                for kc in range(CH):
                    p.mm(ps["D"][:, 0:4],
                         XNF[:, kc * 256 + j * 128:kc * 256 + (j + 1) * 128],
                         IWB[:, kc * 4:(kc + 1) * 4],
                         start=(kc == 0), stop=(kc == CH - 1),
                         inc=("pe", 1) if kc == CH - 1 else None)
                p.wait("scalar", "pe", p.counts["pe"])
                va = p.op("scalar", "activation", out=WIT[:, j * 4:(j + 1) * 4],
                          in_=ps["D"][:, 0:4], func=AF.Copy, inc=("act", 1))
                p.wait("tensor", "act", va)
            p.wait("vector", "act", p.counts["act"])

        if dbg < 3:
            continue
        for j in (0, 1):
            i = 2 * wi + j
            t0 = i * 128
            extent = t0 + 128
            if P == 0:
                _l0_idx_bisect(ctx, ph, p, bis, wi, j, evac)
            else:
                vs = p.op("sync", "dma_start",
                          out=SEL[:, j * 4096:j * 4096 + extent],
                          in_=scr["sel"][t0:t0 + 128, 0:extent],
                          inc=("dma", 16))
                p.wait("vector", "dma", vs)

        if dbg < 4:
            continue
        aoh_evac = None
        p.wait("tensor", "dve", p.counts["dve"])
        p.wait("tensor", "q_vector", p.counts.get("q_vector", 0))
        for hl in range(8):
            normdone = {}
            lc = hl // 2
            hp = (hl % 2) * 64
            for j in (0, 1):
                i = 2 * wi + j
                t0 = i * 128
                extent = t0 + 128
                nch = (extent + 511) // 512
                for sc in range(nch):
                    sw = min(512, extent - sc * 512)
                    bank = (ps["A"], ps["B"], ps["D"])[sc % 3]
                    meng = "vector"
                    msem = "dve"
                    if ("sb", sc % 3) in evac:
                        gs, gv = evac[("sb", sc % 3)]
                        p.wait("tensor", gs, gv)
                    p.mm(bank[:, 0:sw],
                         QW[hp:hp + 64,
                            lc * 256 + j * 128:lc * 256 + (j + 1) * 128],
                         KH[hp:hp + 64,
                            lc * 4096 + sc * 512:lc * 4096 + sc * 512 + sw],
                         start=True, stop=True, inc=("pe", 1))
                    p.wait("scalar", "pe", p.counts["pe"])
                    pbsl = PB[:, j * 4096 + sc * 512:j * 4096 + sc * 512 + sw]
                    va = p.op("scalar", "activation", out=pbsl,
                              in_=bank[:, 0:sw], func=AF.Exp, scale=0.125,
                              bias=BND[:, j * 4 + lc:j * 4 + lc + 1],
                              inc=("act", 1))
                    p.wait(meng, "act", va)
                    vd = p.op(meng, "scalar_tensor_tensor", out=pbsl,
                              in0=pbsl, scalar=1.0,
                              in1=SEL[:, j * 4096 + sc * 512:j * 4096 + sc * 512 + sw],
                              op0=ALU.mult, op1=ALU.mult,
                              accum_out=SMS[:, j * 18 + sc:j * 18 + sc + 1],
                              inc=(msem, 1))
                    evac[("sb", sc % 3)] = (msem, vd)
                p.dwait("vector")
                p.qop("vector", "reduce_sum", SMS[:, j * 18 + 16:j * 18 + 17],
                      SMS[:, j * 18:j * 18 + nch], axis=AX)
                p.dwait("vector")
                p.qop("vector", "reciprocal", SMS[:, j * 18 + 16:j * 18 + 17],
                      SMS[:, j * 18 + 16:j * 18 + 17])
                p.dwait("vector")
                for sc in range(nch):
                    sw = min(512, extent - sc * 512)
                    normdone[(j, sc)] = p.op(
                        "vector", "tensor_scalar",
                        out=PB[:, j * 4096 + sc * 512:j * 4096 + sc * 512 + sw],
                        in0=PB[:, j * 4096 + sc * 512:j * 4096 + sc * 512 + sw],
                        scalar1=SMS[:, j * 18 + 16:j * 18 + 17], scalar2=None,
                        op0=ALU.mult, inc=("dve", 1))
                if j == 0:
                    tail = p.op("vector", "memset", PB[:, extent:extent + 128],
                                0.0, inc=("dve", 1))
                    normdone[(0, "tail")] = tail
            if aoh_evac is not None:
                p.wait("tensor", "act", aoh_evac)
            extent1 = (2 * wi + 2) * 128
            nch1 = (extent1 + 511) // 512
            first = True
            copy_done = {}
            nsub_of = {}
            def emit_T(sc):
                par = sc % 2
                nsub = (min(512, extent1 - sc * 512) + 127) // 128
                nsub_of[sc] = nsub
                nd0 = normdone.get((0, sc), normdone.get((0, "tail"), 0))
                p.wait("tensor", "dve", max(nd0, normdone.get((1, sc), 0)))
                if sc >= 2:
                    cd = copy_done[sc - 2]
                    p.wait("tensor", cd[0], cd[1])
                    p.wait("tensor", cd[2], cd[3])
                for u in range(nsub):
                    if u == 2:
                        cs_, cv_ = copy_done[(sc, 0)]
                        p.wait("tensor", cs_, cv_)
                    for j in (0, 1):
                        p.op("tensor", "transpose",
                             ps["T"][:, par * 1024 + (u % 2) * 256 + j * 128:
                                     par * 1024 + (u % 2) * 256 + (j + 1) * 128],
                             PB[:, j * 4096 + sc * 512 + u * 128:
                                j * 4096 + sc * 512 + (u + 1) * 128],
                             t["IDT"][:, :],
                             inc=("pe", 1) if (u % 2 == 1 or u == nsub - 1) and j == 1 else None)
                    if u % 2 == 1 or u == nsub - 1:
                        grp = u // 2
                        v = p.counts["pe"]
                        ceng = "vector" if grp % 2 == 0 else "scalar"
                        csem = "dve" if grp % 2 == 0 else "act"
                        p.wait(ceng, "pe", v)
                        w = min(2, nsub - grp * 2) * 256
                        if ceng == "vector":
                            vd = p.op("vector", "tensor_copy",
                                      PTB[:, par * 1024 + grp * 512:par * 1024 + grp * 512 + w],
                                      ps["T"][:, par * 1024:par * 1024 + w],
                                      inc=("dve", 1))
                        else:
                            vd = p.op("scalar", "activation",
                                      out=PTB[:, par * 1024 + grp * 512:par * 1024 + grp * 512 + w],
                                      in_=ps["T"][:, par * 1024:par * 1024 + w],
                                      func=AF.Copy, inc=("act", 1))
                        copy_done[(sc, grp)] = (csem, vd)
                copy_done[sc] = ("dve", p.counts["dve"], "act", p.counts["act"])
            def emit_PV(sc):
                par = sc % 2
                nsub = nsub_of[sc]
                for u in range(nsub):
                    cs_, cv_ = copy_done[(sc, u // 2)]
                    p.wait("tensor", cs_, cv_)
                    last = (sc == nch1 - 1 and u == nsub - 1)
                    p.mm(ps["C"][0:64, 0:256],
                         VH[:, (sc * 4 + u) * 512 + hl * 64:
                            (sc * 4 + u) * 512 + (hl + 1) * 64],
                         PTB[:, par * 1024 + u * 256:par * 1024 + (u + 1) * 256],
                         start=emit_PV.first, stop=last,
                         inc=("pe", 1) if last else None)
                    emit_PV.first = False
            emit_PV.first = True
            for sc in range(nch1):
                emit_T(sc)
                if sc >= 1:
                    emit_PV(sc - 1)
            emit_PV(nch1 - 1)
            p.wait("scalar", "pe", p.counts["pe"])
            aoh_evac = p.op("scalar", "activation",
                            out=AOH[0:64, hl * 256:(hl + 1) * 256],
                            in_=ps["C"][0:64, 0:256], func=AF.Copy,
                            inc=("act", 1))
        p.wait("tensor", "act", p.counts["act"])

        if dbg < 5:
            continue
        for half in (0, 1):
            for oc2 in range(4):
                oc = half * 4 + oc2
                bank = ps["C"] if oc2 % 2 == 0 else ps["F"]
                if ("wo", oc2 % 2) in evac:
                    p.wait("tensor", "act", evac[("wo", oc2 % 2)])
                for hl in range(8):
                    p.mm(bank[:, 0:256],
                         WOB[0:64, hl * 1024 + oc * 128:hl * 1024 + (oc + 1) * 128],
                         AOH[0:64, hl * 256:(hl + 1) * 256],
                         start=(hl == 0), stop=(hl == 7),
                         inc=("pe", 1) if hl == 7 else None)
                p.wait("scalar", "pe", p.counts["pe"])
                va = p.op("scalar", "activation",
                          out=AWBs[:, oc2 * 256:(oc2 + 1) * 256],
                          in_=bank[:, 0:256], func=AF.Copy, inc=("act", 1))
                evac[("wo", oc2 % 2)] = va
            p.wait("tensor", "act", p.counts["act"])
            for j in (0, 1):
                i = 2 * wi + j
                for oc2 in range(4):
                    oc = half * 4 + oc2
                    vp = p.op("tensor", "transpose", ps["G"][:, 0:128],
                              AWBs[:, oc2 * 256 + j * 128:oc2 * 256 + (j + 1) * 128],
                              t["IDT2"][:, :], inc=("pe", 1))
                    p.wait("vector", "pe", vp)
                    vd = p.op("vector", "tensor_copy",
                              XATT[:, oc2 * 128:(oc2 + 1) * 128],
                              ps["G"][:, 0:128], inc=("dve", 1))
                    p.wait("tensor", "dve", vd)
                p.wait("sync", "dve", p.counts["dve"])
                p.wait("gpsimd", "dve", p.counts["dve"])
                if P == 0:
                    vs = p.op("sync", "dma_start",
                              out=scr["x_att"][i * 128:(i + 1) * 128,
                                               half * 512:(half + 1) * 512],
                              in_=XATT[:, 0:512], inc=("dma", 16))
                    p.wait("vector", "dma", vs)
                else:
                    vs = p.op("gpsimd", "dma_start",
                              out=scr["x_att"][i * 128:(i + 1) * 128,
                                               half * 512:(half + 1) * 512],
                              in_=XATT[:, 0:512], accum_op=ALU.add,
                              inc=("dmag", 16))
                    p.wait("vector", "dmag", vs)
    ph.close()


def _l0_idx_bisect(ctx, ph, p, b, wi, j, evac):
    t, ps = ctx["t"], ctx["ps"]
    scr = ctx["scr"]
    p.wait("tensor", "act", p.counts["act"])
    p.wait("vector", "dma", p.counts["dma"])
    QI, WIT, KIB = b["QI"], b["WIT"], b["KIB"]
    SEL = b["SEL"]
    U0, U1, CSB, BIS = b["U0"], b["U1"], b["CSB"], b["BIS"]
    if j == 0:
        VM, UB = b["VM"], b["UB"]
        bcol = 0
    else:
        VM = b["PB"][:, 0:4096]
        UB = b["PB"][:, 4096:8192]
        bcol = 8
    i = 2 * wi + j
    t0 = i * 128
    extent = t0 + 128
    nch = (extent + 511) // 512
    for sc in range(nch):
        sw = min(512, extent - sc * 512)
        vmsl = VM[:, sc * 512:sc * 512 + sw]
        for h in range(4):
            bank = ps["D"] if h % 2 == 0 else ps["F"]
            if ("il", h % 2) in evac:
                p.wait("tensor", "dve", evac[("il", h % 2)])
            p.mm(bank[:, 0:sw],
                 QI[0:64, h * 256 + j * 128:h * 256 + (j + 1) * 128],
                 KIB[0:64, sc * 512:sc * 512 + sw],
                 start=True, stop=True, inc=("pe", 1))
            p.wait("vector", "pe", p.counts["pe"])
            U = U0 if h % 2 == 0 else U1
            vd = p.op("vector", "tensor_scalar", out=U[:, 0:sw],
                      in0=bank[:, 0:sw], scalar1=0.0,
                      scalar2=WIT[:, j * 4 + h:j * 4 + h + 1],
                      op0=ALU.max, op1=ALU.mult, inc=("dve", 1))
            evac[("il", h % 2)] = vd
            if h == 1:
                p.wait("vector", "dve", vd)
                p.qop("vector", "tensor_tensor", vmsl, U0[:, 0:sw], U1[:, 0:sw],
                      ALU.add)
            elif h == 3:
                p.wait("vector", "dve", vd)
                p.qop("vector", "tensor_tensor", U0[:, 0:sw], U0[:, 0:sw],
                      U1[:, 0:sw], ALU.add)
                p.dwait("vector")
                vd = p.op("vector", "tensor_tensor", vmsl, vmsl, U0[:, 0:sw],
                          ALU.add, inc=("dve", 1))
                evac[("il", 0)] = vd
                evac[("il", 1)] = vd
    p.dwait("vector")
    p.qop("vector", "tensor_reduce", BIS[:, bcol:bcol + 1], VM[:, 0:extent],
          AX, ALU.min)
    p.dwait("vector")
    p.qop("vector", "tensor_scalar", out=BIS[:, bcol:bcol + 1],
          in0=BIS[:, bcol:bcol + 1], scalar1=0.0, scalar2=-1.0, op0=ALU.min,
          op1=ALU.add)
    dv = i % 4
    ds = ((extent - 1) // 512) * 512
    p.dwait("vector")
    p.qop("vector", "tensor_tensor", VM[:, ds:ds + 512], VM[:, ds:ds + 512],
          t["DIAG"][:, dv * 512:(dv + 1) * 512], ALU.add)
    p.dwait("vector")
    p.qop("vector", "tensor_reduce", BIS[:, bcol + 2:bcol + 3], VM[:, 0:extent],
          AX, ALU.max)
    p.dwait("vector")
    v = p.qop("vector", "tensor_scalar", out=BIS[:, bcol + 2:bcol + 3],
              in0=BIS[:, bcol + 2:bcol + 3], scalar1=0.0, scalar2=1.0,
              op0=ALU.max, op1=ALU.add)
    b.setdefault("chain", {})[j] = ("q_vector", p.counts["q_vector"])
    b.setdefault("vmref", {})[j] = (VM, UB, bcol, extent, nch)
    if j == 0:
        return

    chains = {}
    for jj in (0, 1):
        VMj, UBj, bc, ext, nchj = b["vmref"][jj]
        chains[jj] = {"VM": VMj, "UB": UBj, "bc": bc, "ext": ext,
                      "last": b["chain"][jj][1] if jj in b["chain"] else 0}
    def step(c, it, k):
        bc = c["bc"]
        lo = BIS[:, bc + it % 2:bc + it % 2 + 1]
        lon = BIS[:, bc + (it + 1) % 2:bc + (it + 1) % 2 + 1]
        hi = BIS[:, bc + 2 + it % 2:bc + 3 + it % 2]
        hin = BIS[:, bc + 2 + (it + 1) % 2:bc + 3 + (it + 1) % 2]
        m = BIS[:, bc + 4:bc + 5]
        cnt = BIS[:, bc + 5:bc + 6]
        cond = BIS[:, bc + 6:bc + 7]
        d1 = BIS[:, bc + 7:bc + 8]
        d2 = BIS[:, 17 + c["bc"] // 8:18 + c["bc"] // 8]
        p.wait("vector", "q_vector", c["last"])
        if k == 0:
            c["last"] = p.qop("vector", "tensor_tensor", m, lo, hi, ALU.add)
        elif k == 1:
            c["last"] = p.qop("vector", "tensor_scalar", out=m, in0=m,
                              scalar1=0.5, scalar2=None, op0=ALU.mult)
        elif k == 2:
            c["last"] = p.qop("vector", "tensor_scalar",
                              out=c["UB"][:, 0:c["ext"]],
                              in0=c["VM"][:, 0:c["ext"]], scalar1=m,
                              scalar2=None, op0=ALU.is_gt, op1=ALU.add,
                              accum_out=cnt)
        elif k == 3:
            p.qop("vector", "tensor_scalar", out=cond, in0=cnt, scalar1=511.5,
                  scalar2=None, op0=ALU.is_le)
            p.qop("vector", "tensor_tensor", d1, m, hi, ALU.subtract)
            c["last"] = p.qop("vector", "tensor_tensor", d2, lo, m,
                              ALU.subtract)
        elif k == 4:
            c["last"] = p.qop("vector", "scalar_tensor_tensor", out=hin,
                              in0=d1, scalar=cond, in1=hi, op0=ALU.mult,
                              op1=ALU.add)
        else:
            c["last"] = p.qop("vector", "scalar_tensor_tensor", out=lon,
                              in0=d2, scalar=cond, in1=m, op0=ALU.mult,
                              op1=ALU.add)
    for it in range(NBIS):
        for k in range(6):
            step(chains[0], it, k)
            step(chains[1], it, k)
        p.drip(ctx)
    for jj in (0, 1):
        c = chains[jj]
        VMj, UBj, bc, ext = c["VM"], c["UB"], c["bc"], c["ext"]
        i_j = 2 * wi + jj
        t0j = i_j * 128
        nchj = (ext + 511) // 512
        lo = BIS[:, bc + NBIS % 2:bc + NBIS % 2 + 1]
        hi = BIS[:, bc + 2 + NBIS % 2:bc + 3 + NBIS % 2]
        chi = BIS[:, bc + 7:bc + 8]
        need = BIS[:, 16:17]
        selsl = SEL[:, jj * 4096:jj * 4096 + ext]
        p.wait("vector", "q_vector", c["last"])
        p.qop("vector", "tensor_scalar", out=UBj[:, 0:ext],
              in0=VMj[:, 0:ext], scalar1=hi, scalar2=None, op0=ALU.is_gt,
              op1=ALU.add, accum_out=chi)
        p.dwait("vector")
        p.qop("vector", "tensor_scalar", out=need, in0=chi, scalar1=-1.0,
              scalar2=512.0, op0=ALU.mult, op1=ALU.add)
        p.qop("vector", "tensor_scalar", out=selsl, in0=VMj[:, 0:ext],
              scalar1=lo, scalar2=None, op0=ALU.is_gt)
        p.dwait("vector")
        p.qop("vector", "tensor_tensor", selsl, selsl, UBj[:, 0:ext],
              ALU.subtract)
        for sc in range(nchj):
            sw = min(512, ext - sc * 512)
            init = 0.0 if sc == 0 else CSB[:, sc * 512 - 1:sc * 512]
            p.dwait("vector")
            p.qop("vector", "tensor_tensor_scan",
                  out=CSB[:, sc * 512:sc * 512 + sw],
                  data0=SEL[:, jj * 4096 + sc * 512:jj * 4096 + sc * 512 + sw],
                  data1=t["ZER"][:, 0:sw], initial=init, op0=ALU.add,
                  op1=ALU.add)
        p.dwait("vector")
        p.qop("vector", "scalar_tensor_tensor", out=selsl,
              in0=CSB[:, 0:ext], scalar=need, in1=selsl, op0=ALU.is_le,
              op1=ALU.mult)
        p.dwait("vector")
        vfin = p.op("vector", "tensor_tensor", selsl, selsl, UBj[:, 0:ext],
                    ALU.add, inc=("dve", 1))
        p.wait("sync", "dve", vfin)
        vs = p.op("sync", "dma_start",
                  out=scr["sel"][t0j:t0j + 128, 0:ext], in_=selsl,
                  inc=("dma", 16))
    p.wait("scalar", "dve", p.counts["dve"])
    p.wait("scalar", "q_vector", p.counts["q_vector"])


def _l0_xatt_fix(ctx):
    p, t = ctx["p"], ctx["t"]
    scr = ctx["scr"]
    ph = Phase(ctx)
    ST = t["STATS"]
    mu0, r0 = ctx["st0_mu"], ctx["st0_r"]
    base = ctx["stats_base"]
    ctx["st2_r"] = base
    ctx["stats_base"] += NT
    XT = ph.sb("f_XT", (128, 2048), F32)
    XA = ph.sb("f_XA", (128, 2048), F32)
    DU = ph.sb("f_DU", (128, 1024), F32)
    guard = {}
    for i in range(NT):
        sl = (i % 2) * 1024
        if i >= 2:
            p.wait("sync", "act", guard[i - 2])
        v1 = p.op("sync", "dma_start", out=XT[:, sl:sl + 1024],
                  in_=ctx["x_d"][i * 128:(i + 1) * 128, :], inc=("dma", 16))
        v2 = p.op("sync", "dma_start", out=XA[:, sl:sl + 1024],
                  in_=scr["x_att"][i * 128:(i + 1) * 128, :], inc=("dma", 16))
        p.wait("vector", "dma", v2)
        ln_apply(ctx, ph, p, XT[:, sl:sl + 1024], XT[:, sl:sl + 1024], i,
                 mu0, r0, inc=("dve", 1))
        p.dwait("vector")
        vd = p.op("vector", "tensor_tensor", XA[:, sl:sl + 1024],
                  XA[:, sl:sl + 1024], XT[:, sl:sl + 1024], ALU.add,
                  inc=("dve", 1))
        p.wait("scalar", "dve", vd)
        va = p.op("scalar", "activation", out=DU[:, :], in_=XA[:, sl:sl + 1024],
                  func=AF.Square, accum_out=ST[:, base + i:base + i + 1],
                  inc=("act", 1))
        guard[i] = va
        p.drip(ctx, "act")
        p.wait("sync", "act", va)
        p.op("sync", "dma_start", out=scr["x_att"][i * 128:(i + 1) * 128, :],
             in_=XA[:, sl:sl + 1024], inc=("dma", 16))
    sq = ST[:, base:base + NT]
    p.wait("vector", "act", p.counts["act"])
    vv = p.op("vector", "tensor_scalar", out=sq, in0=sq, scalar1=1.0 / DM,
              scalar2=1e-6, op0=ALU.mult, op1=ALU.add, inc=("dve", 1))
    p.wait("scalar", "dve", vv)
    va = p.op("scalar", "activation", out=sq, in_=sq, func=AF.Sqrt,
              inc=("act", 1))
    p.wait("vector", "act", va)
    p.op("vector", "reciprocal", out=sq, in_=sq, inc=("dve", 1))
    ph.close()


def emit_rank_top2(ctx, ph, p, E, logits, sel, CG, CE, junk):
    for e in range(E):
        p.qop("vector", "tensor_scalar", out=junk[:, 0:E], in0=logits,
              scalar1=logits[:, e:e + 1], scalar2=None, op0=ALU.is_gt,
              op1=ALU.add, accum_out=CG[:, e:e + 1])
    p.qop("vector", "memset", CE[:, 0:1], 0.0)
    for e in range(1, E):
        p.qop("vector", "tensor_scalar", out=junk[:, 0:e], in0=logits[:, 0:e],
              scalar1=logits[:, e:e + 1], scalar2=None, op0=ALU.is_equal,
              op1=ALU.add, accum_out=CE[:, e:e + 1])
    p.dwait("vector")
    p.qop("vector", "tensor_tensor", CG[:, 0:E], CG[:, 0:E], CE[:, 0:E],
          ALU.add)
    p.dwait("vector")
    p.qop("vector", "tensor_scalar", out=sel, in0=CG[:, 0:E], scalar1=1.5,
          scalar2=None, op0=ALU.is_le)
    p.dwait("vector")


def _gate_prepass(ctx, src_d, r_base, mu_base, nrm_off, gw_d, E, gates_off,
                  xnf_d, gate_bias=None, mode="renorm"):
    p, t, ps = ctx["p"], ctx["t"], ctx["ps"]
    ph = Phase(ctx)
    ST = t["STATS"]
    XT = ph.sb("g_XT", (128, 1024), F32)
    XNB = ph.sb("g_XNB", (128, 1024), BF16)
    XNF = ph.sb("g_XNF", (128, CH * 512), BF16)
    GWB = ph.sb("g_GWB", (128, CH * E), BF16)
    GL = ph.sb("g_GL", (128, 16), F32)
    EE = ph.sb("g_EE", (128, 16), F32)
    CG = ph.sb("g_CG", (128, 8), F32)
    CE = ph.sb("g_CE", (128, 8), F32)
    JK = ph.sb("g_JK", (128, 8), F32)
    SELg = ph.sb("g_SEL", (128, 8), F32)
    BBg = ph.sb("g_BB", (128, 8), F32)
    SM1 = ph.sb("g_SM1", (128, 4), F32)
    for kc in range(CH):
        vw = p.op("gpsimd", "dma_start", out=GWB[:, kc * E:(kc + 1) * E],
                  in_=gw_d[kc * 128:(kc + 1) * 128, :], inc=("dmag", 16))
    if gate_bias is not None:
        vb = p.op("sync", "dma_start", out=BBg[0:1, 0:E], in_=gate_bias,
                  inc=("dma", 16))
        p.wait("vector", "dma", vb)
        p.qop("vector", "tensor_copy", XNB[0:1, 0:E], BBg[0:1, 0:E])
        p.dwait("vector")
        vd = p.inc("dve", 1)
        p.emit("vector", lambda e, s=p.sems["dve"]: e.nop().then_inc(s, 1))
        p.wait("tensor", "dve", vd)
        vp = p.mm(ps["D"][:, 0:E], t["ONES"][0:1, 0:128], XNB[0:1, 0:E],
                  start=True, stop=True, inc=("pe", 1))
        p.wait("vector", "pe", vp)
        p.qop("vector", "tensor_copy", BBg[:, 0:E], ps["D"][:, 0:E])
        p.dwait("vector")
    p.wait("tensor", "dmag", p.counts["dmag"])

    for wi in range(8):
        for j in range(4):
            i = wi * 4 + j
            if wi > 0 or j > 0:
                p.wait("sync", "dve", xfree)
            vx = p.op("sync", "dma_start", out=XT[:, :],
                      in_=src_d[i * 128:(i + 1) * 128, :], inc=("dma", 16))
            p.wait("vector", "dma", vx)
            ln_apply(ctx, ph, p, XT[:, :], XT[:, :], i, mu_base, r_base,
                     inc=("dve", 1))
            p.dwait("vector")
            p.wait("vector", "pe", p.counts.get("pe", 0))
            v2 = p.op("vector", "tensor_copy", XNB[:, :], XT[:, :],
                      inc=("dve", 1))
            xfree = v2
            p.wait("tensor", "dve", v2)
            for c in range(CH):
                vp = p.op("tensor", "transpose", ps["T"][:, 0:128],
                          XNB[:, c * 128:(c + 1) * 128], t["IDT"][:, :],
                          inc=("pe", 1))
                p.wait("scalar", "pe", vp)
                va = p.op("scalar", "activation",
                          out=XNF[:, c * 512 + j * 128:c * 512 + (j + 1) * 128],
                          in_=ps["T"][:, 0:128], func=AF.Copy,
                          scale=t["NRM"][:, nrm_off + c:nrm_off + c + 1]
                          if nrm_off is not None else 1.0, inc=("act", 1))
                p.wait("tensor", "act", va)
        p.drip(ctx, "act")
        va_all = p.counts["act"]
        p.wait("sync", "act", va_all)
        for c in range(CH):
            vs = p.op("sync", "dma_start",
                      out=xnf_d[c * 128:(c + 1) * 128, wi * 512:(wi + 1) * 512],
                      in_=XNF[:, c * 512:(c + 1) * 512], inc=("dma", 16))
        for j in range(4):
            i = wi * 4 + j
            p.wait("tensor", "dve", p.counts["dve"])
            for kc in range(CH):
                p.mm(ps["D"][:, 0:E],
                     XNF[:, kc * 512 + j * 128:kc * 512 + (j + 1) * 128],
                     GWB[:, kc * E:(kc + 1) * E],
                     start=(kc == 0), stop=(kc == CH - 1),
                     inc=("pe", 1) if kc == CH - 1 else None)
            p.wait("vector", "pe", p.counts["pe"])
            p.qop("vector", "tensor_copy", GL[:, 0:E], ps["D"][:, 0:E])
            p.dwait("vector")
            if gate_bias is not None:
                p.qop("vector", "tensor_tensor", GL[:, 0:E], GL[:, 0:E],
                      BBg[:, 0:E], ALU.add)
                p.dwait("vector")
            p.qop("vector", "reduce_max", SM1[:, 0:1], GL[:, 0:E], axis=AX,
                  negate=True)
            p.dwait("vector")
            vneg = p.inc("dve", 1)
            p.emit("vector", lambda e, s=p.sems["dve"]: e.nop().then_inc(s, 1))
            p.wait("scalar", "dve", vneg)
            va = p.op("scalar", "activation", out=EE[:, 0:E], in_=GL[:, 0:E],
                      func=AF.Exp, bias=SM1[:, 0:1], inc=("act", 1))
            p.wait("vector", "act", va)
            emit_rank_top2(ctx, ph, p, E, GL[:, 0:E], SELg[:, 0:E], CG, CE, JK)
            if mode == "sel":
                p.qop("vector", "tensor_copy",
                      t["GATES"][:, gates_off + i * E:gates_off + (i + 1) * E],
                      SELg[:, 0:E])
                p.dwait("vector")
                continue
            if mode == "renorm":
                p.qop("vector", "scalar_tensor_tensor", out=JK[:, 0:E],
                      in0=EE[:, 0:E], scalar=1.0, in1=SELg[:, 0:E],
                      op0=ALU.mult, op1=ALU.mult, accum_out=SM1[:, 1:2])
            else:
                p.qop("vector", "tensor_scalar", out=JK[:, 0:E],
                      in0=EE[:, 0:E], scalar1=1.0, scalar2=None, op0=ALU.mult,
                      op1=ALU.add, accum_out=SM1[:, 1:2])
                p.dwait("vector")
                p.qop("vector", "tensor_tensor", JK[:, 0:E], EE[:, 0:E],
                      SELg[:, 0:E], ALU.mult)
            p.dwait("vector")
            p.qop("vector", "reciprocal", SM1[:, 2:3], SM1[:, 1:2])
            p.dwait("vector")
            p.qop("vector", "tensor_scalar",
                  out=t["GATES"][:, gates_off + i * E:gates_off + (i + 1) * E],
                  in0=JK[:, 0:E], scalar1=SM1[:, 2:3], scalar2=None,
                  op0=ALU.mult)
            p.dwait("vector")
            p.drip(ctx)
        p.wait("scalar", "dma", p.counts["dma"])
        p.wait("tensor", "dma", p.counts["dma"])
    ph.close()


def _moe_experts(ctx, E, DF, w1_d, w2_d, b1_l_d, b2row_d, xnf_d, out_d,
                 gates_off, act_func, tbw=512, ffdb=True):
    p, t, ps = ctx["p"], ctx["t"], ctx["ps"]
    ph = Phase(ctx)
    NHC = DF // 128
    NJ = tbw // 128
    NTB = S // tbw
    W1B = ph.sb("m_W1B", (128, CH * DF), BF16)
    W2B = ph.sb("m_W2B", (128, NHC * DM), BF16)
    HB = ph.sb("m_HB", (128, NHC * tbw), BF16)
    XL2 = ph.sb("m_XL", (128, 2 * CH * tbw), BF16)
    EOB = ph.sb("m_EOB", (128, tbw), BF16)
    FF2 = ph.sb("m_FF", (128, (2 if ffdb else 1) * NJ * 1024), BF16)
    B1 = ph.sb("m_B1", (128, NHC), F32)
    B2R = ph.sb("m_B2R", (1, DM), BF16)
    evac = {}
    xl_consumed = {}
    ff_stored = {}
    for e in range(E):
        vb1 = p.op("gpsimd", "dma_start", out=B1[:, :], in_=b1_l_d[e],
                   inc=("dmag", 16))
        vb2 = p.op("gpsimd", "dma_start", out=B2R[0:1, :], in_=b2row_d[e],
                   inc=("dmag", 16))
        for kc in range(CH):
            p.op("gpsimd", "dma_start", out=W1B[:, kc * DF:(kc + 1) * DF],
                 in_=w1_d[e, kc * 128:(kc + 1) * 128, :], inc=("dmag", 16))
        for hc in range(NHC):
            vw = p.op("gpsimd", "dma_start", out=W2B[:, hc * DM:(hc + 1) * DM],
                      in_=w2_d[e, hc * 128:(hc + 1) * 128, :], inc=("dmag", 16))
        p.wait("tensor", "dmag", vw)
        p.wait("scalar", "dmag", vw)
        xl_loaded = {}
        for tbl in range(NTB + 1):
            if tbl < NTB:
                par = tbl % 2
                if (e, tbl - 2) in xl_consumed:
                    p.wait("sync", "pe", xl_consumed[(e, tbl - 2)])
                elif tbl < 2 and (e - 1, tbl + NTB - 2) in xl_consumed:
                    p.wait("sync", "pe", xl_consumed[(e - 1, tbl + NTB - 2)])
                XL = XL2[:, par * CH * tbw:(par + 1) * CH * tbw]
                vx = p.op("sync", "dma_start",
                          out=XL[:, :].rearrange("p (c n) -> p c n", c=CH),
                          in_=xnf_d[:, tbl * tbw:(tbl + 1) * tbw].rearrange(
                              "(c p) n -> p c n", p=128), inc=("dma", 16))
                xl_loaded[tbl] = vx
            if tbl == 0:
                continue
            tb = tbl - 1
            par = tb % 2
            XL = XL2[:, par * CH * tbw:(par + 1) * CH * tbw]
            FF = FF2[:, par * NJ * 1024:(par + 1) * NJ * 1024] if ffdb else FF2
            p.wait("tensor", "dma", xl_loaded[tb])
            back = 2 if ffdb else 1
            prev = ff_stored.get((e, tb - back)) or ff_stored.get((e - 1, tb + NTB - back))
            if prev is not None:
                p.wait("scalar", prev[0], prev[1])
            for hc in range(NHC):
                bank = (ps["A"], ps["B"], ps["D"])[hc % 3]
                if ("h", hc % 3) in evac:
                    p.wait("tensor", "act", evac[("h", hc % 3)])
                for kc in range(CH):
                    p.mm(bank[:, 0:tbw],
                         W1B[:, kc * DF + hc * 128:kc * DF + (hc + 1) * 128],
                         XL[:, kc * tbw:(kc + 1) * tbw],
                         start=(kc == 0), stop=(kc == CH - 1),
                         inc=("pe", 1) if kc == CH - 1 else None)
                p.wait("scalar", "pe", p.counts["pe"])
                va = p.op("scalar", "activation",
                          out=HB[:, hc * tbw:(hc + 1) * tbw], in_=bank[:, 0:tbw],
                          func=act_func, bias=B1[:, hc:hc + 1], inc=("act", 1))
                evac[("h", hc % 3)] = va
            xl_consumed[(e, tb)] = p.counts["pe"]
            p.wait("tensor", "act", p.counts["act"])
            for oc in range(CH):
                bank = (ps["C"], ps["F"], ps["G"])[oc % 3]
                if ("eo", oc % 3) in evac:
                    p.wait("tensor", "dve", evac[("eo", oc % 3)])
                for hc in range(NHC):
                    p.mm(bank[:, 0:tbw],
                         W2B[:, hc * DM + oc * 128:hc * DM + (oc + 1) * 128],
                         HB[:, hc * tbw:(hc + 1) * tbw],
                         start=(hc == 0), stop=False)
                p.mm(bank[:, 0:tbw], B2R[0:1, oc * 128:(oc + 1) * 128],
                     t["ONES"][0:1, 0:tbw], start=False, stop=True,
                     inc=("pe", 1))
                p.wait("vector", "pe", p.counts["pe"])
                va = p.op("vector", "tensor_copy", EOB[:, 0:tbw],
                          bank[:, 0:tbw], inc=("dve", 1))
                evac[("eo", oc % 3)] = va
                p.wait("tensor", "dve", va)
                for jj in range(NJ):
                    i = tb * NJ + jj
                    if ("fft", jj % 2) in evac:
                        p.wait("tensor", "act", evac[("fft", jj % 2)])
                    vp = p.op("tensor", "transpose",
                              ps["T"][:, (jj % 2) * 1024:(jj % 2) * 1024 + 128],
                              EOB[:, jj * 128:(jj + 1) * 128], t["IDT"][:, :],
                              inc=("pe", 1))
                    p.wait("scalar", "pe", vp)
                    vd = p.op("scalar", "activation",
                              out=FF[:, jj * 1024 + oc * 128:jj * 1024 + (oc + 1) * 128],
                              in_=ps["T"][:, (jj % 2) * 1024:(jj % 2) * 1024 + 128],
                              func=AF.Copy,
                              scale=t["GATES"][:, gates_off + i * E + e:
                                               gates_off + i * E + e + 1],
                              inc=("act", 1))
                    evac[("fft", jj % 2)] = vd
            p.wait("gpsimd", "dve", p.counts["dve"])
            p.wait("gpsimd", "act", p.counts["act"])
            for jj in range(NJ):
                i = tb * NJ + jj
                vs = p.op("gpsimd", "dma_start",
                          out=out_d[i * 128:(i + 1) * 128, :],
                          in_=FF[:, jj * 1024:(jj + 1) * 1024],
                          **({} if e == 0 else {"accum_op": ALU.add}),
                          inc=("dmag", 16))
                ff_stored[(e, tb)] = ("dmag", p.counts["dmag"])
        p.wait("gpsimd", "pe", p.counts["pe"])
        p.wait("gpsimd", "act", p.counts["act"])
        p.wait("gpsimd", "dve", p.counts["dve"])
        if e == 0:
            p.wait("gpsimd", "dma", p.counts["dma"])
    ph.close()


def plan_l0_moe(ctx):
    w, scr = ctx["w"], ctx["scr"]
    _gate_prepass(ctx, scr["x_att"], ctx["st2_r"], None, CH, w["gate_w"],
                  E0A, 0, scr["xn2f"], gate_bias=None, mode="renorm")
    _moe_experts(ctx, E0A, DF0A, w["moe_w1"], w["moe_w2"], w["moe_b1_l"],
                 w["moe_b2_row"], scr["xn2f"], scr["ff0"], 0,
                 AF.Gelu_apprx_tanh, tbw=512, ffdb=False)


def plan_l0_ffn(ctx):
    w, scr = ctx["w"], ctx["scr"]
    _gate_prepass(ctx, ctx["x_d"], ctx["st0_r"], ctx["st0_mu"], None,
                  w["fgate_w"], E0F, 128, scr["xnf"], gate_bias=w["fgate_b"],
                  mode="probs")
    _moe_experts(ctx, E0F, DF0F, w["ffn_w1"], w["ffn_w2"], w["ffn_b1_l"],
                 w["ffn_b2_row"], scr["xnf"], scr["gffn"], 128, AF.Relu)


def plan_l0_combine(ctx):
    p, t = ctx["p"], ctx["t"]
    scr = ctx["scr"]
    ph = Phase(ctx)
    ST = t["STATS"]
    base = ctx["stats_base"]
    ctx["st1_mu"] = base
    ctx["st1_r"] = base + NT
    ctx["stats_base"] = base + 2 * NT
    XA = ph.sb("c_XA", (128, 1024), F32)
    XF = ph.sb("c_XF", (128, 1024), F32)
    XG = ph.sb("c_XG", (128, 1024), F32)
    XX = ph.sb("c_XX", (128, 1024), F32)
    DU = ph.sb("c_DU", (128, 1024), F32)
    for i in range(NT):
        sl = slice(i * 128, (i + 1) * 128)
        if i > 0:
            p.wait("sync", "dma", store_v)
            p.wait("sync", "act", p.counts["act"])
        v1 = p.op("sync", "dma_start", out=XA[:, :], in_=scr["x_att"][sl, :],
                  inc=("dma", 16))
        v2 = p.op("sync", "dma_start", out=XF[:, :], in_=scr["ff0"][sl, :],
                  inc=("dma", 16))
        v3 = p.op("sync", "dma_start", out=XG[:, :], in_=scr["gffn"][sl, :],
                  inc=("dma", 16))
        v4 = p.op("sync", "dma_start", out=XX[:, :], in_=ctx["x_d"][sl, :],
                  inc=("dma", 16))
        p.wait("scalar", "dma", v3)
        va = p.op("scalar", "activation", out=XG[:, :], in_=XG[:, :],
                  func=AF.Silu, inc=("act", 1))
        p.wait("vector", "dma", v4)
        p.wait("vector", "act", va)
        p.qop("vector", "tensor_tensor", XA[:, :], XA[:, :], XF[:, :], ALU.add)
        p.dwait("vector")
        p.qop("vector", "tensor_tensor", XA[:, :], XA[:, :], XG[:, :], ALU.mult)
        p.dwait("vector")
        vd = p.op("vector", "tensor_tensor", XA[:, :], XA[:, :], XX[:, :],
                  ALU.add, inc=("dve", 1))
        p.wait("scalar", "dve", vd)
        p.op("scalar", "activation", out=DU[:, :], in_=XA[:, :], func=AF.Copy,
             accum_out=ST[:, base + i:base + i + 1])
        va = p.op("scalar", "activation", out=DU[:, :], in_=XA[:, :],
                  func=AF.Square, accum_out=ST[:, base + NT + i:base + NT + i + 1],
                  inc=("act", 1))
        p.wait("sync", "dve", vd)
        store_v = p.op("sync", "dma_start", out=scr["x1"][sl, :], in_=XA[:, :],
                       inc=("dma", 16))
        p.drip(ctx)
    mu = ST[:, base:base + NT]
    sq = ST[:, base + NT:base + 2 * NT]
    p.wait("vector", "act", p.counts["act"])
    p.qop("vector", "tensor_scalar", out=sq, in0=sq, scalar1=1.0 / DM,
          scalar2=None, op0=ALU.mult)
    p.qop("vector", "tensor_scalar", out=mu, in0=mu, scalar1=1.0 / DM,
          scalar2=None, op0=ALU.mult)
    p.dwait("vector")
    mu2 = ph.sb("c_mu2", (128, NT), F32)
    p.qop("vector", "tensor_tensor", mu2[:, :], mu, mu, ALU.mult)
    p.dwait("vector")
    p.qop("vector", "tensor_tensor", sq, sq, mu2[:, :], ALU.subtract)
    p.dwait("vector")
    vv = p.op("vector", "tensor_scalar", out=sq, in0=sq, scalar1=1e-5,
              scalar2=None, op0=ALU.add, inc=("dve", 1))
    p.wait("scalar", "dve", vv)
    va = p.op("scalar", "activation", out=sq, in_=sq, func=AF.Sqrt,
              inc=("act", 1))
    p.wait("vector", "act", va)
    p.op("vector", "reciprocal", out=sq, in_=sq, inc=("dve", 1))
    ph.close()

def plan_l1_router(ctx):
    w, scr = ctx["w"], ctx["scr"]
    _gate_prepass(ctx, scr["x1"], ctx["st1_r"], ctx["st1_mu"], None,
                  w["router_wT"], E1F, 0, scr["xn1f"],
                  gate_bias=w["router_b"], mode="sel")


def plan_l1_wdkv(ctx):
    p, t, w, ps = ctx["p"], ctx["t"], ctx["w"], ctx["ps"]
    scr = ctx["scr"]
    ph = Phase(ctx)
    LWB = ph.sb("wd_LW", (128, CH * L1L), BF16)
    LB = ph.sb("wd_LB", (128, 3), F32)
    XNF = ph.sb("wd_XNF", (128, CH * 512), BF16)
    WFM = ph.sb("wd_WFM", (128, 3 * 512), BF16)
    WTM = ph.sb("wd_WTM", (128, 4 * 384), BF16)
    NRMS = ph.sb("wd_NR", (128, 40), F32)
    DU = ph.sb("wd_DU", (128, 512), F32)
    for kc in range(CH):
        vw = p.op("gpsimd", "dma_start", out=LWB[:, kc * L1L:(kc + 1) * L1L],
                  in_=w["lat_w"][kc * 128:(kc + 1) * 128, :], inc=("dmag", 16))
    vb = p.op("sync", "dma_start", out=LB[:, :], in_=w["lat_b_l"], inc=("dma", 16))
    p.wait("tensor", "dmag", vw)
    p.wait("tensor", "dma", vb)
    evac = {}
    for wi in range(8):
        p.wait("sync", "pe", p.counts["pe"])
        vx = p.op("sync", "dma_start",
                  out=XNF[:, :].rearrange("p (c n) -> p c n", c=CH),
                  in_=scr["xn1f"][:, wi * 512:(wi + 1) * 512].rearrange(
                      "(c p) n -> p c n", p=128), inc=("dma", 16))
        p.wait("tensor", "dma", vx)
        p.wait("vector", "pe", p.counts["pe"])
        p.wait("vector", "dma", p.counts["dma"])
        p.op("vector", "memset", WFM[:, 2 * 512:3 * 512], 0.0, inc=("dve", 1))
        for mc in range(3):
            mw = 128 if mc < 2 else 32
            bank = ps["A"] if mc % 2 == 0 else ps["B"]
            if ("wd", mc % 2) in evac:
                p.wait("tensor", "dve", evac[("wd", mc % 2)])
            for kc in range(CH):
                p.mm(bank[0:mw, :],
                     LWB[:, kc * L1L + mc * 128:kc * L1L + mc * 128 + mw],
                     XNF[:, kc * 512:(kc + 1) * 512],
                     start=(kc == 0), stop=(kc == CH - 1),
                     inc=("pe", 1) if kc == CH - 1 else None)
            p.wait("vector", "pe", p.counts["pe"])
            p.wait("vector", "dma", p.counts["dma"])
            vd = p.op("vector", "tensor_scalar", out=WFM[0:mw, mc * 512:(mc + 1) * 512],
                      in0=bank[0:mw, :], scalar1=LB[0:mw, mc:mc + 1],
                      scalar2=None, op0=ALU.add, inc=("dve", 1))
            evac[("wd", mc % 2)] = vd
        p.dwait("vector")
        p.wait("sync", "dve", p.counts["dve"])
        for mc in range(3):
            vs = p.op("sync", "dma_start",
                      out=scr["wdkvF"][mc * 128:mc * 128 + (128 if mc < 2 else 32),
                                       wi * 512:(wi + 1) * 512],
                      in_=WFM[0:(128 if mc < 2 else 32), mc * 512:(mc + 1) * 512],
                      inc=("dma", 16))
        p.wait("tensor", "dve", p.counts["dve"])
        for u in range(4):
            for mc in range(3):
                p.op("tensor", "transpose",
                     ps["T"][:, mc * 128:(mc + 1) * 128],
                     WFM[:, mc * 512 + u * 128:mc * 512 + (u + 1) * 128],
                     t["IDT"][:, :],
                     inc=("pe", 1) if mc == 2 else None)
            p.wait("vector", "pe", p.counts["pe"])
            vd = p.op("vector", "tensor_copy", WTM[:, u * 384:(u + 1) * 384],
                      ps["T"][:, 0:384], inc=("dve", 1))
            p.wait("tensor", "dve", vd)
        p.wait("scalar", "dve", p.counts["dve"])
        for u in range(4):
            va = p.op("scalar", "activation", out=DU[:, 0:384],
                      in_=WTM[:, u * 384:(u + 1) * 384], func=AF.Square,
                      accum_out=NRMS[:, wi * 4 + u:wi * 4 + u + 1],
                      inc=("act", 1))
        p.wait("sync", "act", p.counts["act"])
        p.wait("sync", "dve", p.counts["dve"])
        vs = p.op("sync", "dma_start",
                  out=scr["wdkvT"][wi * 512:(wi + 1) * 512, :].rearrange(
                      "(u p) l -> p u l", p=128),
                  in_=WTM[:, :].rearrange("p (u l) -> p u l", u=4),
                  inc=("dma", 16))
        p.wait("vector", "dma", vs)
    p.wait("vector", "act", p.counts["act"])
    p.qop("vector", "reduce_max", NRMS[:, 32:33], NRMS[:, 0:32], axis=AX)
    p.dwait("vector")
    p.qop("vector", "tensor_copy", WTM[0:128, 0:1], NRMS[:, 32:33])
    p.dwait("vector")
    vd = p.inc("dve", 1)
    p.emit("vector", lambda e, s=p.sems["dve"]: e.nop().then_inc(s, 1))
    p.wait("tensor", "dve", vd)
    vp = p.op("tensor", "transpose", ps["T"][:, 0:128], WTM[:, 0:128],
              t["IDT"][:, :], inc=("pe", 1))
    p.wait("vector", "pe", vp)
    p.qop("vector", "reduce_max", NRMS[0:1, 33:34], ps["T"][0:1, 0:128], axis=AX)
    p.dwait("vector")
    p.qop("vector", "tensor_copy", WTM[0:1, 0:1], NRMS[0:1, 33:34])
    p.dwait("vector")
    vd = p.inc("dve", 1)
    p.emit("vector", lambda e, s=p.sems["dve"]: e.nop().then_inc(s, 1))
    p.wait("tensor", "dve", vd)
    p.mm(ps["C"][:, 0:1], t["ONES"][0:1, 0:128], WTM[0:1, 0:1],
         start=True, stop=True, inc=("pe", 1))
    p.wait("scalar", "pe", p.counts["pe"])
    va = p.op("scalar", "activation", out=t["KMX"][:, 32:33],
              in_=ps["C"][:, 0:1], func=AF.Sqrt, inc=("act", 1))
    p.wait("vector", "act", va)
    p.op("vector", "tensor_scalar", out=t["KMX"][:, 32:33],
         in0=t["KMX"][:, 32:33], scalar1=-0.125, scalar2=None, op0=ALU.mult,
         inc=("dve", 1))
    ph.close()


def plan_l1_attn(ctx):
    _l1_attn_pass(ctx, 0)
    _l1_attn_pass(ctx, 1)


def _l1_attn_pass(ctx, P):
    p, t, w, ps = ctx["p"], ctx["t"], ctx["w"], ctx["ps"]
    scr = ctx["scr"]
    ph = Phase(ctx)
    WDW = ph.sb("l1_WDW", (128, 8 * CH * L1L), BF16)
    WDB = ph.sb("l1_WDB", (128, 8 * 3), F32)
    OWB = ph.sb("l1_OWB", (128, 24 * 1024), BF16)
    WFM = ph.sb("l1_WFM", (128, 3 * 4096), BF16)
    WTM = ph.sb("l1_WTM", (128, 32 * 384), BF16)
    XNF = ph.sb("l1_XNF", (128, CH * 256), BF16)
    Q8 = ph.sb("l1_Q8", (128, 24 * 256), BF16)
    QSQ = ph.sb("l1_QSQ", (128, 256), BF16)
    ZP = ph.sb("l1_ZP", (128, 24 * 256), BF16)
    PB = ph.sb("l1_PB", (128, 2 * 4096), BF16)
    PTB = ph.sb("l1_PTB", (128, 2048), BF16)
    AWB = ph.sb("l1_AWB", (128, 4 * 256), F32)
    XATT = ph.sb("l1_XATT", (128, 1024), F32)
    OB = ph.sb("l1_OB", (128, CH), F32)
    QN = ph.sb("l1_QN", (128, 16), F32)
    BND = ph.sb("l1_BND", (128, 16), F32)
    NSK = ph.sb("l1_NSK", (128, 16), F32)
    SMS = ph.sb("l1_SMS", (128, 36), F32)
    hbase = 8 * P

    for h in range(8):
        for kc in range(CH):
            p.op("gpsimd", "dma_start",
                 out=WDW[:, (h * CH + kc) * L1L:(h * CH + kc + 1) * L1L],
                 in_=w["wd_w"][hbase + h, kc * 128:(kc + 1) * 128, :],
                 inc=("dmag", 16))
        p.op("gpsimd", "dma_start", out=WDB[:, h * 3:(h + 1) * 3],
             in_=w["wd_b_l"][hbase + h], inc=("dmag", 16))
    for kc in range(24):
        p.op("gpsimd", "dma_start", out=OWB[:, kc * 1024:(kc + 1) * 1024],
             in_=w["out_w"][P * 3072 + kc * 128:P * 3072 + (kc + 1) * 128, :],
             inc=("dmag", 16))
    vb = p.op("sync", "dma_start", out=OB[:, :], in_=w["out_b_l"], inc=("dma", 16))
    for mc in range(3):
        mw = 128 if mc < 2 else 32
        p.op("sync", "dma_start", out=WFM[0:mw, mc * 4096:(mc + 1) * 4096],
             in_=scr["wdkvF"][mc * 128:mc * 128 + mw, :], inc=("dma", 16))
    vv_ = p.op("sync", "dma_start",
               out=WTM[:, :].rearrange("p (u l) -> p u l", u=32),
               in_=scr["wdkvT"][:, :].rearrange("(u p) l -> p u l", p=128),
               inc=("dma", 16))
    p.wait("vector", "dve", p.counts["dve"])
    for jj in range(16):
        p.qop("vector", "tensor_copy", NSK[:, jj:jj + 1], t["KMX"][:, 32:33])
    p.dwait("vector")
    p.wait("tensor", "dmag", p.counts["dmag"])
    p.wait("tensor", "dma", p.counts["dma"])
    p.op("gpsimd", "memset", ZP[:, :], 0.0, inc=("pool", 1))
    p.wait("scalar", "pool", p.counts["pool"])
    p.wait("vector", "pool", p.counts["pool"])

    evac = {}
    for wi in range(16):
        p.wait("sync", "pe", p.counts["pe"])
        vx = p.op("sync", "dma_start",
                  out=XNF[:, :].rearrange("p (c n) -> p c n", c=CH),
                  in_=scr["xn1f"][:, wi * 256:(wi + 1) * 256].rearrange(
                      "(c p) n -> p c n", p=128), inc=("dma", 16))
        p.wait("tensor", "dma", vx)
        p.wait("tensor", "dve", p.counts["dve"])
        p.wait("tensor", "act", p.counts["act"])
        for h in range(8):
            for mc in range(3):
                mw = 128 if mc < 2 else 32
                bank = ps["A"] if (h * 3 + mc) % 2 == 0 else ps["B"]
                if ("q", (h * 3 + mc) % 2) in evac:
                    p.wait("tensor", "dve", evac[("q", (h * 3 + mc) % 2)])
                for kc in range(CH):
                    p.mm(bank[0:mw, 0:256],
                         WDW[:, (h * CH + kc) * L1L + mc * 128:
                             (h * CH + kc) * L1L + mc * 128 + mw],
                         XNF[:, kc * 256:(kc + 1) * 256],
                         start=(kc == 0), stop=(kc == CH - 1),
                         inc=("pe", 1) if kc == CH - 1 else None)
                p.wait("vector", "pe", p.counts["pe"])
                vd = p.op("vector", "tensor_scalar",
                          out=Q8[0:mw, (h * 3 + mc) * 256:(h * 3 + mc + 1) * 256],
                          in0=bank[0:mw, 0:256], scalar1=WDB[0:mw, h * 3 + mc:h * 3 + mc + 1],
                          scalar2=None, op0=ALU.add, inc=("dve", 1))
                evac[("q", (h * 3 + mc) % 2)] = vd
        p.dwait("vector")
        p.wait("tensor", "dve", p.counts["dve"])
        for h in range(8):
            for mc in range(3):
                mw = 128 if mc < 2 else 32
                p.qop("vector", "tensor_tensor", QSQ[0:mw, 0:256],
                      Q8[0:mw, (h * 3 + mc) * 256:(h * 3 + mc + 1) * 256],
                      Q8[0:mw, (h * 3 + mc) * 256:(h * 3 + mc + 1) * 256],
                      ALU.mult)
                p.dwait("vector")
                vd = p.inc("dve", 1)
                p.emit("vector", lambda e, s=p.sems["dve"]: e.nop().then_inc(s, 1))
                p.wait("tensor", "dve", vd)
                for j in (0, 1):
                    p.mm(ps["D"][:, 8 + j * 8 + h:9 + j * 8 + h],
                         QSQ[0:mw, j * 128:j * 128 + 128], t["ONES"][0:mw, 0:1],
                         start=(mc == 0), stop=(mc == 2),
                         inc=("pe", 1) if (mc == 2 and j == 1) else None)
        p.wait("scalar", "pe", p.counts["pe"])
        qn_evac = p.op("scalar", "activation", out=QN[:, 0:16],
                       in_=ps["D"][:, 8:24], func=AF.Copy, inc=("act", 1))
        p.wait("scalar", "act", qn_evac)
        va = p.op("scalar", "activation", out=QN[:, 0:16], in_=QN[:, 0:16],
                  func=AF.Sqrt, inc=("act", 1))
        p.wait("vector", "act", va)
        vbnd = p.op("vector", "tensor_tensor", BND[:, 0:16], QN[:, 0:16],
                    NSK[:, 0:16], ALU.mult, inc=("dve", 1))
        p.wait("scalar", "dve", vbnd)
        p.wait("tensor", "dve", vbnd)

        for hl in range(8):
            normdone = {}
            for j in (0, 1):
                i = 2 * wi + j
                t0 = i * 128
                extent = t0 + 128
                nch = (extent + 511) // 512
                dchunk = (extent - 1) // 512
                dvar = i % 4
                for sc in range(nch):
                    sw = min(512, extent - sc * 512)
                    bank = (ps["A"], ps["B"], ps["F"])[sc % 3]
                    if ("sb", sc % 3) in evac:
                        ga, gd = evac[("sb", sc % 3)]
                        p.wait("tensor", "act", ga)
                        p.wait("tensor", "dve", gd)
                    for mc in range(3):
                        mw = 128 if mc < 2 else 32
                        p.mm(bank[:, 0:sw],
                             Q8[0:mw, (hl * 3 + mc) * 256 + j * 128:
                                (hl * 3 + mc) * 256 + (j + 1) * 128],
                             WFM[0:mw, mc * 4096 + sc * 512:mc * 4096 + sc * 512 + sw],
                             start=(mc == 0), stop=(mc == 2),
                             inc=("pe", 1) if mc == 2 else None)
                    p.wait("scalar", "pe", p.counts["pe"])
                    pbsl = PB[:, j * 4096 + sc * 512:j * 4096 + sc * 512 + sw]
                    if sc == dchunk:
                        va = p.op("scalar", "activation", out=pbsl,
                                  in_=bank[:, 0:sw], func=AF.Exp, scale=0.125,
                                  bias=BND[:, j * 8 + hl:j * 8 + hl + 1],
                                  inc=("act", 1))
                        p.wait("vector", "act", va)
                        vd = p.op("vector", "scalar_tensor_tensor", out=pbsl,
                                  in0=pbsl, scalar=1.0,
                                  in1=t["DIAGM"][:, dvar * 512:dvar * 512 + sw],
                                  op0=ALU.mult, op1=ALU.mult,
                                  accum_out=SMS[:, j * 18 + sc:j * 18 + sc + 1],
                                  inc=("dve", 1))
                        evac[("sb", sc % 3)] = (p.counts["act"], vd)
                    else:
                        va = p.op("scalar", "activation", out=pbsl,
                                  in_=bank[:, 0:sw], func=AF.Exp, scale=0.125,
                                  bias=BND[:, j * 8 + hl:j * 8 + hl + 1],
                                  accum_out=SMS[:, j * 18 + sc:j * 18 + sc + 1],
                                  inc=("act", 1))
                        evac[("sb", sc % 3)] = (va, p.counts["dve"])
                        p.wait("vector", "act", va)
                p.dwait("vector")
                p.wait("vector", "act", p.counts["act"])
                p.qop("vector", "reduce_sum", SMS[:, j * 18 + 16:j * 18 + 17],
                      SMS[:, j * 18:j * 18 + nch], axis=AX)
                p.dwait("vector")
                p.qop("vector", "reciprocal", SMS[:, j * 18 + 16:j * 18 + 17],
                      SMS[:, j * 18 + 16:j * 18 + 17])
                p.dwait("vector")
                for sc in range(nch):
                    sw = min(512, extent - sc * 512)
                    normdone[(j, sc)] = p.op(
                        "vector", "tensor_scalar",
                        out=PB[:, j * 4096 + sc * 512:j * 4096 + sc * 512 + sw],
                        in0=PB[:, j * 4096 + sc * 512:j * 4096 + sc * 512 + sw],
                        scalar1=SMS[:, j * 18 + 16:j * 18 + 17], scalar2=None,
                        op0=ALU.mult, inc=("dve", 1))
                if j == 0:
                    tail = p.op("vector", "memset", PB[:, extent:extent + 128],
                                0.0, inc=("dve", 1))
                    normdone[(0, "tail")] = tail
            if ("y", 0) in evac:
                p.wait("tensor", "dve", evac[("y", 0)])
            extent1 = (2 * wi + 2) * 128
            nch1 = (extent1 + 511) // 512
            copy_done = {}
            nsub_of = {}
            def emit_T(sc):
                par = sc % 2
                nsub = (min(512, extent1 - sc * 512) + 127) // 128
                nsub_of[sc] = nsub
                nd0 = normdone.get((0, sc), normdone.get((0, "tail"), 0))
                p.wait("tensor", "dve", max(nd0, normdone.get((1, sc), 0)))
                if sc >= 2:
                    cd = copy_done[sc - 2]
                    p.wait("tensor", cd[0], cd[1])
                    p.wait("tensor", cd[2], cd[3])
                for u in range(nsub):
                    if u == 2:
                        cs_, cv_ = copy_done[(sc, 0)]
                        p.wait("tensor", cs_, cv_)
                    for j in (0, 1):
                        p.op("tensor", "transpose",
                             ps["T"][:, par * 1024 + (u % 2) * 256 + j * 128:
                                     par * 1024 + (u % 2) * 256 + (j + 1) * 128],
                             PB[:, j * 4096 + sc * 512 + u * 128:
                                j * 4096 + sc * 512 + (u + 1) * 128],
                             t["IDT"][:, :],
                             inc=("pe", 1) if (u % 2 == 1 or u == nsub - 1) and j == 1 else None)
                    if u % 2 == 1 or u == nsub - 1:
                        grp = u // 2
                        v = p.counts["pe"]
                        ceng = "vector" if grp % 2 == 0 else "scalar"
                        csem = "dve" if grp % 2 == 0 else "act"
                        p.wait(ceng, "pe", v)
                        w = min(2, nsub - grp * 2) * 256
                        if ceng == "vector":
                            vd = p.op("vector", "tensor_copy",
                                      PTB[:, par * 1024 + grp * 512:par * 1024 + grp * 512 + w],
                                      ps["T"][:, par * 1024:par * 1024 + w],
                                      inc=("dve", 1))
                        else:
                            vd = p.op("scalar", "activation",
                                      out=PTB[:, par * 1024 + grp * 512:par * 1024 + grp * 512 + w],
                                      in_=ps["T"][:, par * 1024:par * 1024 + w],
                                      func=AF.Copy, inc=("act", 1))
                        copy_done[(sc, grp)] = (csem, vd)
                copy_done[sc] = ("dve", p.counts["dve"], "act", p.counts["act"])
            def emit_PV(sc):
                par = sc % 2
                nsub = nsub_of[sc]
                for u in range(nsub):
                    cs_, cv_ = copy_done[(sc, u // 2)]
                    p.wait("tensor", cs_, cv_)
                    ug = sc * 4 + u
                    last = (sc == nch1 - 1 and u == nsub - 1)
                    for mc in range(3):
                        mw = 128 if mc < 2 else 32
                        ybank = (ps["C"][0:128, 0:256], ps["D"][0:128, 0:256],
                                 ps["D"][0:32, 256:512])[mc]
                        p.mm(ybank,
                             WTM[:, ug * 384 + mc * 128:ug * 384 + mc * 128 + mw],
                             PTB[:, par * 1024 + u * 256:par * 1024 + (u + 1) * 256],
                             start=emit_PV.first, stop=last,
                             inc=("pe", 1) if (last and mc == 2) else None)
                    emit_PV.first = False
            emit_PV.first = True
            for sc in range(nch1):
                emit_T(sc)
                if sc >= 1:
                    emit_PV(sc - 1)
            emit_PV(nch1 - 1)
            p.wait("vector", "pe", p.counts["pe"])
            for mc in range(3):
                mw = 128 if mc < 2 else 32
                ybank = (ps["C"][0:128, 0:256], ps["D"][0:128, 0:256],
                         ps["D"][0:32, 256:512])[mc]
                vd = p.op("vector", "tensor_copy",
                          ZP[0:mw, (hl * 3 + mc) * 256:(hl * 3 + mc + 1) * 256],
                          ybank, inc=("dve", 1) if mc == 2 else None)
            evac[("y", 0)] = p.counts["dve"]
        p.wait("tensor", "dve", p.counts["dve"])

        p.wait("tensor", "dve", p.counts["dve"])
        p.wait("tensor", "act", p.counts["act"])
        for half in (0, 1):
            for oc2 in range(4):
                oc = half * 4 + oc2
                bank = ps["F"]
                if ("wo", 0) in evac:
                    p.wait("tensor", "dve", evac[("wo", 0)])
                for kc in range(24):
                    p.mm(bank[:, 0:256],
                         OWB[:, kc * 1024 + oc * 128:kc * 1024 + (oc + 1) * 128],
                         ZP[:, kc * 256:(kc + 1) * 256],
                         start=(kc == 0), stop=(kc == 23),
                         inc=("pe", 1) if kc == 23 else None)
                p.wait("vector", "pe", p.counts["pe"])
                if P == 0:
                    vd = p.op("vector", "tensor_scalar",
                              out=AWB[:, oc2 * 256:(oc2 + 1) * 256],
                              in0=bank[:, 0:256], scalar1=OB[:, oc:oc + 1],
                              scalar2=None, op0=ALU.add, inc=("dve", 1))
                else:
                    vd = p.op("vector", "tensor_copy",
                              AWB[:, oc2 * 256:(oc2 + 1) * 256],
                              bank[:, 0:256], inc=("dve", 1))
                evac[("wo", 0)] = vd
            p.wait("tensor", "dve", p.counts["dve"])
            for j in (0, 1):
                i = 2 * wi + j
                for oc2 in range(4):
                    oc = half * 4 + oc2
                    vp = p.op("tensor", "transpose", ps["G"][:, 0:128],
                              AWB[:, oc2 * 256 + j * 128:oc2 * 256 + (j + 1) * 128],
                              t["IDT2"][:, :], inc=("pe", 1))
                    p.wait("vector", "pe", vp)
                    vd = p.op("vector", "tensor_copy",
                              XATT[:, oc2 * 128:(oc2 + 1) * 128],
                              ps["G"][:, 0:128], inc=("dve", 1))
                    p.wait("tensor", "dve", vd)
                p.wait("sync", "dve", p.counts["dve"])
                p.wait("gpsimd", "dve", p.counts["dve"])
                if P == 0:
                    vs = p.op("sync", "dma_start",
                              out=scr["s1"][i * 128:(i + 1) * 128,
                                            half * 512:(half + 1) * 512],
                              in_=XATT[:, 0:512], inc=("dma", 16))
                    p.wait("vector", "dma", vs)
                else:
                    vs = p.op("gpsimd", "dma_start",
                              out=scr["s1"][i * 128:(i + 1) * 128,
                                            half * 512:(half + 1) * 512],
                              in_=XATT[:, 0:512], accum_op=ALU.add,
                              inc=("dmag", 16))
                    p.wait("vector", "dmag", vs)
    ph.close()


def plan_l1_ffn(ctx):
    p, t, ps = ctx["p"], ctx["t"], ctx["ps"]
    w, scr = ctx["w"], ctx["scr"]
    ph = Phase(ctx)
    WB = ph.sb("f1_WB", (128, CH * DM), BF16)
    XL2 = ph.sb("f1_XL", (128, 2 * CH * 512), BF16)
    EOB2 = ph.sb("f1_EOB", (128, 512), BF16)
    FF2 = ph.sb("f1_FF", (128, 2 * 4 * 1024), F32)
    B2R = ph.sb("f1_B2R", (1, DM), BF16)
    evac = {}
    xl_consumed = {}
    ff_stored = {}
    for e in range(E1F):
        vb2 = p.op("gpsimd", "dma_start", out=B2R[0:1, :], in_=w["l1f_b_row"][e],
                   inc=("dmag", 16))
        for kc in range(CH):
            vw = p.op("gpsimd", "dma_start", out=WB[:, kc * DM:(kc + 1) * DM],
                      in_=w["l1f_w"][e, kc * 128:(kc + 1) * 128, :],
                      inc=("dmag", 16))
        p.wait("tensor", "dmag", vw)
        p.wait("vector", "dmag", vw)
        xl_loaded = {}
        for tbl in range(9):
            if tbl < 8:
                par = tbl % 2
                if (e, tbl - 2) in xl_consumed:
                    p.wait("sync", "pe", xl_consumed[(e, tbl - 2)])
                elif tbl < 2 and (e - 1, tbl + 6) in xl_consumed:
                    p.wait("sync", "pe", xl_consumed[(e - 1, tbl + 6)])
                XL = XL2[:, par * CH * 512:(par + 1) * CH * 512]
                vx = p.op("sync", "dma_start",
                          out=XL[:, :].rearrange("p (c n) -> p c n", c=CH),
                          in_=scr["xn1f"][:, tbl * 512:(tbl + 1) * 512].rearrange(
                              "(c p) n -> p c n", p=128), inc=("dma", 16))
                xl_loaded[tbl] = vx
            if tbl == 0:
                continue
            tb = tbl - 1
            par = tb % 2
            XL = XL2[:, par * CH * 512:(par + 1) * CH * 512]
            FF = FF2[:, par * 4096:(par + 1) * 4096]
            p.wait("tensor", "dma", xl_loaded[tb])
            prev = ff_stored.get((e, tb - 2)) or ff_stored.get((e - 1, tb + 6))
            if prev is not None:
                p.wait("scalar", prev[0], prev[1])
            for oc in range(CH):
                bank = (ps["C"], ps["F"], ps["G"])[oc % 3]
                if ("eo", oc % 3) in evac:
                    p.wait("tensor", "dve", evac[("eo", oc % 3)])
                for kc in range(CH):
                    p.mm(bank[:, :],
                         WB[:, kc * DM + oc * 128:kc * DM + (oc + 1) * 128],
                         XL[:, kc * 512:(kc + 1) * 512],
                         start=(kc == 0), stop=False)
                p.mm(bank[:, :], B2R[0:1, oc * 128:(oc + 1) * 128],
                     t["ONES"][0:1, 0:512], start=False, stop=True,
                     inc=("pe", 1))
                p.wait("vector", "pe", p.counts["pe"])
                va = p.op("vector", "tensor_copy", EOB2[:, :],
                          bank[:, :], inc=("dve", 1))
                evac[("eo", oc % 3)] = va
                p.wait("tensor", "dve", va)
                for jj in range(4):
                    i = tb * 4 + jj
                    if ("fft", jj % 2) in evac:
                        p.wait("tensor", "act", evac[("fft", jj % 2)])
                    vp = p.op("tensor", "transpose",
                              ps["T"][:, (jj % 2) * 1024:(jj % 2) * 1024 + 128],
                              EOB2[:, jj * 128:(jj + 1) * 128], t["IDT"][:, :],
                              inc=("pe", 1))
                    p.wait("scalar", "pe", vp)
                    vd = p.op("scalar", "activation",
                              out=FF[:, jj * 1024 + oc * 128:jj * 1024 + (oc + 1) * 128],
                              in_=ps["T"][:, (jj % 2) * 1024:(jj % 2) * 1024 + 128],
                              func=AF.Copy,
                              scale=t["GATES"][:, i * E1F + e:
                                               i * E1F + e + 1],
                              inc=("act", 1))
                    evac[("fft", jj % 2)] = vd
            xl_consumed[(e, tb)] = p.counts["pe"]
            p.wait("sync", "dve", p.counts["dve"])
            p.wait("sync", "act", p.counts["act"])
            p.wait("gpsimd", "dve", p.counts["dve"])
            p.wait("gpsimd", "act", p.counts["act"])
            for jj in range(4):
                i = tb * 4 + jj
                if e == 0:
                    vs = p.op("sync", "dma_start",
                              out=scr["l1f"][i * 128:(i + 1) * 128, :],
                              in_=FF[:, jj * 1024:(jj + 1) * 1024],
                              inc=("dma", 16))
                    ff_stored[(e, tb)] = ("dma", p.counts["dma"])
                else:
                    vs = p.op("gpsimd", "dma_start",
                              out=scr["l1f"][i * 128:(i + 1) * 128, :],
                              in_=FF[:, jj * 1024:(jj + 1) * 1024],
                              accum_op=ALU.add, inc=("dmag", 16))
                    ff_stored[(e, tb)] = ("dmag", p.counts["dmag"])
        p.wait("gpsimd", "pe", p.counts["pe"])
        p.wait("gpsimd", "act", p.counts["act"])
        p.wait("gpsimd", "dve", p.counts["dve"])
        if e == 0:
            p.wait("gpsimd", "dma", p.counts["dma"])
    ph.close()


def plan_l1_combine(ctx):
    p, t = ctx["p"], ctx["t"]
    scr = ctx["scr"]
    ph = Phase(ctx)
    XA = ph.sb("o_XA", (128, 1024), F32)
    XG = ph.sb("o_XG", (128, 1024), F32)
    XX = ph.sb("o_XX", (128, 1024), F32)
    for i in range(NT):
        sl = slice(i * 128, (i + 1) * 128)
        if i > 0:
            p.wait("sync", "dma", store_v)
            p.wait("sync", "dve", p.counts["dve"])
            p.wait("sync", "act", p.counts["act"])
        v1 = p.op("sync", "dma_start", out=XA[:, :], in_=scr["s1"][sl, :],
                  inc=("dma", 16))
        v3 = p.op("sync", "dma_start", out=XG[:, :], in_=scr["l1f"][sl, :],
                  inc=("dma", 16))
        v4 = p.op("sync", "dma_start", out=XX[:, :], in_=scr["x1"][sl, :],
                  inc=("dma", 16))
        p.wait("scalar", "dma", v3)
        va = p.op("scalar", "activation", out=XG[:, :], in_=XG[:, :],
                  func=AF.Silu, inc=("act", 1))
        p.wait("vector", "dma", v4)
        p.wait("vector", "act", va)
        p.qop("vector", "tensor_tensor", XA[:, :], XA[:, :], XG[:, :], ALU.mult)
        p.dwait("vector")
        vd = p.op("vector", "tensor_tensor", XA[:, :], XA[:, :], XX[:, :],
                  ALU.add, inc=("dve", 1))
        p.wait("sync", "dve", vd)
        store_v = p.op("sync", "dma_start", out=ctx["out_d"][sl, :],
                       in_=XA[:, :], inc=("dma", 16))
        p.drip(ctx)
    ph.close()


def plan_finish(ctx):
    p = ctx["p"]
    for s_name in list(p.counts):
        p.wait("sync", s_name, p.counts[s_name])


DEFAULT_PLANNERS = None


def build(taps=(), planners=None):
    ctx = make_ctx(taps)
    if planners is None:
        planners = [plan_init,
                    lambda c: plan_stats(c, c["x_d"], "st0"),
                    plan_l0_kv, plan_l0_attn, plan_l0_moe, plan_l0_ffn,
                    plan_l0_combine,
                    plan_l1_router,
                    plan_l1_wdkv, plan_l1_attn, plan_l1_ffn, plan_l1_combine,
                    plan_finish]
    for fn in planners:
        fn(ctx)
    ctx["p"].run_block()
    ctx["stack"].close()
    return ctx


def make_in_maps(d, cores=8):
    x_full = d["L0_attn_x"]
    return [{k: (np.ascontiguousarray(x_full[b]) if k == "L0_attn_x" else d[k])
             for k in DECLARED_INPUTS} for b in range(cores)]


def kernel(**inputs):
    d = host_prep(inputs)
    ctx = build()
    per_core = make_in_maps(d, 8)
    res = run_bass_kernel_spmd(ctx["nc"], per_core, core_ids=list(range(8)))
    return np.stack([res.results[b]["out"] for b in range(8)], axis=0)
